# version 02
# baseline (speedup 1.0000x reference)
import contextlib
import os
KPHASES = int(os.environ.get("KPHASES", "5"))
KCSUB = int(os.environ.get("KCSUB", "4"))
KCA = int(os.environ.get("KCA", "4"))
KCB = int(os.environ.get("KCB", "3"))
import numpy as np
import ml_dtypes
import concourse.bass as bass
import concourse.mybir as mybir
from concourse.bass_utils import run_bass_kernel_spmd

F32 = mybir.dt.float32
BF16 = mybir.dt.bfloat16
AL = mybir.AluOpType
AF = mybir.ActivationFunctionType
AX = mybir.AxisListType

T, NT, D, DI, NH, DS, CONVD, DIP = 2048, 16, 1024, 2048, 16, 64, 2176, 4240
LN2 = float(np.log(2.0))
LN_EPS = 1e-5
RMS_EPS = 1e-5


class St:
    pass


def inc(rec, inst, k=1):
    rec[1] += k
    inst.then_inc(rec[0], k)
    return rec[1]


def wt(eng, rec, val=None):
    eng.wait_ge(rec[0], rec[1] if val is None else val)


def dinc(rec):
    rec[1] += 16
    return 16


def mxfp4_quant_emit(v, st, x_ap, nrows, ncols, kmin, kmax, scl2d, iscl2d, out_ap):
    nb = ncols // 32
    x3 = x_ap.rearrange("p (b k) -> p b k", k=32)
    amax = st.qs[0:nrows, 0:nb]
    pexp = st.qs[0:nrows, nb:2 * nb]
    ipexp = st.qs[0:nrows, 2 * nb:3 * nb]
    tmpb = st.qs[0:nrows, 3 * nb:4 * nb]
    tmpb2 = st.qs[0:nrows, 4 * nb:5 * nb]
    inc(st.vt, v.tensor_reduce(out=amax, in_=x3, axis=AX.X, op=AL.max, apply_absolute_value=True))
    wt(v, st.vt)
    v.tensor_scalar(out=pexp, in0=amax, scalar1=float(2.0 ** kmin), scalar2=float(2.0 ** kmin),
                    op0=AL.is_ge, op1=AL.mult)
    v.tensor_scalar(out=ipexp, in0=amax, scalar1=float(2.0 ** kmin), scalar2=float(2.0 ** -kmin),
                    op0=AL.is_ge, op1=AL.mult)
    for k in range(kmin + 1, kmax + 1):
        v.tensor_scalar(out=tmpb, in0=amax, scalar1=float(2.0 ** k),
                        scalar2=float(2.0 ** k - 2.0 ** (k - 1)), op0=AL.is_ge, op1=AL.mult)
        v.tensor_scalar(out=tmpb2, in0=amax, scalar1=float(2.0 ** k),
                        scalar2=float(2.0 ** -k - 2.0 ** (-k + 1)), op0=AL.is_ge, op1=AL.mult)
        v.tensor_tensor(out=pexp, in0=pexp, in1=tmpb, op=AL.add)
        v.tensor_tensor(out=ipexp, in0=ipexp, in1=tmpb2, op=AL.add)
    v.tensor_tensor(out=ipexp, in0=ipexp, in1=iscl2d, op=AL.mult)
    v.tensor_tensor(out=pexp, in0=pexp, in1=scl2d, op=AL.mult)
    xs = st.q1[0:nrows, 0:ncols]
    v.tensor_tensor(out=xs.rearrange("p (b k) -> p b k", k=32), in0=x3,
                    in1=ipexp.unsqueeze(2).broadcast_to([nrows, nb, 32]), op=AL.mult)
    sgn = st.q2[0:nrows, 0:ncols]
    v.tensor_scalar(out=sgn, in0=xs, scalar1=0.0, scalar2=2.0, op0=AL.is_ge, op1=AL.mult)
    v.tensor_scalar(out=sgn, in0=sgn, scalar1=-1.0, scalar2=None, op0=AL.add)
    a = xs
    v.tensor_tensor(out=a, in0=xs, in1=sgn, op=AL.mult)
    v.tensor_scalar(out=a, in0=a, scalar1=6.0, scalar2=None, op0=AL.min)
    tq = out_ap
    v.tensor_scalar(out=tq, in0=a, scalar1=4.0, scalar2=0.5, op0=AL.is_lt, op1=AL.mult)
    v.tensor_scalar(out=tq, in0=tq, scalar1=0.5, scalar2=None, op0=AL.add)
    v.scalar_tensor_tensor(out=tq, in0=a, scalar=2.0, in1=tq, op0=AL.is_lt, op1=AL.add)
    v.tensor_tensor(out=tq, in0=tq, in1=a, op=AL.mult)
    r = st.lndump[0:nrows, 0:ncols]
    v.tensor_scalar(out=r, in0=tq, scalar1=0.5, scalar2=None, op0=AL.is_gt)
    for thr, op in [(1.5, AL.is_ge), (2.5, AL.is_gt), (3.5, AL.is_ge), (4.5, AL.is_gt), (5.5, AL.is_ge)]:
        v.scalar_tensor_tensor(out=r, in0=tq, scalar=float(thr), in1=r, op0=op, op1=AL.add)
    stp = tq
    v.tensor_scalar(out=stp, in0=a, scalar1=4.0, scalar2=-1.0, op0=AL.is_lt, op1=AL.mult)
    v.tensor_scalar(out=stp, in0=stp, scalar1=2.0, scalar2=None, op0=AL.add)
    v.tensor_scalar(out=a, in0=a, scalar1=2.0, scalar2=-0.5, op0=AL.is_lt, op1=AL.mult)
    v.tensor_tensor(out=stp, in0=stp, in1=a, op=AL.add)
    v.tensor_tensor(out=out_ap, in0=stp, in1=r, op=AL.mult)
    v.tensor_tensor(out=out_ap.rearrange("p (b k) -> p b k", k=32),
                    in0=out_ap.rearrange("p (b k) -> p b k", k=32),
                    in1=pexp.unsqueeze(2).broadcast_to([nrows, nb, 32]), op=AL.mult)
    v.tensor_tensor(out=out_ap, in0=out_ap, in1=sgn, op=AL.mult)


def build_program(debug=False):
    nc = bass.Bass("TRN2", target_bir_lowering=False, detect_race_conditions=False)
    st = St()
    st.nc = nc
    st.es = contextlib.ExitStack()
    st.debug = debug

    def di(name, shape, dtype=F32):
        return nc.dram_tensor(name, shape, dtype, kind="ExternalInput").ap()

    st.u = di("L0_attn_u", [T, D])
    st.ipw = di("L0_attn_in_proj_w", [DIP, D])
    st.convw = di("L0_attn_conv_w", [CONVD, 4])
    st.convb = di("L0_attn_conv_b", [CONVD])
    st.dt_bias = di("L0_attn_dt_bias", [NH])
    st.A_log = di("L0_attn_A_log", [NH])
    st.Dp = di("L0_attn_D", [NH])
    st.xnw = di("L0_attn_x_norm_w", [D])
    st.xnb = di("L0_attn_x_norm_b", [D])
    st.rmsw = di("L0_attn_rms_w", [DI])
    st.opw = di("L0_attn_out_proj_w", [D, DI])
    st.fw = di("L0_ffn_weight", [D, D])
    st.fb = di("L0_ffn_bias", [D])
    st.fdw = di("L0_ffn_delta_w", [D, 32])
    st.fda = di("L0_ffn_delta_act", [1])
    st.rot = di("L1_attn_rotation_params", [16, 64])
    st.ent = di("L1_attn_entangle_params", [16, 64])
    st.sq_in = di("L1_ffn_slot_query", [1, 16, D])
    for n in ["d_Wq", "d_Wk", "d_Wv", "d_Wo", "c_Wq", "c_Wk", "c_Wv", "c_Wo"]:
        setattr(st, n, di("L1_ffn_" + n, [D, D]))
    for n in ["d_bq", "d_bk", "d_bv", "d_bo", "c_bq", "c_bk", "c_bv", "c_bo"]:
        setattr(st, n, di("L1_ffn_" + n, [D]))
    st.elng = di("L1_ffn_e_ln_g", [16, D])
    st.elnb = di("L1_ffn_e_ln_b", [16, D])
    st.eW1 = di("L1_ffn_e_W1", [16, D, 4096])
    st.eb1 = di("L1_ffn_e_b1", [16, 4096])
    st.eWg = di("L1_ffn_e_Wg", [16, D, 4096])
    st.ebg = di("L1_ffn_e_bg", [16, 4096])
    st.eW2 = di("L1_ffn_e_W2", [16, 4096, D])
    st.eb2 = di("L1_ffn_e_b2", [16, D])
    st.identf = di("c_identf", [128, 128])
    st.identb = di("c_identb", [128, 128], BF16)
    st.mask0 = di("c_mask0", [128, 256])
    st.mask1 = di("c_mask1", [128, 256])
    st.sel16i = di("c_sel16", [16, 2048])
    st.sel64i = di("c_sel64", [128, 64])

    st.out = nc.dram_tensor("out", [T, D], F32, kind="ExternalOutput").ap()
    st.s_ipWh = nc.dram_tensor("s_ipWh", [D, DIP], BF16, kind="Internal").ap()
    st.s_ipWl = nc.dram_tensor("s_ipWl", [D, DIP], BF16, kind="Internal").ap()
    st.s_opWh = nc.dram_tensor("s_opWh", [DI, D], BF16, kind="Internal").ap()
    st.s_opWl = nc.dram_tensor("s_opWl", [DI, D], BF16, kind="Internal").ap()
    st.s_fqWh = nc.dram_tensor("s_fqWh", [D, D], BF16, kind="Internal").ap()
    st.s_fqWl = nc.dram_tensor("s_fqWl", [D, D], BF16, kind="Internal").ap()

    st.dbg = {}
    if debug:
        for name, shape in [
            ("d_xn0", [128, 2 * D]), ("d_u", [128, 2 * D]),
            ("d_gT", [128, 16 * 256]), ("d_xBC", [128, 17 * 256]),
            ("d_xBCp", [128, 17 * 256]), ("d_dt1", [16, 2048]),
            ("d_xn1h", [128, 8192]), ("d_o", [16, 1024]), ("d_o2", [128, 128]),
            ("d_aX", [128, 16384]), ("d_negm", [1, 2048]), ("d_S0", [128, 512]),
            ("d_Qh", [65, 1024]), ("d_Ql", [65, 1024]), ("d_Kh", [65, 1024]), ("d_Kl", [65, 1024]),
            ("d_dqh", [16, 1024]), ("d_dkh", [128, 2048]), ("d_dvh", [128, 2048]),
            ("d_eh", [16, 1024]), ("d_eu", [128, 32]), ("d_eg", [128, 32]), ("d_ey", [1, 1024]),
            ("d_eur", [128, 32]),
            ("d_eb1T", [128, 512]),
            ("d_dS", [16, 2048]), ("d_daD2", [64, 256]),
            ("d_P0", [128, 512]), ("d_aO0", [65, 1024]), ("d_dd", [128, 16]),
            ("d_dt", [16, 2048]), ("d_xq", [128, 2 * D]),
            ("d_xL0", [128, NT * D]),
        ]:
            st.dbg[name] = nc.dram_tensor(name, shape, F32, kind="ExternalOutput").ap()

    with st.es:
        es = st.es
        sbt = lambda name, shape, dt=F32: es.enter_context(nc.sbuf_tensor(name, shape, dt))

        st.X = sbt("X", [128, NT * D])
        st.idf = sbt("idf", [128, 128])
        st.idb = sbt("idb", [128, 128], BF16)
        st.msk = sbt("msk", [128, 512])
        st.ones1 = sbt("ones1", [1, 128])
        st.onesc = sbt("onesc", [128, 1])
        st.xwb = sbt("xwb", [128, D])
        st.xbb = sbt("xbb", [128, D])
        st.fbb = sbt("fbb", [128, D])
        st.cwT = sbt("cwT", [128, 68])
        st.cbT = sbt("cbT", [128, 17])
        st.rwT = sbt("rwT", [128, 16])
        st.dtbc = sbt("dtbc", [16, 1])
        st.Acol = sbt("Acol", [16, 1])
        st.DB = sbt("DB", [128, 16])
        st.sm1 = sbt("sm1", [1, 528])
        st.sm2 = sbt("sm2", [128, 16])
        st.srow = sbt("srow", [1, 256])
        st.sel16 = sbt("sel16", [16, 2048])
        st.sel64 = sbt("sel64", [128, 64])
        st.AR = sbt("AR", [128, 29824])
        AR = st.AR
        st.xn0c = AR[:, 0:2048]
        st.gT = AR[:, 2048:6144]
        st.xBCc = AR[:, 6144:10496]
        st.q1 = AR[:, 10496:11520]
        st.q2 = AR[:, 11520:12544]
        st.uT = AR[:, 12544:14592].bitcast(BF16)
        st.wip = AR[:, 14592:16640].bitcast(BF16)
        st.ghl = AR[:, 16640:18688].bitcast(BF16)
        st.wop = AR[:, 18688:20736].bitcast(BF16)
        st.wfq = AR[:, 20736:22784].bitcast(BF16)
        st.dtp = AR[0:16, 22784:24832]
        st.PS = AR[0:64, 24832:26880]
        st.attb = AR[:, 26880:27392]
        st.Eb = AR[:, 27392:27648]
        st.bcs = AR[0:64, 27648:28160].bitcast(BF16)
        st.Cf32 = AR[0:64, 28160:28416]
        st.btok = AR[:, 28416:28544]
        st.xhT = AR[:, 28544:28800]
        st.wBh = AR[:, 28800:28928]
        st.sS = AR[:, 28928:29024]
        st.cdB = AR[:, 29024:29040]
        st.hist = AR[:, 29040:29092]
        st.cin = AR[:, 29092:29352]
        st.qs = AR[:, 29352:29512]
        st.rows = AR[0:1, 0:3088]
        st.uc = st.gT[:, 0:2048]
        st.lndump = st.gT[:, 2048:3072]
        st.pnat = st.gT
        st.pspl = st.uT
        st.ptp = AR[:, 6144:8192].bitcast(BF16)
        st.psc = st.qs[:, 144:152]
        st.xqT = st.uT
        st.xn1Th = AR[:, 0:8192].bitcast(BF16)
        st.xn1Tl = AR[:, 8192:16384].bitcast(BF16)
        st.t_tok = AR[:, 16384:17408]
        st.tsp = AR[:, 17408:18432].bitcast(BF16)
        st.ldmp1 = AR[:, 18432:19456]
        st.Sd = AR[0:16, 16384:18432]
        st.khs_h = AR[:, 20480:21504].bitcast(BF16)
        st.khs_l = AR[:, 22544:23568].bitcast(BF16)
        st.khT = AR[:, 23568:25616]
        st.dWs = AR[:, 25616:27664].bitcast(BF16)
        st.wnat = AR[:, 27664:29712]
        st.aD = AR[:, 29184:29312]
        st.sqT = AR[:, 29040:29168]
        st.o_tok = AR[0:16, 24592:25616]
        st.Q65h = AR[0:65, 16384:17408].bitcast(BF16)
        st.Q65l = AR[0:65, 17408:18432].bitcast(BF16)
        st.K65h = AR[0:65, 18432:19456].bitcast(BF16)
        st.K65l = AR[0:65, 19456:20480].bitcast(BF16)
        st.aT = AR[:, 20480:21504]
        st.ecast = AR[:, 20480:21504].bitcast(BF16)
        st.Vb = AR[:, 21504:22544]
        st.Pb = AR[:, 22544:23568]
        st.aO = AR[0:65, 23568:24592]
        st.b1T = AR[:, 25616:26128]
        st.bgT = AR[:, 26128:26640]
        st.hT = AR[:, 26640:26768]
        st.hsp = AR[:, 26768:26832].bitcast(BF16)
        st.hsz = AR[:, 26832:26896].bitcast(BF16)
        st.uTe = AR[:, 26896:26928]
        st.gTe = AR[:, 26928:26960]
        st.ugh = AR[:, 26960:26992].bitcast(BF16)
        st.espr = AR[0:1, 26992:27248]
        st.eb2n = AR[0:16, 25616:26640]
        st.ebnat = AR[0:16, 20480:24576]
        st.aCt = AR[:, 16384:17408]
        st.qhC = AR[:, 17408:18432]
        st.qhCT = AR[:, 18432:19456]
        st.cwqs = AR[:, 20480:22528].bitcast(BF16)
        st.ccos = AR[:, 22544:24592].bitcast(BF16)
        st.kvo = AR[0:16, 19456:20480]
        st.kvoT = AR[:, 29184:29312]
        st.Pc = AR[:, 26640:26896]
        st.rotT = AR[0:64, 29760:29776]
        st.entT = AR[0:64, 29776:29792]
        st.dbT = AR[:, 29792:29808]

        st.pb = [es.enter_context(nc.psum_tensor(f"pb{i}", [128, 512], F32)) for i in range(8)]

        def sem(name):
            return [es.enter_context(nc.semaphore(name)), 0]
        st.sy = sem("sy")
        st.sd = sem("sd")
        st.gd = sem("gd")
        st.ot = sem("ot")
        st.tt = sem("tt")
        st.vt = sem("vt")
        st.ct = sem("ct")
        st.ipc = sem("ipc")
        st.ipe = sem("ipe")
        st.opc = sem("opc")
        st.fqc = sem("fqc")

        st.block = es.enter_context(nc.Block())

        emit_p0(st)
        emit_prep(st)
        for c in range(8):
            emit_l0_chunk(st, c)
        emit_p2(st)
        if KPHASES >= 3:
            emit_dpath(st)
        if KPHASES >= 4:
            emit_p4(st)
        if KPHASES >= 5:
            emit_cpath(st)

        block = st.block

        @block.sync
        def _(s):
            wt(s, st.vt)
            wt(s, st.tt)
            wt(s, st.ct)
            s.dma_start(out=st.out.rearrange("(t p) d -> p t d", p=128),
                        in_=st.X[:, :].rearrange("p (t d) -> p t d", d=D)).then_inc(st.ot[0], dinc(st.ot))
            s.wait_ge(st.ot[0], 16)

        if debug:
            @block.scalar
            def _(s):
                wt(s, st.vt)
                wt(s, st.tt)
                s.dma_start(out=st.dbg["d_xL0"], in_=st.X[:, :]).then_inc(st.sd[0], dinc(st.sd))
                s.wait_ge(st.sd[0], st.sd[1])

    return nc


def emit_p0(st):
    nc, block = st.nc, st.block

    @block.sync
    def _(s):
        s.dma_start(out=st.X[:, :].rearrange("p (t d) -> p t d", d=D),
                    in_=st.u.rearrange("(t p) d -> p t d", p=128)).then_inc(st.sy[0], dinc(st.sy))
        s.dma_start(out=st.idf[:, :], in_=st.identf).then_inc(st.sy[0], dinc(st.sy))
        s.dma_start(out=st.idb[:, :], in_=st.identb).then_inc(st.sy[0], dinc(st.sy))
        s.dma_start(out=st.msk[:, 0:256], in_=st.mask0).then_inc(st.sy[0], dinc(st.sy))
        s.dma_start(out=st.msk[:, 256:512], in_=st.mask1).then_inc(st.sy[0], dinc(st.sy))
        s.dma_start(out=st.sel16[:, :], in_=st.sel16i).then_inc(st.sy[0], dinc(st.sy))
        s.dma_start(out=st.sel64[:, :], in_=st.sel64i).then_inc(st.sy[0], dinc(st.sy))
        with nc.allow_non_contiguous_dma(reason="one-time small constant rearranges"):
            s.dma_start(out=st.rows[0:1, 0:1024], in_=st.xnw.unsqueeze(0)).then_inc(st.sy[0], dinc(st.sy))
            s.dma_start(out=st.rows[0:1, 1024:2048], in_=st.xnb.unsqueeze(0)).then_inc(st.sy[0], dinc(st.sy))
            s.dma_start(out=st.rows[0:1, 2048:3072], in_=st.fb.unsqueeze(0)).then_inc(st.sy[0], dinc(st.sy))
            s.dma_start(out=st.rows[0:1, 3072:3088], in_=st.Dp.unsqueeze(0)).then_inc(st.sy[0], dinc(st.sy))
            s.dma_start(out=st.cwT[:, :].rearrange("p (ct k) -> p ct k", k=4),
                        in_=st.convw.rearrange("(ct p) k -> p ct k", p=128)).then_inc(st.sy[0], dinc(st.sy))
            s.dma_start(out=st.cbT[:, :], in_=st.convb.rearrange("(ct p) -> p ct", p=128),
                        allow_slow_non_contiguous=True).then_inc(st.sy[0], dinc(st.sy))
            s.dma_start(out=st.rwT[:, :], in_=st.rmsw.rearrange("(i p) -> p i", p=128),
                        allow_slow_non_contiguous=True).then_inc(st.sy[0], dinc(st.sy))
            s.dma_start(out=st.dtbc[:, :], in_=st.dt_bias.unsqueeze(1)).then_inc(st.sy[0], dinc(st.sy))
            s.dma_start(out=st.Acol[:, :], in_=st.A_log.unsqueeze(1)).then_inc(st.sy[0], dinc(st.sy))
            s.dma_start(out=st.sm1[0:1, 0:1], in_=st.fda.unsqueeze(0)).then_inc(st.sy[0], dinc(st.sy))
    st.LOADS0 = st.sy[1]

    @block.vector
    def _(v):
        wt(v, st.sy, st.LOADS0)
        v.memset(st.ones1[:, :], 1.0)
        inc(st.vt, v.memset(st.onesc[:, :], 1.0))
    v_init = st.vt[1]

    @block.scalar
    def _(s):
        wt(s, st.sy, st.LOADS0)
        s.activation(out=st.Acol[:, :], in_=st.Acol[:, :], func=AF.Exp)
        s.activation(out=st.sm1[0:1, 1:2], in_=st.sm1[0:1, 0:1], func=AF.Tanh, scale=4.0)
        s.activation(out=st.sm1[0:1, 2:3], in_=st.sm1[0:1, 1:2], func=AF.Exp, scale=0.5 * LN2)
        inc(st.ct, s.activation(out=st.sm1[0:1, 3:4], in_=st.sm1[0:1, 1:2], func=AF.Exp, scale=-0.5 * LN2))
    s_da = st.ct[1]

    @block.vector
    def _(v):
        wt(v, st.ct, s_da)
        inc(st.vt, v.tensor_scalar_mul(st.Acol[:, :], st.Acol[:, :], -1.0))

    @block.tensor
    def _(t):
        wt(t, st.vt, v_init)
        wt(t, st.ct, s_da)
        t.matmul(st.pb[0][:, :], st.ones1[:, :], st.rows[0:1, 0:512], start=True, stop=True)
        t.matmul(st.pb[1][:, :], st.ones1[:, :], st.rows[0:1, 512:1024], start=True, stop=True)
        t.matmul(st.pb[2][:, :], st.ones1[:, :], st.rows[0:1, 1024:1536], start=True, stop=True)
        t.matmul(st.pb[3][:, :], st.ones1[:, :], st.rows[0:1, 1536:2048], start=True, stop=True)
        t.matmul(st.pb[4][:, :], st.ones1[:, :], st.rows[0:1, 2048:2560], start=True, stop=True)
        t.matmul(st.pb[5][:, :], st.ones1[:, :], st.rows[0:1, 2560:3072], start=True, stop=True)
        t.matmul(st.pb[6][:, 0:16], st.ones1[:, :], st.rows[0:1, 3072:3088], start=True, stop=True)
        inc(st.tt, t.matmul(st.pb[7][:, 0:2], st.ones1[:, :], st.sm1[0:1, 2:4], start=True, stop=True))
    t_bc = st.tt[1]
    st.t_bc = t_bc

    @block.vector
    def _(v):
        wt(v, st.tt, t_bc)
        v.tensor_copy(out=st.xwb[:, 0:512], in_=st.pb[0][:, :])
        v.tensor_copy(out=st.xwb[:, 512:1024], in_=st.pb[1][:, :])
        v.tensor_copy(out=st.xbb[:, 0:512], in_=st.pb[2][:, :])
        v.tensor_copy(out=st.xbb[:, 512:1024], in_=st.pb[3][:, :])
        v.tensor_copy(out=st.fbb[:, 0:512], in_=st.pb[4][:, :])
        v.tensor_copy(out=st.fbb[:, 512:1024], in_=st.pb[5][:, :])
        v.tensor_copy(out=st.DB[:, :], in_=st.pb[6][:, 0:16])
        v.tensor_scalar(out=st.sm2[:, 0:1], in0=st.pb[7][:, 0:1], scalar1=0.25, scalar2=None, op0=AL.mult)
        v.tensor_scalar(out=st.sm2[:, 1:2], in0=st.pb[7][:, 1:2], scalar1=4.0, scalar2=None, op0=AL.mult)
        v.memset(st.hist[:, :], 0.0)
        v.memset(st.dtp[:, :], 0.0)
        inc(st.vt, v.memset(st.PS[:, :], 0.0))


def emit_norm_scale(st, natv, nat_rows, two_cols, extra_mult, dma_done):
    nc, block = st.nc, st.block

    @block.scalar
    def _(s):
        wt(s, st.sy, dma_done)
        s.activation(out=st.q1[0:nat_rows, :], in_=natv[:, 0:1024], func=AF.Square,
                     accum_out=st.psc[0:nat_rows, 0:1])
        if two_cols:
            s.activation(out=st.q1[0:nat_rows, :], in_=natv[:, 1024:2048], func=AF.Square,
                         accum_out=st.psc[0:nat_rows, 6:7])
        inc(st.ct, s.activation(out=st.psc[0:nat_rows, 7:8], in_=st.psc[0:nat_rows, 0:1], func=AF.Identity))
    s_sq = st.ct[1]

    @block.vector
    def _(v):
        wt(v, st.ct, s_sq)
        if two_cols:
            inc(st.vt, v.tensor_tensor(out=st.psc[0:nat_rows, 0:1], in0=st.psc[0:nat_rows, 0:1],
                                       in1=st.psc[0:nat_rows, 6:7], op=AL.add))
            wt(v, st.vt)
        inc(st.vt, v.tensor_scalar(out=st.psc[0:nat_rows, 1:2], in0=st.psc[0:nat_rows, 0:1],
                                   scalar1=1e-40, scalar2=None, op0=AL.max))
    v_m = st.vt[1]

    @block.scalar
    def _(s):
        wt(s, st.vt, v_m)
        s.activation(out=st.psc[0:nat_rows, 2:3], in_=st.psc[0:nat_rows, 1:2], func=AF.Ln)
        inc(st.ct, s.activation(out=st.psc[0:nat_rows, 3:4], in_=st.psc[0:nat_rows, 2:3],
                                func=AF.Exp, scale=-0.5))
    s_seed = st.ct[1]

    @block.vector
    def _(v):
        wt(v, st.ct, s_seed)
        y = st.psc[0:nat_rows, 3:4]
        for _i in range(2):
            inc(st.vt, v.tensor_tensor(out=st.psc[0:nat_rows, 4:5], in0=y, in1=y, op=AL.mult))
            wt(v, st.vt)
            inc(st.vt, v.tensor_tensor(out=st.psc[0:nat_rows, 4:5], in0=st.psc[0:nat_rows, 4:5],
                                       in1=st.psc[0:nat_rows, 1:2], op=AL.mult))
            wt(v, st.vt)
            inc(st.vt, v.tensor_scalar(out=st.psc[0:nat_rows, 4:5], in0=st.psc[0:nat_rows, 4:5],
                                       scalar1=-0.5, scalar2=1.5, op0=AL.mult, op1=AL.add))
            wt(v, st.vt)
            inc(st.vt, v.tensor_tensor(out=y, in0=y, in1=st.psc[0:nat_rows, 4:5], op=AL.mult))
            wt(v, st.vt)
        inc(st.vt, v.tensor_scalar(out=st.psc[0:nat_rows, 5:6], in0=y,
                                   scalar1=1e6, scalar2=extra_mult, op0=AL.min, op1=AL.mult))


def emit_prep_store(st, nat_rows, dcols, dst_h, dst_l, col0):
    nc, block = st.nc, st.block
    ncolt = dcols // 128
    for dc in range(ncolt):
        bh, bl = st.pb[4 + (dc % 2)], st.pb[6 + (dc % 2)]

        @block.tensor
        def _(t, dc=dc, bh=bh, bl=bl):
            wt(t, st.vt)
            t.transpose(bh[:, 0:64].bitcast(BF16)[:, 0:nat_rows],
                        st.pspl[0:nat_rows, dc * 128:(dc + 1) * 128], st.idb[0:nat_rows, 0:nat_rows])
            inc(st.tt, t.transpose(bl[:, 0:64].bitcast(BF16)[:, 0:nat_rows],
                                   st.pspl[0:nat_rows, 2048 + dc * 128:2048 + (dc + 1) * 128],
                                   st.idb[0:nat_rows, 0:nat_rows]))
        t_tp = st.tt[1]

        @block.vector
        def _(v, dc=dc, bh=bh, bl=bl, t_tp=t_tp):
            wt(v, st.tt, t_tp)
            if dc == 0:
                wt(v, st.sd, st.ptp_free)
            v.tensor_copy(out=st.ptp[:, dc * nat_rows:(dc + 1) * nat_rows],
                          in_=bh[:, 0:64].bitcast(BF16)[:, 0:nat_rows])
            inc(st.vt, v.tensor_copy(out=st.ptp[:, 2048 + dc * nat_rows:2048 + (dc + 1) * nat_rows],
                                     in_=bl[:, 0:64].bitcast(BF16)[:, 0:nat_rows]))
    v_done = st.vt[1]

    @block.scalar
    def _(s):
        wt(s, st.vt, v_done)
        s.dma_start(out=dst_h[0:dcols, col0:col0 + nat_rows].rearrange("(dc p) o -> p dc o", p=128),
                    in_=st.ptp[:, 0:ncolt * nat_rows]).then_inc(st.sd[0], dinc(st.sd))
        s.dma_start(out=dst_l[0:dcols, col0:col0 + nat_rows].rearrange("(dc p) o -> p dc o", p=128),
                    in_=st.ptp[:, 2048:2048 + ncolt * nat_rows]).then_inc(st.sd[0], dinc(st.sd))
    st.ptp_free = st.sd[1]


def emit_prep(st):
    nc, block = st.nc, st.block
    st.ptp_free = 0

    for ot in range(34):
        nat_rows = 128 if ot < 33 else 16
        par = ot % 2
        natv = st.pnat[0:nat_rows, par * 2048: par * 2048 + 1024]

        @block.sync
        def _(s, ot=ot, natv=natv, nat_rows=nat_rows):
            if ot == 0:
                wt(s, st.tt, st.t_bc)
            if ot >= 2:
                wt(s, st.vt)
            s.dma_start(out=natv, in_=st.ipw[ot * 128: ot * 128 + nat_rows, :]).then_inc(st.sy[0], dinc(st.sy))
        emit_norm_scale(st, natv, nat_rows, False, float(2.0 ** -5), st.sy[1])

        @block.vector
        def _(v, natv=natv, nat_rows=nat_rows):
            wt(v, st.vt)
            v.tensor_scalar(out=natv, in0=natv, scalar1=st.psc[0:nat_rows, 5:6], scalar2=None, op0=AL.mult)
            v.tensor_copy(out=st.pspl[0:nat_rows, 0:1024], in_=natv)
            inc(st.vt, v.tensor_tensor(out=st.pspl[0:nat_rows, 2048:3072], in0=natv,
                                       in1=st.pspl[0:nat_rows, 0:1024], op=AL.subtract))
        emit_prep_store(st, nat_rows, 1024, st.s_ipWh, st.s_ipWl, ot * 128)

    for ot in range(8):
        par = ot % 2
        natv = st.pnat[:, par * 2048:(par + 1) * 2048]

        @block.sync
        def _(s, ot=ot, natv=natv):
            wt(s, st.vt)
            s.dma_start(out=natv, in_=st.opw[ot * 128:(ot + 1) * 128, :]).then_inc(st.sy[0], dinc(st.sy))
        emit_norm_scale(st, natv, 128, True, 1.0, st.sy[1])

        @block.vector
        def _(v, natv=natv):
            v.tensor_scalar(out=natv, in0=natv, scalar1=st.psc[:, 5:6], scalar2=None, op0=AL.mult)
            v.tensor_copy(out=st.pspl[:, 0:2048], in_=natv)
            inc(st.vt, v.tensor_tensor(out=st.pspl[:, 2048:4096], in0=natv,
                                       in1=st.pspl[:, 0:2048], op=AL.subtract))
        emit_prep_store(st, 128, 2048, st.s_opWh, st.s_opWl, ot * 128)

    for ot in range(8):
        par = ot % 2
        natv = st.pnat[:, par * 2048: par * 2048 + 1024]

        @block.sync
        def _(s, ot=ot, natv=natv):
            wt(s, st.vt)
            s.dma_start(out=natv, in_=st.fw[ot * 128:(ot + 1) * 128, :]).then_inc(st.sy[0], dinc(st.sy))
            s.dma_start(out=st.qs[:, 96:128], in_=st.fdw[ot * 128:(ot + 1) * 128, :]).then_inc(st.sy[0], dinc(st.sy))
        dma_d = st.sy[1]

        @block.scalar
        def _(s, dma_d=dma_d):
            wt(s, st.sy, dma_d)
            s.activation(out=st.qs[:, 128:160] if False else st.q1[:, 0:32],
                         in_=st.qs[:, 96:128], func=AF.Tanh, scale=4.0)
            s.activation(out=st.q2[:, 0:32], in_=st.q1[:, 0:32], func=AF.Exp, scale=0.5 * LN2)
            inc(st.ct, s.activation(out=st.q2[:, 32:64], in_=st.q1[:, 0:32], func=AF.Exp, scale=-0.5 * LN2))
        s_dd = st.ct[1]

        @block.vector
        def _(v, natv=natv, s_dd=s_dd):
            wt(v, st.ct, s_dd)
            v.tensor_scalar(out=st.cin[:, 32:64], in0=st.q2[:, 0:32], scalar1=0.25,
                            scalar2=None, op0=AL.mult)
            v.tensor_scalar(out=st.cin[:, 0:32], in0=st.q2[:, 32:64], scalar1=4.0,
                            scalar2=None, op0=AL.mult)
            mxfp4_quant_emit(v, st, natv, 128, 1024, -16, -1,
                             st.cin[:, 32:64], st.cin[:, 0:32], st.xn0c[:, 0:1024])
            v.tensor_copy(out=st.pspl[:, 0:1024], in_=st.xn0c[:, 0:1024])
            inc(st.vt, v.tensor_tensor(out=st.pspl[:, 2048:3072], in0=st.xn0c[:, 0:1024],
                                       in1=st.pspl[:, 0:1024], op=AL.subtract))
        emit_prep_store(st, 128, 1024, st.s_fqWh, st.s_fqWl, ot * 128)

    st.prep_sd = st.sd[1]


def emit_ln(st, src_ap, dst_ap, w_bcast, b_bcast, ncols, dump=None):
    nc, block = st.nc, st.block

    if dump is None:
        dump = st.lndump

    @block.scalar
    def _(s):
        wt(s, st.vt)
        inc(st.ct, s.activation(out=dump[:, 0:ncols], in_=src_ap, func=AF.Identity,
                                accum_out=st.sm2[:, 4:5]))
    s_sum = st.ct[1]

    @block.vector
    def _(v):
        wt(v, st.ct, s_sum)
        inc(st.vt, v.tensor_scalar_mul(st.sm2[:, 4:5], st.sm2[:, 4:5], -1.0 / ncols))
    v_mu = st.vt[1]

    @block.scalar
    def _(s):
        wt(s, st.vt, v_mu)
        inc(st.ct, s.activation(out=dump[:, 0:ncols], in_=src_ap, func=AF.Square,
                                bias=st.sm2[:, 4:5], accum_out=st.sm2[:, 5:6]))
    s_sq = st.ct[1]

    @block.vector
    def _(v):
        wt(v, st.ct, s_sq)
        inc(st.vt, v.tensor_scalar(out=st.sm2[:, 6:7], in0=st.sm2[:, 5:6], scalar1=1.0 / ncols,
                                   scalar2=LN_EPS, op0=AL.mult, op1=AL.add))
    v_v = st.vt[1]

    @block.scalar
    def _(s):
        wt(s, st.vt, v_v)
        s.activation(out=st.sm2[:, 7:8], in_=st.sm2[:, 6:7], func=AF.Ln)
        inc(st.ct, s.activation(out=st.sm2[:, 5:6], in_=st.sm2[:, 7:8], func=AF.Exp, scale=-0.5))
    s_seed = st.ct[1]

    @block.vector
    def _(v):
        wt(v, st.ct, s_seed)
        y = st.sm2[:, 5:6]
        for _i in range(2):
            inc(st.vt, v.tensor_tensor(out=st.sm2[:, 7:8], in0=y, in1=y, op=AL.mult))
            wt(v, st.vt)
            inc(st.vt, v.tensor_tensor(out=st.sm2[:, 7:8], in0=st.sm2[:, 7:8], in1=st.sm2[:, 6:7], op=AL.mult))
            wt(v, st.vt)
            inc(st.vt, v.tensor_scalar(out=st.sm2[:, 7:8], in0=st.sm2[:, 7:8], scalar1=-0.5, scalar2=1.5,
                                       op0=AL.mult, op1=AL.add))
            wt(v, st.vt)
            inc(st.vt, v.tensor_tensor(out=y, in0=y, in1=st.sm2[:, 7:8], op=AL.mult))
            wt(v, st.vt)
        v.tensor_scalar(out=dst_ap, in0=src_ap, scalar1=st.sm2[:, 4:5], scalar2=None, op0=AL.add)
        v.tensor_scalar(out=dst_ap, in0=dst_ap, scalar1=y, scalar2=None, op0=AL.mult)
        if w_bcast is not None:
            v.tensor_tensor(out=dst_ap, in0=dst_ap, in1=w_bcast, op=AL.mult)
            v.tensor_tensor(out=dst_ap, in0=dst_ap, in1=b_bcast, op=AL.add)
        inc(st.vt, v.engine_nop())


def emit_l0_chunk(st, c):
    nc, block = st.nc, st.block
    xsl = st.X[:, (2 * c) * D:(2 * c + 2) * D]

    @block.tensor
    def _(t):
        wt(t, st.vt)
        wt(t, st.ct)
        wt(t, st.ipe)
        inc(st.tt, t.matmul(st.pb[7][0:1, 4:5], st.ones1[:, 0:1], st.ones1[0:1, 0:1], start=True, stop=True))

    @block.vector
    def _(v):
        wt(v, st.ct)
        wt(v, st.tt)
        wt(v, st.ipc)
        wt(v, st.opc)
        wt(v, st.fqc)
        inc(st.vt, v.engine_nop())

    @block.scalar
    def _(s):
        wt(s, st.vt)
        wt(s, st.tt)
        inc(st.ct, s.activation(out=st.sm1[0:1, 8:9], in_=st.sm1[0:1, 0:1], func=AF.Identity))

    for tsub in range(2):
        emit_ln(st, xsl[:, tsub * D:(tsub + 1) * D], st.xn0c[:, tsub * D:(tsub + 1) * D], None, None, D)
        emit_ln(st, st.xn0c[:, tsub * D:(tsub + 1) * D], st.uc[:, tsub * D:(tsub + 1) * D],
                st.xwb[:, :], st.xbb[:, :], D)

    @block.vector
    def _(v):
        if c > 0:
            wt(v, st.ipc, 34 * c)
        uh = st.q1[:, :].bitcast(BF16)
        ul = st.q2[:, :].bitcast(BF16)
        v.tensor_copy(out=uh, in_=st.uc[:, :])
        inc(st.vt, v.tensor_tensor(out=ul, in0=st.uc[:, :], in1=uh, op=AL.subtract))
    v_usp = st.vt[1]

    if st.debug and c == 0:
        @block.scalar
        def _(s):
            wt(s, st.vt, v_usp)
            s.dma_start(out=st.dbg["d_xn0"], in_=st.xn0c[:, :]).then_inc(st.sd[0], dinc(st.sd))
            s.dma_start(out=st.dbg["d_u"], in_=st.uc[:, :]).then_inc(st.sd[0], dinc(st.sd))

        @block.vector
        def _(v):
            wt(v, st.sd)
            inc(st.vt, v.engine_nop())


    for i in range(16):
        tsub, dc = divmod(i, 8)
        bnk = st.pb[4 + (i % 2)]

        @block.tensor
        def _(t, i=i, tsub=tsub, dc=dc, bnk=bnk):
            wt(t, st.vt)
            src_h = st.q1[:, :].bitcast(BF16)[:, tsub * 1024 + dc * 128: tsub * 1024 + (dc + 1) * 128]
            src_l = st.q2[:, :].bitcast(BF16)[:, tsub * 1024 + dc * 128: tsub * 1024 + (dc + 1) * 128]
            t.transpose(bnk[:, 0:64].bitcast(BF16), src_h, st.idb[:, :])
            inc(st.tt, t.transpose(bnk[:, 64:128].bitcast(BF16), src_l, st.idb[:, :]))
        t_tp = st.tt[1]

        @block.vector
        def _(v, tsub=tsub, dc=dc, bnk=bnk, t_tp=t_tp):
            wt(v, st.tt, t_tp)
            v.tensor_copy(out=st.uT[:, dc * 256 + tsub * 128: dc * 256 + (tsub + 1) * 128],
                          in_=bnk[:, 0:64].bitcast(BF16))
            inc(st.vt, v.tensor_copy(out=st.uT[:, 2048 + dc * 256 + tsub * 128: 2048 + dc * 256 + (tsub + 1) * 128],
                                     in_=bnk[:, 64:128].bitcast(BF16)))
    v_uT = st.vt[1]

    for ot in range(34):
        nat_rows = 128 if ot < 33 else 16
        slot = ot % 2
        unit = c * 34 + ot

        @block.sync
        def _(s, ot=ot, slot=slot, unit=unit, nat_rows=nat_rows):
            if unit == 0:
                wt(s, st.sd, st.prep_sd)
            if unit >= 2:
                wt(s, st.ipc, unit - 1)
            s.dma_start(out=st.wip[:, slot * 2048: slot * 2048 + 8 * nat_rows],
                        in_=st.s_ipWh[:, ot * 128: ot * 128 + nat_rows].rearrange("(dc p) o -> p dc o", p=128)).then_inc(st.sy[0], dinc(st.sy))
            s.dma_start(out=st.wip[:, slot * 2048 + 1024: slot * 2048 + 1024 + 8 * nat_rows],
                        in_=st.s_ipWl[:, ot * 128: ot * 128 + nat_rows].rearrange("(dc p) o -> p dc o", p=128)).then_inc(st.sy[0], dinc(st.sy))
        w_done = st.sy[1]

        bnk = st.pb[ot % 2]

        @block.tensor
        def _(t, ot=ot, slot=slot, unit=unit, nat_rows=nat_rows, bnk=bnk, w_done=w_done):
            wt(t, st.sy, w_done)
            if ot == 0:
                wt(t, st.vt, v_uT)
            if unit >= 2:
                wt(t, st.ipe, unit - 1)
            last = None
            for dc in range(8):
                wh = st.wip[:, slot * 2048 + dc * nat_rows: slot * 2048 + (dc + 1) * nat_rows]
                wl = st.wip[:, slot * 2048 + 1024 + dc * nat_rows: slot * 2048 + 1024 + (dc + 1) * nat_rows]
                uh = st.uT[:, dc * 256:(dc + 1) * 256]
                ul = st.uT[:, 2048 + dc * 256: 2048 + (dc + 1) * 256]
                t.matmul(bnk[0:nat_rows, 0:256], wh, uh, start=(dc == 0), stop=False)
                t.matmul(bnk[0:nat_rows, 0:256], wh, ul, start=False, stop=False)
                last = t.matmul(bnk[0:nat_rows, 0:256], wl, uh, start=False, stop=(dc == 7))
            inc(st.ipc, last)

        if ot < 16:
            @block.scalar
            def _(s, ot=ot, unit=unit, bnk=bnk):
                wt(s, st.ipc, unit + 1)
                a = s.activation(out=st.gT[:, ot * 256:(ot + 1) * 256], in_=bnk[:, 0:256], func=AF.Silu)
                inc(st.ipe, a)
        elif ot < 33:
            @block.vector
            def _(v, ot=ot, unit=unit, bnk=bnk):
                wt(v, st.ipc, unit + 1)
                inc(st.ipe, v.tensor_copy(out=st.xBCc[:, (ot - 16) * 256:(ot - 15) * 256], in_=bnk[:, 0:256]))
        else:
            @block.vector
            def _(v, unit=unit, bnk=bnk):
                wt(v, st.ipc, unit + 1)
                inc(st.ipe, v.tensor_copy(out=st.dtp[0:16, 0:256], in_=bnk[0:16, 0:256]))
    ip_done = st.ipe[1]

    if st.debug and c == 1:
        @block.scalar
        def _(s):
            wt(s, st.ipe, ip_done)
            s.dma_start(out=st.dbg["d_xBC"], in_=st.xBCc[:, :]).then_inc(st.sd[0], dinc(st.sd))

        @block.vector
        def _(v):
            wt(v, st.sd)
            inc(st.vt, v.engine_nop())

    @block.vector
    def _(v):
        wt(v, st.ipe, ip_done)
        for ct in range(17):
            tile = st.xBCc[:, ct * 256:(ct + 1) * 256]
            v.tensor_copy(out=st.cin[:, 0:3], in_=st.hist[:, ct * 3:(ct + 1) * 3])
            inc(st.vt, v.tensor_copy(out=st.cin[:, 3:259], in_=tile))
            wt(v, st.vt)
            v.tensor_scalar(out=tile, in0=st.cin[:, 0:256], scalar1=st.cwT[:, 4 * ct:4 * ct + 1],
                            scalar2=None, op0=AL.mult)
            for k in range(1, 4):
                v.scalar_tensor_tensor(out=tile, in0=st.cin[:, k:k + 256],
                                       scalar=st.cwT[:, 4 * ct + k:4 * ct + k + 1],
                                       in1=tile, op0=AL.mult, op1=AL.add)
            v.tensor_copy(out=st.hist[:, ct * 3:(ct + 1) * 3], in_=st.cin[:, 256:259])
        inc(st.vt, v.engine_nop())
    v_conv = st.vt[1]

    @block.scalar
    def _(s):
        wt(s, st.vt, v_conv)
        for ct in range(17):
            tile = st.xBCc[:, ct * 256:(ct + 1) * 256]
            s.activation(out=tile, in_=tile, func=AF.Silu, bias=st.cbT[:, ct:ct + 1])
        inc(st.ct, s.activation(out=st.dtp[0:16, 256:512], in_=st.dtp[0:16, 0:256],
                                func=AF.Abs, bias=st.dtbc[:, :]))
    s_conv = st.ct[1]

    if st.debug and c == 0:
        @block.scalar
        def _(s):
            s.dma_start(out=st.dbg["d_gT"], in_=st.gT[:, :]).then_inc(st.sd[0], dinc(st.sd))

        @block.vector
        def _(v):
            wt(v, st.sd)
            inc(st.vt, v.engine_nop())

    if st.debug and c == 1:
        @block.vector
        def _(v):
            wt(v, st.ct, s_conv)
            inc(st.vt, v.engine_nop())

        @block.scalar
        def _(s):
            wt(s, st.vt)
            s.dma_start(out=st.dbg["d_xBCp"], in_=st.xBCc[:, :]).then_inc(st.sd[0], dinc(st.sd))

        @block.vector
        def _(v):
            wt(v, st.sd)
            inc(st.vt, v.engine_nop())

    @block.vector
    def _(v):
        wt(v, st.ct, s_conv)
        inc(st.vt, v.tensor_scalar(out=st.dtp[0:16, 0:256], in0=st.dtp[0:16, 0:256],
                                   scalar1=st.dtbc[:, :], scalar2=None, op0=AL.add))
    v_dtx = st.vt[1]

    @block.scalar
    def _(s):
        wt(s, st.vt, v_dtx)
        s.activation(out=st.dtp[0:16, 512:768], in_=st.dtp[0:16, 256:512], func=AF.Exp, scale=-1.0)
        inc(st.ct, s.activation(out=st.dtp[0:16, 512:768], in_=st.dtp[0:16, 512:768], func=AF.Ln, bias=1.0))
    s_dt1 = st.ct[1]

    @block.vector
    def _(v):
        wt(v, st.ct, s_dt1)
        v.scalar_tensor_tensor(out=st.dtp[0:16, 256:512], in0=st.dtp[0:16, 0:256], scalar=0.0,
                               in1=st.dtp[0:16, 512:768], op0=AL.max, op1=AL.add)
        v.tensor_scalar(out=st.dtp[0:16, 512:768], in0=st.dtp[0:16, 256:512],
                        scalar1=st.Acol[:, :], scalar2=None, op0=AL.mult)
        v.tensor_tensor_scan(out=st.dtp[0:16, 768:1024], data0=st.dtp[0:16, 512:768],
                             data1=st.dtp[0:16, 512:768], initial=0.0, op0=AL.add, op1=AL.bypass)
        inc(st.vt, v.tensor_scalar_mul(st.dtp[0:16, 1024:1280], st.dtp[0:16, 768:1024], -1.0))
    v_dt2 = st.vt[1]

    @block.scalar
    def _(s):
        wt(s, st.vt, v_dt2)
        s.activation(out=st.dtp[0:16, 1280:1536], in_=st.dtp[0:16, 1024:1280], func=AF.Exp,
                     bias=st.dtp[0:16, 1023:1024])
        s.activation(out=st.dtp[0:16, 1536:1792], in_=st.dtp[0:16, 768:1024], func=AF.Exp)
        inc(st.ct, s.activation(out=st.dtp[0:16, 1792:1793], in_=st.dtp[0:16, 1023:1024], func=AF.Exp))
    s_dt3 = st.ct[1]

    @block.vector
    def _(v):
        wt(v, st.ct, s_dt3)
        inc(st.vt, v.tensor_tensor(out=st.dtp[0:16, 1280:1536], in0=st.dtp[0:16, 1280:1536],
                                   in1=st.dtp[0:16, 256:512], op=AL.mult))
    v_dt4 = st.vt[1]

    @block.tensor
    def _(t):
        wt(t, st.vt, v_dt4)
        for js in range(2):
            t.transpose(st.pb[2 + js][:, 0:16], st.dtp[0:16, 256 + js * 128:256 + (js + 1) * 128],
                        st.idf[0:16, 0:16])
            t.transpose(st.pb[4 + js][:, 0:16], st.dtp[0:16, 1280 + js * 128:1280 + (js + 1) * 128],
                        st.idf[0:16, 0:16])
            t.transpose(st.pb[6 + js][:, 0:16], st.dtp[0:16, 1024 + js * 128:1024 + (js + 1) * 128],
                        st.idf[0:16, 0:16])
        inc(st.tt, t.transpose(st.pb[2][0:1, 16:32], st.dtp[0:16, 1792:1793], st.idf[0:16, 0:16]))
    t_dts = st.tt[1]

    @block.vector
    def _(v):
        wt(v, st.tt, t_dts)
        for js in range(2):
            v.tensor_copy(out=st.sS[:, js * 16:(js + 1) * 16], in_=st.pb[2 + js][:, 0:16])
            v.tensor_copy(out=st.sS[:, 32 + js * 16:32 + (js + 1) * 16], in_=st.pb[4 + js][:, 0:16])
            v.tensor_copy(out=st.sS[:, 64 + js * 16:64 + (js + 1) * 16], in_=st.pb[6 + js][:, 0:16])
        inc(st.vt, v.tensor_copy(out=st.srow[0:1, 0:16], in_=st.pb[2][0:1, 16:32]))
    v_dts = st.vt[1]

    @block.tensor
    def _(t):
        wt(t, st.vt, v_dts)
        inc(st.tt, t.matmul(st.pb[3][:, 16:32], st.ones1[:, :], st.srow[0:1, 0:16], start=True, stop=True))
    t_cd = st.tt[1]

    @block.tensor
    def _(t):
        wt(t, st.ct, s_conv)
        inc(st.tt, t.matmul(st.pb[6][0:64, 0:256], st.sel64[:, :],
                            st.xBCc[:, 16 * 256:17 * 256], start=True, stop=True))
    t_cf = st.tt[1]

    @block.vector
    def _(v):
        wt(v, st.tt, t_cf)
        wt(v, st.tt, t_cd)
        v.tensor_copy(out=st.Cf32[:, :], in_=st.pb[6][0:64, 0:256])
        v.tensor_copy(out=st.cdB[:, :], in_=st.pb[3][:, 16:32])
        BT = st.xBCc[0:64, 16 * 256:17 * 256]
        v.tensor_copy(out=st.bcs[:, 0:256], in_=BT)
        v.tensor_copy(out=st.bcs[:, 512:768], in_=st.Cf32[:, :])
        v.tensor_tensor(out=st.bcs[:, 256:512], in0=BT, in1=st.bcs[:, 0:256], op=AL.subtract)
        inc(st.vt, v.tensor_tensor(out=st.bcs[:, 768:1024], in0=st.Cf32[:, :],
                                   in1=st.bcs[:, 512:768], op=AL.subtract))
    v_bc = st.vt[1]

    @block.tensor
    def _(t):
        wt(t, st.vt, v_bc)
        t.transpose(st.pb[4][:, 0:64], st.xBCc[0:64, 16 * 256 + 0 * 128:16 * 256 + 1 * 128], st.idf[0:64, 0:64])
        inc(st.tt, t.transpose(st.pb[5][:, 0:64], st.xBCc[0:64, 16 * 256 + 1 * 128:16 * 256 + 2 * 128],
                               st.idf[0:64, 0:64]))
    t_btok = st.tt[1]

    @block.vector
    def _(v):
        wt(v, st.tt, t_btok)
        v.tensor_copy(out=st.btok[:, 0:64], in_=st.pb[4][:, 0:64])
        inc(st.vt, v.tensor_copy(out=st.btok[:, 64:128], in_=st.pb[5][:, 0:64]))
    v_btok = st.vt[1]

    if st.debug and c == 0:
        @block.scalar
        def _(s):
            wt(s, st.vt, v_btok)
            s.dma_start(out=st.dbg["d_dt"], in_=st.dtp[:, :]).then_inc(st.sd[0], dinc(st.sd))

        @block.vector
        def _(v):
            wt(v, st.sd)
            inc(st.vt, v.engine_nop())

    if st.debug and c == 1:
        @block.scalar
        def _(s):
            wt(s, st.vt, v_btok)
            s.dma_start(out=st.dbg["d_dt1"], in_=st.dtp[:, :]).then_inc(st.sd[0], dinc(st.sd))

        @block.vector
        def _(v):
            wt(v, st.sd)
            inc(st.vt, v.engine_nop())

    for h in range(NH):
        xh = st.xBCc[:, h * 256:(h + 1) * 256]

        @block.vector
        def _(v, h=h):
            wt(v, st.vt)
            v.tensor_scalar(out=st.wBh[:, 0:64], in0=st.btok[:, 0:64],
                            scalar1=st.sS[:, 32 + h:32 + h + 1], scalar2=None, op0=AL.mult)
            inc(st.vt, v.tensor_scalar(out=st.wBh[:, 64:128], in0=st.btok[:, 64:128],
                                       scalar1=st.sS[:, 48 + h:48 + h + 1], scalar2=None, op0=AL.mult))
        v_wb = st.vt[1]

        @block.tensor
        def _(t, h=h, xh=xh, v_wb=v_wb):
            wt(t, st.vt, v_wb)
            t.transpose(st.pb[4][:, 0:128], xh[:, 0:128], st.idf[:, :])
            inc(st.tt, t.transpose(st.pb[4][:, 128:256], xh[:, 128:256], st.idf[:, :]))
            for js in range(2):
                Bh = st.bcs[:, js * 128:(js + 1) * 128]
                Bl = st.bcs[:, 256 + js * 128:256 + (js + 1) * 128]
                t.matmul(st.pb[js][:, 0:256], Bh, st.bcs[:, 512:768], start=True, stop=False)
                t.matmul(st.pb[js][:, 0:256], Bh, st.bcs[:, 768:1024], start=False, stop=False)
                t.matmul(st.pb[js][:, 0:256], Bl, st.bcs[:, 512:768], start=False, stop=True)
            t.matmul(st.pb[2][:, 0:256], st.sel16[:, h * 128:(h + 1) * 128],
                     st.dtp[0:16, 768:1024], start=True, stop=True)
            inc(st.tt, t.matmul(st.pb[3][:, 0:256], st.sel16[:, h * 128:(h + 1) * 128],
                                st.dtp[0:16, 1536:1792], start=True, stop=True))
        t_sd = st.tt[1]

        @block.vector
        def _(v, h=h, t_sd=t_sd):
            wt(v, st.tt, t_sd)
            v.tensor_copy(out=st.xhT[:, 0:128], in_=st.pb[4][:, 0:128])
            inc(st.vt, v.tensor_copy(out=st.xhT[:, 128:256], in_=st.pb[4][:, 128:256]))
        v_xh = st.vt[1]

        for js in range(2):
            @block.vector
            def _(v, h=h, js=js, t_sd=t_sd):
                wt(v, st.tt, t_sd)
                inc(st.vt, v.tensor_scalar(out=st.Eb[:, :], in0=st.pb[2][:, 0:256],
                                           scalar1=st.sS[:, 64 + js * 16 + h:64 + js * 16 + h + 1],
                                           scalar2=0.0, op0=AL.add, op1=AL.min))
            v_d2 = st.vt[1]

            @block.scalar
            def _(s, v_d2=v_d2):
                wt(s, st.vt, v_d2)
                inc(st.ct, s.activation(out=st.Eb[:, :], in_=st.Eb[:, :], func=AF.Exp))
            s_e = st.ct[1]

            @block.vector
            def _(v, h=h, js=js, s_e=s_e):
                wt(v, st.ct, s_e)
                v.scalar_tensor_tensor(out=st.attb[:, js * 256:(js + 1) * 256], in0=st.Eb[:, :],
                                       scalar=st.sS[:, js * 16 + h:js * 16 + h + 1],
                                       in1=st.pb[js][:, 0:256], op0=AL.mult, op1=AL.mult)
                inc(st.vt, v.tensor_tensor(out=st.attb[:, js * 256:(js + 1) * 256],
                                           in0=st.attb[:, js * 256:(js + 1) * 256],
                                           in1=st.msk[:, js * 256:(js + 1) * 256], op=AL.mult))
        v_att = st.vt[1]

        @block.tensor
        def _(t, h=h, v_att=v_att, v_xh=v_xh):
            wt(t, st.vt, v_att)
            for js in range(2):
                t.matmul(st.pb[5][:, 0:256], st.xhT[:, js * 128:(js + 1) * 128],
                         st.attb[:, js * 256:(js + 1) * 256], start=(js == 0), stop=(js == 1))
            for js in range(2):
                t.matmul(st.pb[6][0:64, 0:128], st.wBh[:, js * 64:(js + 1) * 64],
                         st.xhT[:, js * 128:(js + 1) * 128], start=(js == 0), stop=(js == 1))
            inc(st.tt, t.matmul(st.pb[7][:, 0:256], st.PS[0:64, h * 128:(h + 1) * 128],
                                st.Cf32[:, :], start=True, stop=True))
        t_y = st.tt[1]

        @block.vector
        def _(v, h=h, t_y=t_y):
            wt(v, st.tt, t_y)
            v.tensor_copy(out=st.attb[:, 0:256], in_=st.pb[3][:, 0:256])
            v.tensor_tensor(out=st.Eb[:, :], in0=st.pb[7][:, 0:256], in1=st.attb[:, 0:256], op=AL.mult)
            v.tensor_tensor(out=st.Eb[:, :], in0=st.Eb[:, :], in1=st.pb[5][:, 0:256], op=AL.add)
            v.scalar_tensor_tensor(out=st.Eb[:, :], in0=st.xBCc[:, h * 256:(h + 1) * 256],
                                   scalar=st.DB[:, h:h + 1], in1=st.Eb[:, :], op0=AL.mult, op1=AL.add)
            v.tensor_tensor(out=st.gT[:, h * 256:(h + 1) * 256], in0=st.Eb[:, :],
                            in1=st.gT[:, h * 256:(h + 1) * 256], op=AL.mult)
            inc(st.vt, v.scalar_tensor_tensor(out=st.PS[0:64, h * 128:(h + 1) * 128],
                                              in0=st.PS[0:64, h * 128:(h + 1) * 128],
                                              scalar=st.cdB[0:64, h:h + 1],
                                              in1=st.pb[6][0:64, 0:128], op0=AL.mult, op1=AL.add))
    v_scan = st.vt[1]

    for dc in range(16):
        bq = st.q1 if dc % 2 == 0 else st.q2

        @block.scalar
        def _(s, dc=dc, bq=bq, v_scan=v_scan):
            if dc == 0:
                wt(s, st.vt, v_scan)
            if dc >= 2:
                wt(s, st.tt)
            inc(st.ct, s.activation(out=bq[:, 0:256], in_=st.gT[:, dc * 256:(dc + 1) * 256], func=AF.Square))
        s_gsq = st.ct[1]

        @block.tensor
        def _(t, dc=dc, bq=bq, s_gsq=s_gsq):
            wt(t, st.ct, s_gsq)
            inc(st.tt, t.matmul(st.pb[4][0:1, 0:256], st.onesc[:, :], bq[:, 0:256],
                                start=(dc == 0), stop=(dc == 15)))
    t_ss = st.tt[1]

    @block.vector
    def _(v):
        wt(v, st.tt, t_ss)
        inc(st.vt, v.tensor_scalar(out=st.sm1[0:1, 16:272], in0=st.pb[4][0:1, 0:256],
                                   scalar1=1.0 / DI, scalar2=RMS_EPS, op0=AL.mult, op1=AL.add))
    v_rp = st.vt[1]

    @block.scalar
    def _(s):
        wt(s, st.vt, v_rp)
        s.activation(out=st.srow[0:1, 0:256], in_=st.sm1[0:1, 16:272], func=AF.Ln)
        inc(st.ct, s.activation(out=st.srow[0:1, 0:256], in_=st.srow[0:1, 0:256], func=AF.Exp, scale=-0.5))
    s_rs = st.ct[1]

    @block.vector
    def _(v):
        wt(v, st.ct, s_rs)
        y = st.srow[0:1, 0:256]
        a = st.sm1[0:1, 16:272]
        w2 = st.sm1[0:1, 272:528]
        for _i in range(2):
            v.tensor_tensor(out=w2, in0=y, in1=y, op=AL.mult)
            v.tensor_tensor(out=w2, in0=w2, in1=a, op=AL.mult)
            v.tensor_scalar(out=w2, in0=w2, scalar1=-0.5, scalar2=1.5, op0=AL.mult, op1=AL.add)
            v.tensor_tensor(out=y, in0=y, in1=w2, op=AL.mult)
        inc(st.vt, v.engine_nop())
    v_rms = st.vt[1]

    @block.tensor
    def _(t):
        wt(t, st.vt, v_rms)
        inc(st.tt, t.matmul(st.pb[3][:, 0:256], st.ones1[:, :], st.srow[0:1, 0:256], start=True, stop=True))
    t_sb = st.tt[1]

    for dcq in range(4):
        slot_g = dcq % 2

        @block.vector
        def _(v, dcq=dcq, slot_g=slot_g, t_sb=t_sb):
            if dcq == 0:
                wt(v, st.tt, t_sb)
            if dcq >= 2:
                wt(v, st.opc, c * 16 + (dcq - 1) * 4)
            for dcl in range(4):
                dc = dcq * 4 + dcl
                v.tensor_scalar(out=st.q1[:, dcl * 256:(dcl + 1) * 256],
                                in0=st.gT[:, dc * 256:(dc + 1) * 256],
                                scalar1=st.rwT[:, dc:dc + 1], scalar2=None, op0=AL.mult)
            for dcl in range(4):
                v.tensor_tensor(out=st.q1[:, dcl * 256:(dcl + 1) * 256],
                                in0=st.q1[:, dcl * 256:(dcl + 1) * 256],
                                in1=st.pb[3][:, 0:256], op=AL.mult)
            for dcl in range(4):
                v.tensor_copy(out=st.ghl[:, slot_g * 2048 + dcl * 512: slot_g * 2048 + dcl * 512 + 256],
                              in_=st.q1[:, dcl * 256:(dcl + 1) * 256])
            for dcl in range(4):
                v.tensor_tensor(out=st.ghl[:, slot_g * 2048 + dcl * 512 + 256: slot_g * 2048 + (dcl + 1) * 512],
                                in0=st.q1[:, dcl * 256:(dcl + 1) * 256],
                                in1=st.ghl[:, slot_g * 2048 + dcl * 512: slot_g * 2048 + dcl * 512 + 256],
                                op=AL.subtract)
            inc(st.vt, v.engine_nop())
        v_spl = st.vt[1]

        for oc in range(2):
            for dh in range(2):
                unit = c * 16 + dcq * 4 + oc * 2 + dh
                slot_w = unit % 2
                r0 = dcq * 512 + dh * 256

                @block.sync
                def _(s, oc=oc, unit=unit, slot_w=slot_w, r0=r0):
                    if unit >= 2:
                        wt(s, st.opc, unit - 1)
                    s.dma_start(out=st.wop[:, slot_w * 2048: slot_w * 2048 + 1024],
                                in_=st.s_opWh[r0:r0 + 256, oc * 512:(oc + 1) * 512]
                                .rearrange("(dc p) o -> p dc o", p=128)).then_inc(st.sy[0], dinc(st.sy))
                    s.dma_start(out=st.wop[:, slot_w * 2048 + 1024: (slot_w + 1) * 2048],
                                in_=st.s_opWl[r0:r0 + 256, oc * 512:(oc + 1) * 512]
                                .rearrange("(dc p) o -> p dc o", p=128)).then_inc(st.sy[0], dinc(st.sy))
                w_done = st.sy[1]

                @block.tensor
                def _(t, dcq=dcq, oc=oc, dh=dh, unit=unit, slot_w=slot_w, slot_g=slot_g,
                      w_done=w_done, v_spl=v_spl):
                    wt(t, st.sy, w_done)
                    wt(t, st.vt, v_spl)
                    last = None
                    for i in range(2):
                        dc = dcq * 4 + dh * 2 + i
                        dcl = dh * 2 + i
                        wh = st.wop[:, slot_w * 2048 + i * 512: slot_w * 2048 + (i + 1) * 512]
                        wl = st.wop[:, slot_w * 2048 + 1024 + i * 512: slot_w * 2048 + 1024 + (i + 1) * 512]
                        for tsub in range(2):
                            gh = st.ghl[:, slot_g * 2048 + dcl * 512 + tsub * 128:
                                        slot_g * 2048 + dcl * 512 + (tsub + 1) * 128]
                            gl = st.ghl[:, slot_g * 2048 + dcl * 512 + 256 + tsub * 128:
                                        slot_g * 2048 + dcl * 512 + 256 + (tsub + 1) * 128]
                            bnk = st.pb[4 + 2 * oc + tsub]
                            t.matmul(bnk[:, 0:512], gh, wh, start=(dc == 0 and tsub in (0, 1)) if False else (dc == 0), stop=False)
                            t.matmul(bnk[:, 0:512], gh, wl, start=False, stop=False)
                            last = t.matmul(bnk[:, 0:512], gl, wh, start=False, stop=(dc == 15))
                    inc(st.opc, last)
    t_op = st.opc[1]

    @block.vector
    def _(v):
        wt(v, st.opc, t_op)
        for oc in range(2):
            for tsub in range(2):
                xa = st.X[:, (2 * c + tsub) * D + oc * 512:(2 * c + tsub) * D + (oc + 1) * 512]
                v.tensor_tensor(out=xa, in0=xa, in1=st.pb[4 + 2 * oc + tsub][:, 0:512], op=AL.add)
        inc(st.vt, v.engine_nop())
    v_op = st.vt[1]

    @block.vector
    def _(v):
        wt(v, st.vt, v_op)
        wt(v, st.ct)
        for tsub in range(2):
            mxfp4_quant_emit(v, st, st.xn0c[:, tsub * D:(tsub + 1) * D], 128, D, -8, 5,
                             st.sm2[:, 0:1].broadcast_to([128, 32]),
                             st.sm2[:, 1:2].broadcast_to([128, 32]),
                             st.uc[:, tsub * D:(tsub + 1) * D])
        qh = st.q1[:, :].bitcast(BF16)
        ql = st.q2[:, :].bitcast(BF16)
        v.tensor_copy(out=qh, in_=st.uc[:, :])
        inc(st.vt, v.tensor_tensor(out=ql, in0=st.uc[:, :], in1=qh, op=AL.subtract))
    v_xq = st.vt[1]

    if st.debug and c == 0:
        @block.scalar
        def _(s):
            wt(s, st.vt, v_xq)
            s.dma_start(out=st.dbg["d_xq"], in_=st.uc[:, :]).then_inc(st.sd[0], dinc(st.sd))

        @block.vector
        def _(v):
            wt(v, st.sd)
            inc(st.vt, v.engine_nop())

    for i in range(16):
        tsub, dc = divmod(i, 8)
        bnk = st.pb[4 + (i % 2)]

        @block.tensor
        def _(t, i=i, tsub=tsub, dc=dc, bnk=bnk, v_xq=v_xq):
            wt(t, st.vt)
            src_h = st.q1[:, :].bitcast(BF16)[:, tsub * 1024 + dc * 128: tsub * 1024 + (dc + 1) * 128]
            src_l = st.q2[:, :].bitcast(BF16)[:, tsub * 1024 + dc * 128: tsub * 1024 + (dc + 1) * 128]
            t.transpose(bnk[:, 0:64].bitcast(BF16), src_h, st.idb[:, :])
            inc(st.tt, t.transpose(bnk[:, 64:128].bitcast(BF16), src_l, st.idb[:, :]))
        t_tp = st.tt[1]

        @block.vector
        def _(v, tsub=tsub, dc=dc, bnk=bnk, t_tp=t_tp):
            wt(v, st.tt, t_tp)
            v.tensor_copy(out=st.xqT[:, dc * 256 + tsub * 128: dc * 256 + (tsub + 1) * 128],
                          in_=bnk[:, 0:64].bitcast(BF16))
            inc(st.vt, v.tensor_copy(out=st.xqT[:, 2048 + dc * 256 + tsub * 128: 2048 + dc * 256 + (tsub + 1) * 128],
                                     in_=bnk[:, 64:128].bitcast(BF16)))
    v_xqT = st.vt[1]

    for oc in range(2):
        for dh in range(2):
            unit = c * 4 + oc * 2 + dh
            r0 = dh * 512

            @block.sync
            def _(s, oc=oc, dh=dh, unit=unit, r0=r0):
                if unit >= 1:
                    wt(s, st.fqc, unit)
                s.dma_start(out=st.wfq[:, 0:2048],
                            in_=st.s_fqWh[r0:r0 + 512, oc * 512:(oc + 1) * 512]
                            .rearrange("(dc p) o -> p dc o", p=128)).then_inc(st.sy[0], dinc(st.sy))
                s.dma_start(out=st.wfq[:, 2048:4096],
                            in_=st.s_fqWl[r0:r0 + 512, oc * 512:(oc + 1) * 512]
                            .rearrange("(dc p) o -> p dc o", p=128)).then_inc(st.sy[0], dinc(st.sy))
            w_done = st.sy[1]

            @block.tensor
            def _(t, oc=oc, dh=dh, unit=unit, w_done=w_done, vcur=st.vt[1]):
                wt(t, st.sy, w_done)
                wt(t, st.vt, vcur)
                last = None
                for i in range(4):
                    dc = dh * 4 + i
                    wh = st.wfq[:, i * 512:(i + 1) * 512]
                    wl = st.wfq[:, 2048 + i * 512:2048 + (i + 1) * 512]
                    for tsub in range(2):
                        xh_ = st.xqT[:, dc * 256 + tsub * 128: dc * 256 + (tsub + 1) * 128]
                        xl_ = st.xqT[:, 2048 + dc * 256 + tsub * 128: 2048 + dc * 256 + (tsub + 1) * 128]
                        bnk = st.pb[tsub]
                        t.matmul(bnk[:, 0:512], xh_, wh, start=(dc == 0), stop=False)
                        t.matmul(bnk[:, 0:512], xh_, wl, start=False, stop=False)
                        last = t.matmul(bnk[:, 0:512], xl_, wh, start=False, stop=(dc == 7))
                inc(st.fqc, last)
        t_fq = st.fqc[1]

        @block.vector
        def _(v, oc=oc, t_fq=t_fq):
            wt(v, st.fqc, t_fq)
            for tsub in range(2):
                xa = st.X[:, (2 * c + tsub) * D + oc * 512:(2 * c + tsub) * D + (oc + 1) * 512]
                v.tensor_tensor(out=st.q1[:, 0:512], in0=st.pb[tsub][:, 0:512],
                                in1=st.fbb[:, oc * 512:(oc + 1) * 512], op=AL.add)
                v.tensor_tensor(out=xa, in0=xa, in1=st.q1[:, 0:512], op=AL.add)
            inc(st.vt, v.engine_nop())




def emit_p2(st):
    nc, block = st.nc, st.block

    @block.vector
    def _(v):
        wt(v, st.ct)
        wt(v, st.tt)
        wt(v, st.ipc)
        wt(v, st.opc)
        wt(v, st.fqc)
        inc(st.vt, v.engine_nop())

    @block.tensor
    def _(t):
        wt(t, st.vt)
        wt(t, st.ct)
        inc(st.tt, t.matmul(st.pb[7][0:1, 4:5], st.ones1[:, 0:1], st.ones1[0:1, 0:1], start=True, stop=True))

    @block.scalar
    def _(s):
        wt(s, st.vt)
        wt(s, st.tt)
        inc(st.ct, s.activation(out=st.sm1[0:1, 8:9], in_=st.sm1[0:1, 0:1], func=AF.Identity))

    @block.sync
    def _(s):
        wt(s, st.vt)
        with nc.allow_non_contiguous_dma(reason="small L1 constants"):
            s.dma_start(out=st.rotT[:, :], in_=st.rot.rearrange("h d -> d h"),
                        allow_slow_non_contiguous=True).then_inc(st.sy[0], dinc(st.sy))
            s.dma_start(out=st.entT[:, :], in_=st.ent.rearrange("h d -> d h"),
                        allow_slow_non_contiguous=True).then_inc(st.sy[0], dinc(st.sy))
            s.dma_start(out=st.dbT[:, 0:8], in_=st.d_bk.rearrange("(a p) -> p a", p=128),
                        allow_slow_non_contiguous=True).then_inc(st.sy[0], dinc(st.sy))
            s.dma_start(out=st.dbT[:, 8:16], in_=st.d_bv.rearrange("(a p) -> p a", p=128),
                        allow_slow_non_contiguous=True).then_inc(st.sy[0], dinc(st.sy))
    st.L1LOADS = st.sy[1]

    @block.vector
    def _(v):
        wt(v, st.sy, st.L1LOADS)
        inc(st.vt, v.tensor_scalar_mul(st.rotT[:, :], st.rotT[:, :], 0.125))

    for tt2 in range(16):
        xsl = st.X[:, tt2 * D:(tt2 + 1) * D]
        emit_ln(st, xsl, st.t_tok[:, :], None, None, D, dump=st.ldmp1)

        @block.vector
        def _(v):
            v.tensor_copy(out=st.tsp[:, 0:1024], in_=st.t_tok[:, :])
            inc(st.vt, v.tensor_tensor(out=st.tsp[:, 1024:2048], in0=st.t_tok[:, :],
                                       in1=st.tsp[:, 0:1024], op=AL.subtract))
        v_sp = st.vt[1]

        for dc in range(8):
            bnk = st.pb[4 + (dc % 2)]

            @block.tensor
            def _(t, tt2=tt2, dc=dc, bnk=bnk, v_sp=v_sp):
                wt(t, st.vt)
                t.transpose(bnk[:, 0:64].bitcast(BF16), st.tsp[:, dc * 128:(dc + 1) * 128], st.idb[:, :])
                inc(st.tt, t.transpose(bnk[:, 64:128].bitcast(BF16),
                                       st.tsp[:, 1024 + dc * 128:1024 + (dc + 1) * 128], st.idb[:, :]))
            t_tp = st.tt[1]

            @block.vector
            def _(v, tt2=tt2, dc=dc, bnk=bnk, t_tp=t_tp):
                wt(v, st.tt, t_tp)
                v.tensor_copy(out=st.xn1Th[:, dc * 2048 + tt2 * 128: dc * 2048 + (tt2 + 1) * 128],
                              in_=bnk[:, 0:64].bitcast(BF16))
                inc(st.vt, v.tensor_copy(out=st.xn1Tl[:, dc * 2048 + tt2 * 128: dc * 2048 + (tt2 + 1) * 128],
                                         in_=bnk[:, 64:128].bitcast(BF16)))
    st.v_xn1 = st.vt[1]

    if st.debug:
        @block.scalar
        def _(s):
            wt(s, st.vt, st.v_xn1)
            s.dma_start(out=st.dbg["d_xn1h"], in_=st.xn1Th[:, :].bitcast(F32)).then_inc(st.sd[0], dinc(st.sd))

        @block.vector
        def _(v):
            wt(v, st.sd)
            inc(st.vt, v.engine_nop())



def emit_dpath(st):
    nc, block = st.nc, st.block
    dbrow = st.AR[0:1, 18432:19456]
    pT = st.AR[:, 19584:19600]
    bbv = st.AR[:, 19600:19728]
    wsp = st.khs_h
    sqTs = st.AR[:, 29312:29440].bitcast(BF16)
    khsh = st.Pb[:, :].bitcast(BF16)
    khsl = st.Vb[:, 0:1024].bitcast(BF16)
    st.aD2 = st.AR[0:64, 19728:19984]

    @block.sync
    def _(s):
        wt(s, st.vt)
        s.dma_start(out=st.t_tok[0:16, :], in_=st.sq_in[0, :, :]).then_inc(st.sy[0], dinc(st.sy))
        with nc.allow_non_contiguous_dma(reason="bias row"):
            s.dma_start(out=dbrow[:, :], in_=st.d_bq.unsqueeze(0)).then_inc(st.sy[0], dinc(st.sy))
    sq_ld = st.sy[1]

    @block.tensor
    def _(t):
        wt(t, st.sy, sq_ld)
        wt(t, st.vt)
        for dc in range(8):
            t.transpose(st.pb[4][:, dc * 16:(dc + 1) * 16], st.t_tok[0:16, dc * 128:(dc + 1) * 128],
                        st.idf[0:16, 0:16])
        inc(st.tt, t.matmul(st.pb[2][0:16, 0:512], st.ones1[0:1, 0:16], dbrow[:, 0:512],
                            start=True, stop=True))
        inc(st.tt, t.matmul(st.pb[3][0:16, 0:512], st.ones1[0:1, 0:16], dbrow[:, 512:1024],
                            start=True, stop=True))
    t_sq = st.tt[1]

    @block.vector
    def _(v):
        wt(v, st.tt, t_sq)
        inc(st.vt, v.tensor_copy(out=st.sqT[:, :], in_=st.pb[4][:, 0:128]))
        wt(v, st.vt)
        v.tensor_copy(out=sqTs[:, 0:128], in_=st.sqT[:, :])
        inc(st.vt, v.tensor_tensor(out=sqTs[:, 128:256], in0=st.sqT[:, :], in1=sqTs[:, 0:128],
                                   op=AL.subtract))
    v_sq = st.vt[1]

    for dc in range(8):
        @block.sync
        def _(s, dc=dc):
            wt(s, st.vt)
            s.dma_start(out=st.wnat[:, 0:1024], in_=st.d_Wq[dc * 128:(dc + 1) * 128, :]).then_inc(st.sy[0], dinc(st.sy))
        w_ld = st.sy[1]

        @block.vector
        def _(v, w_ld=w_ld):
            wt(v, st.sy, w_ld)
            wt(v, st.tt)
            v.tensor_copy(out=wsp[:, 0:1024], in_=st.wnat[:, 0:1024])
            inc(st.vt, v.tensor_tensor(out=wsp[:, 1024:2048], in0=st.wnat[:, 0:1024],
                                       in1=wsp[:, 0:1024], op=AL.subtract))
        v_w = st.vt[1]

        @block.tensor
        def _(t, dc=dc, v_w=v_w):
            wt(t, st.vt, v_w)
            last = None
            for oc in range(2):
                sh = sqTs[:, dc * 16:(dc + 1) * 16]
                sl = sqTs[:, 128 + dc * 16:128 + (dc + 1) * 16]
                t.matmul(st.pb[oc][0:16, 0:512], sh, wsp[:, oc * 512:(oc + 1) * 512],
                         start=(dc == 0), stop=False)
                t.matmul(st.pb[oc][0:16, 0:512], sh, wsp[:, 1024 + oc * 512:1024 + (oc + 1) * 512],
                         start=False, stop=False)
                last = t.matmul(st.pb[oc][0:16, 0:512], sl, wsp[:, oc * 512:(oc + 1) * 512],
                                start=False, stop=(dc == 7))
            inc(st.tt, last)
    t_qh = st.tt[1]

    @block.vector
    def _(v):
        wt(v, st.tt, t_qh)
        for oc in range(2):
            v.tensor_scalar(out=st.qhC[0:16, oc * 512:(oc + 1) * 512], in0=st.pb[oc][0:16, 0:512],
                            scalar1=0.125, scalar2=None, op0=AL.mult)
            v.scalar_tensor_tensor(out=st.qhC[0:16, oc * 512:(oc + 1) * 512],
                                   in0=st.pb[2 + oc][0:16, 0:512], scalar=0.125,
                                   in1=st.qhC[0:16, oc * 512:(oc + 1) * 512], op0=AL.mult, op1=AL.add)
        inc(st.vt, v.engine_nop())
    v_qh = st.vt[1]

    if st.debug:
        @block.scalar
        def _(s):
            wt(s, st.vt, v_qh)
            s.dma_start(out=st.dbg["d_dqh"], in_=st.qhC[0:16, :]).then_inc(st.sd[0], dinc(st.sd))

        @block.vector
        def _(v):
            wt(v, st.sd)
            inc(st.vt, v.engine_nop())

    @block.tensor
    def _(t):
        wt(t, st.vt, v_qh)
        for dc in range(8):
            inc(st.tt, t.transpose(st.pb[4][:, dc * 16:(dc + 1) * 16],
                                   st.qhC[0:16, dc * 128:(dc + 1) * 128], st.idf[0:16, 0:16]))
    t_qT = st.tt[1]

    @block.vector
    def _(v):
        wt(v, st.tt, t_qT)
        inc(st.vt, v.tensor_copy(out=st.sqT[:, :], in_=st.pb[4][:, 0:128]))
        wt(v, st.vt)
        v.tensor_copy(out=sqTs[:, 0:128], in_=st.sqT[:, :])
        inc(st.vt, v.tensor_tensor(out=sqTs[:, 128:256], in0=st.sqT[:, :], in1=sqTs[:, 0:128],
                                   op=AL.subtract))
    v_qT = st.vt[1]

    for p in range(8):
        @block.sync
        def _(s, p=p):
            wt(s, st.vt)
            with nc.allow_non_contiguous_dma(reason="column slice of d_Wk"):
                s.dma_start(out=st.wnat[:, 0:1024].rearrange("p (dc o) -> p dc o", o=128),
                            in_=st.d_Wk[:, p * 128:(p + 1) * 128].rearrange("(dc p) o -> p dc o", p=128)).then_inc(st.sy[0], dinc(st.sy))
        w_ld = st.sy[1]

        @block.vector
        def _(v, w_ld=w_ld):
            wt(v, st.sy, w_ld)
            wt(v, st.tt)
            v.tensor_copy(out=wsp[:, 0:1024], in_=st.wnat[:, 0:1024])
            inc(st.vt, v.tensor_tensor(out=wsp[:, 1024:2048], in0=st.wnat[:, 0:1024],
                                       in1=wsp[:, 0:1024], op=AL.subtract))
        v_dW = st.vt[1]

        for ncol in range(4):
            bnk = st.pb[ncol % 2]

            @block.tensor
            def _(t, ncol=ncol, bnk=bnk, v_dW=v_dW):
                wt(t, st.vt, v_dW)
                if ncol >= 2:
                    wt(t, st.ct)
                last = None
                for dc in range(8):
                    wh = wsp[:, dc * 128:(dc + 1) * 128]
                    wl = wsp[:, 1024 + dc * 128:1024 + (dc + 1) * 128]
                    xh = st.xn1Th[:, dc * 2048 + ncol * 512: dc * 2048 + (ncol + 1) * 512]
                    xl = st.xn1Tl[:, dc * 2048 + ncol * 512: dc * 2048 + (ncol + 1) * 512]
                    t.matmul(bnk[:, 0:512], wh, xh, start=(dc == 0), stop=False)
                    t.matmul(bnk[:, 0:512], wh, xl, start=False, stop=False)
                    last = t.matmul(bnk[:, 0:512], wl, xh, start=False, stop=(dc == 7))
                inc(st.tt, last)
            t_kh = st.tt[1]

            @block.scalar
            def _(s, p=p, ncol=ncol, bnk=bnk, t_kh=t_kh):
                wt(s, st.tt, t_kh)
                inc(st.ct, s.activation(out=st.khT[:, ncol * 512:(ncol + 1) * 512], in_=bnk[:, 0:512],
                                        func=AF.Identity, bias=st.dbT[:, p:p + 1]))
        s_kh = st.ct[1]

        @block.vector
        def _(v, s_kh=s_kh):
            wt(v, st.ct, s_kh)
            v.tensor_copy(out=khsh[:, :], in_=st.khT[:, :])
            inc(st.vt, v.tensor_tensor(out=khsl[:, :], in0=st.khT[:, :], in1=khsh[:, :],
                                       op=AL.subtract))
        v_khs = st.vt[1]

        if st.debug and p == 0:
            @block.scalar
            def _(s, v_khs=v_khs):
                wt(s, st.vt, v_khs)
                s.dma_start(out=st.dbg["d_dkh"], in_=st.khT[:, :]).then_inc(st.sd[0], dinc(st.sd))

            @block.vector
            def _(v):
                wt(v, st.sd)
                inc(st.vt, v.engine_nop())

        @block.sync
        def _(s, p=p, v_khs=v_khs):
            wt(s, st.vt, v_khs)
            with nc.allow_non_contiguous_dma(reason="column slice of d_Wv"):
                s.dma_start(out=st.wnat[:, 0:1024].rearrange("p (dc o) -> p dc o", o=128),
                            in_=st.d_Wv[:, p * 128:(p + 1) * 128].rearrange("(dc p) o -> p dc o", p=128)).then_inc(st.sy[0], dinc(st.sy))
        wv_ld = st.sy[1]

        @block.vector
        def _(v, wv_ld=wv_ld):
            wt(v, st.sy, wv_ld)
            wt(v, st.tt)
            v.tensor_copy(out=wsp[:, 0:1024], in_=st.wnat[:, 0:1024])
            inc(st.vt, v.tensor_tensor(out=wsp[:, 1024:2048], in0=st.wnat[:, 0:1024],
                                       in1=wsp[:, 0:1024], op=AL.subtract))
        v_wv = st.vt[1]

        @block.tensor
        def _(t, p=p, v_wv=v_wv):
            wt(t, st.vt, v_wv)
            inc(st.tt, t.transpose(st.pb[2][0:1, 0:128], st.dbT[:, 8 + p:9 + p], st.idf[:, :]))
        t_bv = st.tt[1]

        @block.vector
        def _(v, t_bv=t_bv):
            wt(v, st.tt, t_bv)
            inc(st.vt, v.tensor_copy(out=st.espr[:, 0:128], in_=st.pb[2][0:1, 0:128]))
        v_bvr = st.vt[1]

        @block.tensor
        def _(t, v_bvr=v_bvr):
            wt(t, st.vt, v_bvr)
            inc(st.tt, t.matmul(st.pb[2][:, 0:128], st.ones1[:, :], st.espr[:, 0:128], start=True, stop=True))
        t_bb = st.tt[1]

        @block.vector
        def _(v, t_bb=t_bb):
            wt(v, st.tt, t_bb)
            inc(st.vt, v.tensor_copy(out=bbv[:, :], in_=st.pb[2][:, 0:128]))
        v_bb = st.vt[1]

        for tt2 in range(16):
            bnk = st.pb[(tt2 % 2)]

            @block.tensor
            def _(t, tt2=tt2, bnk=bnk, v_bb=v_bb):
                wt(t, st.vt)
                last = None
                for dc in range(8):
                    sh = st.xn1Th[:, dc * 2048 + tt2 * 128: dc * 2048 + (tt2 + 1) * 128]
                    sl = st.xn1Tl[:, dc * 2048 + tt2 * 128: dc * 2048 + (tt2 + 1) * 128]
                    wh = wsp[:, dc * 128:(dc + 1) * 128]
                    wl = wsp[:, 1024 + dc * 128:1024 + (dc + 1) * 128]
                    t.matmul(bnk[:, 0:128], sh, wh, start=(dc == 0), stop=False)
                    t.matmul(bnk[:, 0:128], sl, wh, start=False, stop=False)
                    last = t.matmul(bnk[:, 0:128], sh, wl, start=False, stop=(dc == 7))
                inc(st.tt, last)
            t_vh = st.tt[1]

            @block.vector
            def _(v, tt2=tt2, bnk=bnk, t_vh=t_vh):
                wt(v, st.tt, t_vh)
                inc(st.vt, v.tensor_tensor(out=st.khT[:, tt2 * 128:(tt2 + 1) * 128], in0=bnk[:, 0:128],
                                           in1=bbv[:, :], op=AL.add))
        v_vh = st.vt[1]

        if st.debug and p == 0:
            @block.scalar
            def _(s, v_vh=v_vh):
                wt(s, st.vt, v_vh)
                s.dma_start(out=st.dbg["d_dvh"], in_=st.khT[:, :]).then_inc(st.sd[0], dinc(st.sd))

            @block.vector
            def _(v):
                wt(v, st.sd)
                inc(st.vt, v.engine_nop())

        for hh in range(2):
            h = 2 * p + hh
            b0 = hh * 64

            for ncol in range(4):
                bnk = st.pb[2 + (ncol % 2)]

                @block.tensor
                def _(t, h=h, b0=b0, ncol=ncol, bnk=bnk, v_vh=v_vh):
                    wt(t, st.vt)
                    qh_ = sqTs[b0:b0 + 64, (h // 2) * 16:(h // 2) * 16 + 16]
                    ql_ = sqTs[b0:b0 + 64, 128 + (h // 2) * 16:128 + (h // 2) * 16 + 16]
                    kh_ = khsh[b0:b0 + 64, ncol * 512:(ncol + 1) * 512]
                    kl_ = khsl[b0:b0 + 64, ncol * 512:(ncol + 1) * 512]
                    t.matmul(bnk[0:16, 0:512], qh_, kh_, start=True, stop=False)
                    t.matmul(bnk[0:16, 0:512], qh_, kl_, start=False, stop=False)
                    inc(st.tt, t.matmul(bnk[0:16, 0:512], ql_, kh_, start=False, stop=True))
                t_s = st.tt[1]

                @block.vector
                def _(v, ncol=ncol, bnk=bnk, t_s=t_s):
                    wt(v, st.tt, t_s)
                    inc(st.vt, v.tensor_copy(out=st.Sd[:, ncol * 512:(ncol + 1) * 512], in_=bnk[0:16, 0:512]))

            @block.vector
            def _(v):
                wt(v, st.vt)
                inc(st.vt, v.reduce_max(out=st.sm2[0:16, 8:9], in_=st.Sd[:, :], axis=AX.X, negate=True))
            v_m = st.vt[1]

            @block.scalar
            def _(s, v_m=v_m):
                wt(s, st.vt, v_m)
                s.activation(out=st.Sd[:, :], in_=st.Sd[:, :], func=AF.Exp,
                             bias=st.sm2[0:16, 8:9], accum_out=st.sm2[0:16, 9:10])
                s.activation(out=st.sm2[0:16, 10:11], in_=st.sm2[0:16, 9:10], func=AF.Ln)
                inc(st.ct, s.activation(out=st.sm2[0:16, 11:12], in_=st.sm2[0:16, 10:11],
                                        func=AF.Exp, scale=-1.0))
            s_r = st.ct[1]

            @block.vector
            def _(v, s_r=s_r):
                wt(v, st.ct, s_r)
                y = st.sm2[0:16, 11:12]
                for _i in range(2):
                    inc(st.vt, v.tensor_tensor(out=st.sm2[0:16, 12:13], in0=y, in1=st.sm2[0:16, 9:10], op=AL.mult))
                    wt(v, st.vt)
                    inc(st.vt, v.tensor_scalar(out=st.sm2[0:16, 12:13], in0=st.sm2[0:16, 12:13],
                                               scalar1=-1.0, scalar2=2.0, op0=AL.mult, op1=AL.add))
                    wt(v, st.vt)
                    inc(st.vt, v.tensor_tensor(out=y, in0=y, in1=st.sm2[0:16, 12:13], op=AL.mult))
                    wt(v, st.vt)
                inc(st.vt, v.tensor_scalar(out=st.Sd[:, :], in0=st.Sd[:, :], scalar1=y,
                                           scalar2=None, op0=AL.mult))
            v_P = st.vt[1]

            if st.debug and h == 0:
                @block.scalar
                def _(s, v_P=v_P):
                    wt(s, st.vt, v_P)
                    s.dma_start(out=st.dbg["d_dS"], in_=st.Sd[:, :]).then_inc(st.sd[0], dinc(st.sd))

                @block.vector
                def _(v):
                    wt(v, st.sd)
                    inc(st.vt, v.engine_nop())

            for jt in range(16):
                bnk = st.pb[4 + (jt % 2)]

                @block.tensor
                def _(t, jt=jt, bnk=bnk, v_P=v_P):
                    wt(t, st.vt)
                    inc(st.tt, t.transpose(bnk[:, 0:16], st.Sd[:, jt * 128:(jt + 1) * 128],
                                           st.idf[0:16, 0:16]))
                t_pt = st.tt[1]

                @block.vector
                def _(v, bnk=bnk, t_pt=t_pt):
                    wt(v, st.tt, t_pt)
                    inc(st.vt, v.tensor_copy(out=pT[:, :], in_=bnk[:, 0:16]))
                v_pt = st.vt[1]

                @block.tensor
                def _(t, b0=b0, jt=jt, v_pt=v_pt):
                    wt(t, st.vt, v_pt)
                    inc(st.tt, t.matmul(st.pb[6][0:64, 0:16], st.khT[:, jt * 128 + b0: jt * 128 + b0 + 64],
                                        pT[:, :], start=(jt == 0), stop=(jt == 15)))
            t_av = st.tt[1]

            @block.vector
            def _(v, h=h, t_av=t_av):
                wt(v, st.tt, t_av)
                inc(st.vt, v.tensor_copy(out=st.aD2[:, h * 16:(h + 1) * 16], in_=st.pb[6][0:64, 0:16]))

    if st.debug:
        @block.scalar
        def _(s):
            wt(s, st.vt)
            s.dma_start(out=st.dbg["d_daD2"], in_=st.aD2[:, :]).then_inc(st.sd[0], dinc(st.sd))

        @block.vector
        def _(v):
            wt(v, st.sd)
            inc(st.vt, v.engine_nop())

    for h in range(16):
        @block.sync
        def _(s, h=h):
            wt(s, st.vt)
            s.dma_start(out=st.wnat[0:64, 0:1024], in_=st.d_Wo[h * 64:(h + 1) * 64, :]).then_inc(st.sy[0], dinc(st.sy))
        w_ld = st.sy[1]

        @block.tensor
        def _(t, h=h, w_ld=w_ld):
            wt(t, st.sy, w_ld)
            wt(t, st.vt)
            last = None
            for oc in range(2):
                last = t.matmul(st.pb[oc][0:16, 0:512], st.aD2[:, h * 16:(h + 1) * 16],
                                st.wnat[0:64, oc * 512:(oc + 1) * 512],
                                start=(h == 0), stop=(h == 15))
            inc(st.tt, last)
        @block.vector
        def _(v):
            wt(v, st.tt)
            inc(st.vt, v.engine_nop())
    t_o = st.tt[1]

    @block.sync
    def _(s):
        wt(s, st.vt)
        with nc.allow_non_contiguous_dma(reason="bias row"):
            s.dma_start(out=dbrow[:, :], in_=st.d_bo.unsqueeze(0)).then_inc(st.sy[0], dinc(st.sy))
        s.dma_start(out=st.AR[0:16, 20480:21504], in_=st.sq_in[0, :, :]).then_inc(st.sy[0], dinc(st.sy))
    o_ld = st.sy[1]

    @block.tensor
    def _(t, o_ld=o_ld):
        wt(t, st.sy, o_ld)
        t.matmul(st.pb[2][0:16, 0:512], st.ones1[0:1, 0:16], dbrow[:, 0:512], start=True, stop=True)
        inc(st.tt, t.matmul(st.pb[3][0:16, 0:512], st.ones1[0:1, 0:16], dbrow[:, 512:1024],
                            start=True, stop=True))
    t_ob = st.tt[1]

    @block.vector
    def _(v, t_ob=t_ob):
        wt(v, st.tt, t_ob)
        sq2 = st.AR[0:16, 20480:21504]
        for oc in range(2):
            osl = st.o_tok[:, oc * 512:(oc + 1) * 512]
            v.tensor_copy(out=osl, in_=st.pb[oc][0:16, 0:512])
            v.tensor_tensor(out=osl, in0=osl, in1=st.pb[2 + oc][0:16, 0:512], op=AL.add)
            v.tensor_tensor(out=osl, in0=osl, in1=sq2[:, oc * 512:(oc + 1) * 512], op=AL.add)
        inc(st.vt, v.engine_nop())
    st.v_o = st.vt[1]

    if st.debug:
        @block.scalar
        def _(s):
            wt(s, st.vt, st.v_o)
            s.dma_start(out=st.dbg["d_o"], in_=st.o_tok[:, :]).then_inc(st.sd[0], dinc(st.sd))

        @block.vector
        def _(v):
            wt(v, st.sd)
            inc(st.vt, v.engine_nop())


def emit_p4(st):
    nc, block = st.nc, st.block
    AR = st.AR
    hh_s = AR[:, 27248:27312].bitcast(BF16)
    hl_s = AR[:, 27312:27376].bitcast(BF16)
    uTe, gTe = st.uTe, st.gTe
    ugh = AR[:, 26960:26976].bitcast(BF16)
    ugl = AR[:, 26976:26992].bitcast(BF16)
    espr = st.espr
    ew_f = AR[:, 27664:28688]
    ew_h = AR[:, 28688:29200].bitcast(BF16)
    ew_l = AR[:, 24592:25104].bitcast(BF16)
    oT = AR[:, 26768:26896]
    dd = AR[:, 29712:29728]
    ddi = AR[:, 29728:29744]
    ddw = AR[:, 29744:29760]

    @block.sync
    def _(s):
        wt(s, st.vt, st.v_o)
        s.dma_start(out=st.ebnat[:, 0:4096], in_=st.eb1[:, :]).then_inc(st.sy[0], dinc(st.sy))
        s.dma_start(out=st.AR[0:16, 16384:17408], in_=st.elng[:, :]).then_inc(st.sy[0], dinc(st.sy))
        s.dma_start(out=st.AR[0:16, 17408:18432], in_=st.elnb[:, :]).then_inc(st.sy[0], dinc(st.sy))
    p4_ld = st.sy[1]

    hrow = AR[0:16, 18432:19456]
    elg = st.AR[0:16, 16384:17408]
    elb = st.AR[0:16, 17408:18432]

    @block.scalar
    def _(s):
        wt(s, st.sy, p4_ld)
        inc(st.ct, s.activation(out=hrow[:, :], in_=st.o_tok[:, :], func=AF.Identity,
                                accum_out=st.sm2[0:16, 8:9]))
    s_su = st.ct[1]

    @block.vector
    def _(v):
        wt(v, st.ct, s_su)
        inc(st.vt, v.tensor_scalar_mul(st.sm2[0:16, 8:9], st.sm2[0:16, 8:9], -1.0 / 1024))
    v_mu = st.vt[1]

    @block.scalar
    def _(s):
        wt(s, st.vt, v_mu)
        inc(st.ct, s.activation(out=hrow[:, :], in_=st.o_tok[:, :], func=AF.Square,
                                bias=st.sm2[0:16, 8:9], accum_out=st.sm2[0:16, 9:10]))
    s_sq = st.ct[1]

    @block.vector
    def _(v):
        wt(v, st.ct, s_sq)
        inc(st.vt, v.tensor_scalar(out=st.sm2[0:16, 10:11], in0=st.sm2[0:16, 9:10],
                                   scalar1=1.0 / 1024, scalar2=LN_EPS, op0=AL.mult, op1=AL.add))
    v_v = st.vt[1]

    @block.scalar
    def _(s):
        wt(s, st.vt, v_v)
        s.activation(out=st.sm2[0:16, 11:12], in_=st.sm2[0:16, 10:11], func=AF.Ln)
        inc(st.ct, s.activation(out=st.sm2[0:16, 11:12], in_=st.sm2[0:16, 11:12], func=AF.Exp, scale=-0.5))
    s_sd = st.ct[1]

    @block.vector
    def _(v):
        wt(v, st.ct, s_sd)
        y = st.sm2[0:16, 11:12]
        for _i in range(2):
            inc(st.vt, v.tensor_tensor(out=st.sm2[0:16, 12:13], in0=y, in1=y, op=AL.mult))
            wt(v, st.vt)
            inc(st.vt, v.tensor_tensor(out=st.sm2[0:16, 12:13], in0=st.sm2[0:16, 12:13],
                                       in1=st.sm2[0:16, 10:11], op=AL.mult))
            wt(v, st.vt)
            inc(st.vt, v.tensor_scalar(out=st.sm2[0:16, 12:13], in0=st.sm2[0:16, 12:13],
                                       scalar1=-0.5, scalar2=1.5, op0=AL.mult, op1=AL.add))
            wt(v, st.vt)
            inc(st.vt, v.tensor_tensor(out=y, in0=y, in1=st.sm2[0:16, 12:13], op=AL.mult))
            wt(v, st.vt)
        v.tensor_scalar(out=hrow[:, :], in0=st.o_tok[:, :], scalar1=st.sm2[0:16, 8:9],
                        scalar2=y, op0=AL.add, op1=AL.mult)
        v.tensor_tensor(out=hrow[:, :], in0=hrow[:, :], in1=elg[:, :], op=AL.mult)
        inc(st.vt, v.tensor_tensor(out=hrow[:, :], in0=hrow[:, :], in1=elb[:, :], op=AL.add))
    v_h = st.vt[1]

    if st.debug:
        @block.scalar
        def _(s, v_h=v_h):
            wt(s, st.vt, v_h)
            s.dma_start(out=st.dbg["d_eh"], in_=hrow[:, :]).then_inc(st.sd[0], dinc(st.sd))

        @block.vector
        def _(v):
            wt(v, st.sd)
            inc(st.vt, v.engine_nop())

    @block.tensor
    def _(t):
        wt(t, st.vt, v_h)
        for dc in range(8):
            t.transpose(st.pb[6][:, dc * 16:(dc + 1) * 16], hrow[:, dc * 128:(dc + 1) * 128],
                        st.idf[0:16, 0:16])
        for dc in range(8):
            inc(st.tt, t.transpose(st.pb[7][:, dc * 16:(dc + 1) * 16],
                                   st.o_tok[:, dc * 128:(dc + 1) * 128], st.idf[0:16, 0:16]))
    t_h = st.tt[1]

    @block.vector
    def _(v):
        wt(v, st.tt, t_h)
        inc(st.vt, v.tensor_copy(out=st.hT[:, :], in_=st.pb[6][:, 0:128]))
        inc(st.vt, v.tensor_copy(out=oT[:, :], in_=st.pb[7][:, 0:128]))
        wt(v, st.vt)
        v.tensor_copy(out=hh_s[:, :], in_=st.hT[:, :])
        inc(st.vt, v.tensor_tensor(out=hl_s[:, :], in0=st.hT[:, :], in1=hh_s[:, :], op=AL.subtract))
    v_hs = st.vt[1]

    for fc2 in range(16):
        @block.tensor
        def _(t, fc2=fc2):
            wt(t, st.vt)
            t.transpose(st.pb[6][:, 0:16], st.ebnat[:, (2 * fc2) * 128:(2 * fc2 + 1) * 128],
                        st.idf[0:16, 0:16])
            inc(st.tt, t.transpose(st.pb[7][:, 0:16], st.ebnat[:, (2 * fc2 + 1) * 128:(2 * fc2 + 2) * 128],
                                   st.idf[0:16, 0:16]))
        t_p = st.tt[1]

        @block.vector
        def _(v, fc2=fc2, t_p=t_p):
            wt(v, st.tt, t_p)
            v.tensor_copy(out=st.b1T[:, (2 * fc2) * 16:(2 * fc2 + 1) * 16], in_=st.pb[6][:, 0:16])
            inc(st.vt, v.tensor_copy(out=st.b1T[:, (2 * fc2 + 1) * 16:(2 * fc2 + 2) * 16],
                                     in_=st.pb[7][:, 0:16]))
    v_b1 = st.vt[1]

    if st.debug:
        @block.scalar
        def _(s):
            wt(s, st.vt, v_b1)
            s.dma_start(out=st.dbg["d_eb1T"], in_=st.b1T[:, :]).then_inc(st.sd[0], dinc(st.sd))

        @block.vector
        def _(v):
            wt(v, st.sd)
            inc(st.vt, v.engine_nop())

    @block.sync
    def _(s):
        wt(s, st.vt, v_b1)
        s.dma_start(out=st.ebnat[:, 0:4096], in_=st.ebg[:, :]).then_inc(st.sy[0], dinc(st.sy))
    bg_ld = st.sy[1]

    for fc2 in range(16):
        @block.tensor
        def _(t, fc2=fc2, bg_ld=bg_ld):
            wt(t, st.sy, bg_ld)
            wt(t, st.vt)
            t.transpose(st.pb[6][:, 0:16], st.ebnat[:, (2 * fc2) * 128:(2 * fc2 + 1) * 128],
                        st.idf[0:16, 0:16])
            inc(st.tt, t.transpose(st.pb[7][:, 0:16], st.ebnat[:, (2 * fc2 + 1) * 128:(2 * fc2 + 2) * 128],
                                   st.idf[0:16, 0:16]))
        t_p = st.tt[1]

        @block.vector
        def _(v, fc2=fc2, t_p=t_p):
            wt(v, st.tt, t_p)
            v.tensor_copy(out=st.bgT[:, (2 * fc2) * 16:(2 * fc2 + 1) * 16], in_=st.pb[6][:, 0:16])
            inc(st.vt, v.tensor_copy(out=st.bgT[:, (2 * fc2 + 1) * 16:(2 * fc2 + 2) * 16],
                                     in_=st.pb[7][:, 0:16]))
    v_bg = st.vt[1]

    for h in range(16):
        emit_attn_head(st, h)
        emit_expert(st, h, hh_s, hl_s, uTe, gTe, ugh, ugl, espr, ew_f, ew_h, ew_l, oT)

    @block.sync
    def _(s):
        wt(s, st.vt)
        s.dma_start(out=st.eb2n[:, :], in_=st.eb2[:, :]).then_inc(st.sy[0], dinc(st.sy))
    b2_ld = st.sy[1]

    @block.tensor
    def _(t):
        wt(t, st.sy, b2_ld)
        wt(t, st.vt)
        for dc in range(8):
            inc(st.tt, t.transpose(st.pb[6 if dc % 2 == 0 else 7][:, 0:16],
                                   st.eb2n[:, dc * 128:(dc + 1) * 128], st.idf[0:16, 0:16]))
    t_b2 = st.tt[1]

    @block.vector
    def _(v):
        base = t_b2 - 8
        for dc in range(8):
            wt(v, st.tt, base + dc + 1)
            inc(st.vt, v.scalar_tensor_tensor(out=oT[:, dc * 16:(dc + 1) * 16],
                                              in0=st.pb[6 if dc % 2 == 0 else 7][:, 0:16],
                                              scalar=1.0, in1=oT[:, dc * 16:(dc + 1) * 16],
                                              op0=AL.mult, op1=AL.add))
    st.v_oT = st.vt[1]
    st.oT = oT

    if st.debug:
        @block.scalar
        def _(s):
            wt(s, st.vt, st.v_oT)
            s.dma_start(out=st.dbg["d_o2"], in_=oT[:, :]).then_inc(st.sd[0], dinc(st.sd))
            s.dma_start(out=st.dbg["d_aX"], in_=st.X[:, :]).then_inc(st.sd[0], dinc(st.sd))

        @block.vector
        def _(v):
            wt(v, st.sd)
            inc(st.vt, v.engine_nop())


DBGH = 1


def emit_attn_head(st, h):
    nc, block = st.nc, st.block
    AR = st.AR
    dc_h = h // 2
    b0 = (h % 2) * 64
    Qstage = st.Pb[0:64, :]
    dd = AR[:, 29712:29728]
    negmrow = AR[0:1, 27664:29712]

    for (srcT, dsth, dstl, scol) in [
        ("Q", st.Q65h, st.Q65l, st.rotT), ("K", st.K65h, st.K65l, st.entT)]:
        for nc2 in range(2):
            for nq in range(2):
                ncol = nc2 * 2 + nq
                bnkh, bnkl = st.pb[2 * nq], st.pb[2 * nq + 1]

                @block.tensor
                def _(t, ncol=ncol, bnkh=bnkh, bnkl=bnkl):
                    wt(t, st.vt)
                    mv = st.idb[:, 0:64] if b0 == 0 else st.idb[:, 64:128]
                    xh = st.xn1Th[:, dc_h * 2048 + ncol * 512: dc_h * 2048 + (ncol + 1) * 512]
                    xl = st.xn1Tl[:, dc_h * 2048 + ncol * 512: dc_h * 2048 + (ncol + 1) * 512]
                    t.matmul(bnkh[0:64, 0:512], mv, xh, start=True, stop=True)
                    inc(st.tt, t.matmul(bnkl[0:64, 0:512], mv, xl, start=True, stop=True))
            t_mv = st.tt[1]

            @block.vector
            def _(v, srcT=srcT, dsth=dsth, dstl=dstl, scol=scol, nc2=nc2, t_mv=t_mv):
                wt(v, st.tt, t_mv)
                for nq in range(2):
                    ncol = nc2 * 2 + nq
                    qsl = Qstage[:, nq * 512:(nq + 1) * 512]
                    v.tensor_copy(out=qsl, in_=st.pb[2 * nq][0:64, 0:512])
                    v.scalar_tensor_tensor(out=qsl, in0=st.pb[2 * nq + 1][0:64, 0:512], scalar=1.0,
                                           in1=qsl, op0=AL.mult, op1=AL.add)
                    v.tensor_scalar(out=qsl, in0=qsl, scalar1=scol[:, h:h + 1],
                                    scalar2=None, op0=AL.mult)
                for nq in range(2):
                    ncol = nc2 * 2 + nq
                    qsl = Qstage[:, nq * 512:(nq + 1) * 512]
                    v.tensor_copy(out=dsth[0:64, ncol * 512:(ncol + 1) * 512], in_=qsl)
                for nq in range(2):
                    ncol = nc2 * 2 + nq
                    qsl = Qstage[:, nq * 512:(nq + 1) * 512]
                    v.tensor_tensor(out=dstl[0:64, ncol * 512:(ncol + 1) * 512], in0=qsl,
                                    in1=dsth[0:64, ncol * 512:(ncol + 1) * 512], op=AL.subtract)
                inc(st.vt, v.engine_nop())
    v_qk = st.vt[1]

    @block.vector
    def _(v):
        inc(st.vt, v.memset(st.Vb[:, :].rearrange("p (a b) -> p a b", b=65)[:, :, 64:65], 1.0))
    v_vones = st.vt[1]

    identV = st.idb[0:64, 0:64] if b0 == 0 else st.idb[64:128, 64:128]
    for jt2 in range(8):
        @block.tensor
        def _(t, jt2=jt2, identV=identV):
            wt(t, st.vt)
            for k in range(2):
                jt = jt2 * 2 + k
                t.transpose(st.pb[4 + 2 * k][:, 0:32].bitcast(BF16),
                            st.xn1Th[b0:b0 + 64, dc_h * 2048 + jt * 128: dc_h * 2048 + (jt + 1) * 128],
                            identV)
                inc(st.tt, t.transpose(st.pb[5 + 2 * k][:, 0:32].bitcast(BF16),
                                       st.xn1Tl[b0:b0 + 64, dc_h * 2048 + jt * 128: dc_h * 2048 + (jt + 1) * 128],
                                       identV))
        t_v = st.tt[1]

        @block.vector
        def _(v, jt2=jt2, t_v=t_v):
            wt(v, st.tt, t_v)
            for k in range(2):
                jt = jt2 * 2 + k
                vsl = st.Vb[:, jt * 65:jt * 65 + 64]
                v.tensor_copy(out=vsl, in_=st.pb[4 + 2 * k][:, 0:32].bitcast(BF16))
            for k in range(2):
                jt = jt2 * 2 + k
                vsl = st.Vb[:, jt * 65:jt * 65 + 64]
                v.scalar_tensor_tensor(out=vsl, in0=st.pb[5 + 2 * k][:, 0:32].bitcast(BF16),
                                       scalar=1.0, in1=vsl, op0=AL.mult, op1=AL.add)
            inc(st.vt, v.engine_nop())
    v_V = st.vt[1]

    for it in range(16):
        @block.tensor
        def _(t, it=it):
            wt(t, st.vt)
            for jc in range(4):
                inc(st.tt, t.matmul(st.pb[jc][:, 0:512], st.Q65h[0:64, it * 128:(it + 1) * 128],
                                    st.K65h[0:64, jc * 512:(jc + 1) * 512], start=True, stop=True))
        t_st = st.tt[1]

        @block.vector
        def _(v, it=it, t_st=t_st):
            wt(v, st.tt, t_st)
            for jc in range(4):
                v.tensor_reduce(out=st.sm2[:, 8 + jc:9 + jc], in_=st.pb[jc][:, 0:512],
                                axis=AX.X, op=AL.max)
            inc(st.vt, v.engine_nop())
            wt(v, st.vt)
            inc(st.vt, v.tensor_reduce(out=st.sm2[:, 12:13], in_=st.sm2[:, 8:12], axis=AX.X, op=AL.max))
            wt(v, st.vt)
            inc(st.vt, v.tensor_scalar_mul(st.sm2[:, 13:14], st.sm2[:, 12:13], -1.0))
        v_m = st.vt[1]

        @block.tensor
        def _(t, it=it, v_m=v_m):
            wt(t, st.vt, v_m)
            inc(st.tt, t.transpose(st.pb[4][0:1, 0:128], st.sm2[:, 13:14], st.idf[:, :]))
        t_nm = st.tt[1]

        @block.vector
        def _(v, it=it, t_nm=t_nm, negmrow=negmrow):
            wt(v, st.tt, t_nm)
            inc(st.vt, v.tensor_copy(out=negmrow[:, it * 128:(it + 1) * 128],
                                     in_=st.pb[4][0:1, 0:128]))
    v_stats = st.vt[1]

    if st.debug and h == DBGH:
        @block.scalar
        def _(s):
            wt(s, st.vt, v_stats)
            s.dma_start(out=st.dbg["d_negm"], in_=negmrow[:, :]).then_inc(st.sd[0], dinc(st.sd))
            s.dma_start(out=st.dbg["d_Qh"], in_=st.Q65h[:, :].bitcast(F32)).then_inc(st.sd[0], dinc(st.sd))
            s.dma_start(out=st.dbg["d_Ql"], in_=st.Q65l[:, :].bitcast(F32)).then_inc(st.sd[0], dinc(st.sd))
            s.dma_start(out=st.dbg["d_Kh"], in_=st.K65h[:, :].bitcast(F32)).then_inc(st.sd[0], dinc(st.sd))
            s.dma_start(out=st.dbg["d_Kl"], in_=st.K65l[:, :].bitcast(F32)).then_inc(st.sd[0], dinc(st.sd))

        @block.vector
        def _(v):
            wt(v, st.sd)
            inc(st.vt, v.engine_nop())

    for ic in range(2):
        for nc2 in range(2):
            icol = ic * 2 + nc2
            def st_mm(jt, icol=icol):
                sb = st.pb[jt % 2]

                @block.tensor
                def _(t, jt=jt, sb=sb, icol=icol):
                    wt(t, st.vt)
                    Ks = st.K65h[0:64, jt * 128:(jt + 1) * 128]
                    Kl = st.K65l[0:64, jt * 128:(jt + 1) * 128]
                    Qs = st.Q65h[0:64, icol * 512:(icol + 1) * 512]
                    Ql = st.Q65l[0:64, icol * 512:(icol + 1) * 512]
                    t.matmul(sb[:, 0:512], Ks, Qs, start=True, stop=False)
                    t.matmul(sb[:, 0:512], Ks, Ql, start=False, stop=False)
                    t.matmul(sb[:, 0:512], Kl, Qs, start=False, stop=False)
                    inc(st.tt, t.matmul(sb[:, 0:512], st.ones1[:, :],
                                        negmrow[:, icol * 512:(icol + 1) * 512],
                                        start=False, stop=True))
                return st.tt[1]

            def exp_op(jt, t_s):
                sb = st.pb[jt % 2]

                @block.scalar
                def _(s, jt=jt, sb=sb, t_s=t_s):
                    wt(s, st.tt, t_s)
                    inc(st.ct, s.activation(out=st.Pb[:, (jt % 2) * 512:(jt % 2) * 512 + 512],
                                            in_=sb[:, 0:512], func=AF.Exp))
                return st.ct[1]

            def av_mm(jt, s_e):
                @block.tensor
                def _(t, jt=jt, s_e=s_e):
                    wt(t, st.ct, s_e)
                    inc(st.tt, t.matmul(st.pb[2][0:65, 0:512], st.Vb[:, jt * 65:(jt + 1) * 65],
                                        st.Pb[:, (jt % 2) * 512:(jt % 2) * 512 + 512],
                                        start=(jt == 0), stop=(jt == 15)))

            t_s = st_mm(0)
            if st.debug and h == DBGH and icol == 0:
                @block.vector
                def _(v, t_s=t_s):
                    wt(v, st.tt, t_s)
                    inc(st.vt, v.tensor_copy(out=st.aT[:, 0:512], in_=st.pb[0][:, 0:512]))

                @block.scalar
                def _(s):
                    wt(s, st.vt)
                    s.dma_start(out=st.dbg["d_S0"], in_=st.aT[:, 0:512]).then_inc(st.sd[0], dinc(st.sd))

                @block.vector
                def _(v):
                    wt(v, st.sd)
                    inc(st.vt, v.engine_nop())
            for jt in range(16):
                s_e = exp_op(jt, t_s)
                if st.debug and h == DBGH and icol == 0 and jt == 0:
                    @block.scalar
                    def _(s, s_e=s_e):
                        wt(s, st.ct, s_e)
                        s.dma_start(out=st.dbg["d_P0"], in_=st.Pb[:, 0:512]).then_inc(st.sd[0], dinc(st.sd))

                    @block.vector
                    def _(v):
                        wt(v, st.sd)
                        inc(st.vt, v.engine_nop())
                if jt < 15:
                    t_s = st_mm(jt + 1)
                av_mm(jt, s_e)
            t_av = st.tt[1]

            @block.vector
            def _(v, icol=icol, nc2=nc2, t_av=t_av):
                wt(v, st.tt, t_av)
                inc(st.vt, v.tensor_copy(out=st.aO[:, nc2 * 512:(nc2 + 1) * 512], in_=st.pb[2][0:65, 0:512]))
        v_ao = st.vt[1]

        if st.debug and h == DBGH and ic == 0:
            @block.scalar
            def _(s, v_ao=v_ao):
                wt(s, st.vt, v_ao)
                s.dma_start(out=st.dbg["d_aO0"], in_=st.aO[:, :]).then_inc(st.sd[0], dinc(st.sd))

            @block.vector
            def _(v):
                wt(v, st.sd)
                inc(st.vt, v.engine_nop())

        for itl2 in range(4):
            @block.tensor
            def _(t, ic=ic, itl2=itl2, v_ao=v_ao):
                wt(t, st.vt)
                for k in range(2):
                    itl = itl2 * 2 + k
                    inc(st.tt, t.transpose(st.pb[3 + k][:, 0:65], st.aO[:, itl * 128:(itl + 1) * 128],
                                           st.idf[0:65, 0:65]))
            t_at = st.tt[1]

            @block.vector
            def _(v, ic=ic, itl2=itl2, t_at=t_at):
                wt(v, st.tt, t_at)
                for k in range(2):
                    itl = itl2 * 2 + k
                    it = ic * 8 + itl
                    v.tensor_copy(out=st.aT[:, it * 64:(it + 1) * 64], in_=st.pb[3 + k][:, 0:64])
                    inc(st.vt, v.tensor_copy(out=dd[:, it:it + 1], in_=st.pb[3 + k][:, 64:65]))
    v_aT = st.vt[1]

    if st.debug and h == DBGH:
        @block.scalar
        def _(s):
            wt(s, st.vt, v_aT)
            s.dma_start(out=st.dbg["d_dd"], in_=dd[:, :]).then_inc(st.sd[0], dinc(st.sd))

        @block.vector
        def _(v):
            wt(v, st.sd)
            inc(st.vt, v.engine_nop())

    @block.scalar
    def _(s, v_aT=v_aT):
        wt(s, st.vt, v_aT)
        s.activation(out=AR[:, 29728:29744], in_=dd[:, :], func=AF.Ln)
        inc(st.ct, s.activation(out=AR[:, 29728:29744], in_=AR[:, 29728:29744], func=AF.Exp, scale=-1.0))
    s_i = st.ct[1]

    @block.vector
    def _(v, s_i=s_i):
        wt(v, st.ct, s_i)
        y = AR[:, 29728:29744]
        w2 = AR[:, 29744:29760]
        for _i in range(2):
            inc(st.vt, v.tensor_tensor(out=w2, in0=y, in1=dd[:, :], op=AL.mult))
            wt(v, st.vt)
            inc(st.vt, v.tensor_scalar(out=w2, in0=w2, scalar1=-1.0, scalar2=2.0, op0=AL.mult, op1=AL.add))
            wt(v, st.vt)
            inc(st.vt, v.tensor_tensor(out=y, in0=y, in1=w2, op=AL.mult))
            wt(v, st.vt)
        for it in range(16):
            v.scalar_tensor_tensor(out=st.X[:, it * D + h * 64: it * D + h * 64 + 64],
                                   in0=st.aT[:, it * 64:(it + 1) * 64], scalar=y[:, it:it + 1],
                                   in1=st.X[:, it * D + h * 64: it * D + h * 64 + 64],
                                   op0=AL.mult, op1=AL.add)
        inc(st.vt, v.engine_nop())


def emit_expert(st, e, hh_s, hl_s, uTe, gTe, ugh, ugl, espr, ew_f, ew_h, ew_l, oT):
    nc, block = st.nc, st.block
    AR = st.AR

    for mi, (W, bT, dst) in enumerate([(st.eW1, st.b1T, uTe), (st.eWg, st.bgT, gTe)]):
        for fc in range(32):
            @block.sync
            def _(s, e=e, W=W, fc=fc):
                wt(s, st.vt)
                with nc.allow_non_contiguous_dma(reason="expert W1 column chunk"):
                    s.dma_start(out=ew_f[:, :].rearrange("p (dc o) -> p dc o", o=128),
                                in_=W[e, :, fc * 128:(fc + 1) * 128].rearrange("(dc p) o -> p dc o", p=128)).then_inc(st.sy[0], dinc(st.sy))
            wf_ld = st.sy[1]

            @block.gpsimd
            def _(g, e=e, W=W, fc=fc):
                g.wait_ge(st.vt[0], st.vt[1])
                g.wait_ge(st.tt[0], st.tt[1])
                with nc.allow_non_contiguous_dma(reason="expert W1 cast chunk"):
                    g.dma_start(out=ew_h[:, :].rearrange("p (dc o) -> p dc o", o=128),
                                in_=W[e, :, fc * 128:(fc + 1) * 128].rearrange("(dc p) o -> p dc o", p=128)).then_inc(st.gd[0], dinc(st.gd))
            wh_ld = st.gd[1]

            @block.vector
            def _(v, wf_ld=wf_ld, wh_ld=wh_ld):
                wt(v, st.sy, wf_ld)
                wt(v, st.gd, wh_ld)
                wt(v, st.tt)
                inc(st.vt, v.tensor_tensor(out=ew_l[:, :], in0=ew_f[:, :], in1=ew_h[:, :], op=AL.subtract))
            v_wl = st.vt[1]

            @block.tensor
            def _(t, e=e, fc=fc, v_wl=v_wl):
                wt(t, st.vt, v_wl)
                last = None
                for dc in range(8):
                    hh_c = hh_s[:, dc * 16 + e: dc * 16 + e + 1]
                    hl_c = hl_s[:, dc * 16 + e: dc * 16 + e + 1]
                    wh_ = ew_h[:, dc * 128:(dc + 1) * 128]
                    wl_ = ew_l[:, dc * 128:(dc + 1) * 128]
                    t.matmul(st.pb[6][0:1, 0:128], hh_c, wh_, start=(dc == 0), stop=False)
                    t.matmul(st.pb[6][0:1, 0:128], hl_c, wh_, start=False, stop=False)
                    last = t.matmul(st.pb[6][0:1, 0:128], hh_c, wl_, start=False, stop=(dc == 7))
                inc(st.tt, last)
            t_u = st.tt[1]

            @block.vector
            def _(v, fc=fc, t_u=t_u):
                wt(v, st.tt, t_u)
                inc(st.vt, v.tensor_copy(out=espr[:, 0:128], in_=st.pb[6][0:1, 0:128]))
            v_e = st.vt[1]

            @block.tensor
            def _(t, fc=fc, v_e=v_e):
                wt(t, st.vt, v_e)
                inc(st.tt, t.transpose(st.pb[7][:, 0:1], espr[:, 0:128], st.idf[0:1, 0:1]))
            t_tr = st.tt[1]

            @block.vector
            def _(v, dst=dst, fc=fc, t_tr=t_tr):
                wt(v, st.tt, t_tr)
                inc(st.vt, v.tensor_copy(out=dst[:, fc:fc + 1], in_=st.pb[7][:, 0:1]))

        if st.debug and e == 0 and mi == 0:
            @block.scalar
            def _(s):
                wt(s, st.vt)
                s.dma_start(out=st.dbg["d_eur"], in_=uTe[:, :]).then_inc(st.sd[0], dinc(st.sd))

            @block.vector
            def _(v):
                wt(v, st.sd)
                inc(st.vt, v.engine_nop())

    @block.vector
    def _(v, e=e):
        wt(v, st.vt)
        v.tensor_tensor(out=uTe[:, :], in0=uTe[:, :], in1=st.b1T[:, e::16], op=AL.add)
        inc(st.vt, v.tensor_tensor(out=gTe[:, :], in0=gTe[:, :], in1=st.bgT[:, e::16], op=AL.add))
    v_b = st.vt[1]

    @block.scalar
    def _(s, v_b=v_b):
        wt(s, st.vt, v_b)
        s.activation(out=uTe[:, :], in_=uTe[:, :], func=AF.Silu)
        inc(st.ct, s.activation(out=gTe[:, :], in_=gTe[:, :], func=AF.Silu))
    s_si = st.ct[1]

    if st.debug and e == 0:
        @block.scalar
        def _(s, s_si=s_si):
            wt(s, st.ct, s_si)
            s.dma_start(out=st.dbg["d_eu"], in_=uTe[:, :]).then_inc(st.sd[0], dinc(st.sd))
            s.dma_start(out=st.dbg["d_eg"], in_=gTe[:, :]).then_inc(st.sd[0], dinc(st.sd))

        @block.vector
        def _(v):
            wt(v, st.sd)
            inc(st.vt, v.engine_nop())

    @block.vector
    def _(v, s_si=s_si):
        wt(v, st.ct, s_si)
        inc(st.vt, v.tensor_tensor(out=uTe[:, :], in0=uTe[:, :], in1=gTe[:, :], op=AL.mult))
        wt(v, st.vt)
        inc(st.vt, v.tensor_copy(out=ugh[:, :], in_=uTe[:, :]))
        wt(v, st.vt)
        inc(st.vt, v.tensor_tensor(out=ugl[:, :], in0=uTe[:, :], in1=ugh[:, :], op=AL.subtract))
    v_ug = st.vt[1]

    for fc in range(32):
        @block.sync
        def _(s, e=e, fc=fc):
            wt(s, st.vt)
            s.dma_start(out=ew_f[:, :], in_=st.eW2[e, fc * 128:(fc + 1) * 128, :]).then_inc(st.sy[0], dinc(st.sy))
        wf_ld = st.sy[1]

        @block.gpsimd
        def _(g, e=e, fc=fc):
            g.wait_ge(st.vt[0], st.vt[1])
            g.wait_ge(st.tt[0], st.tt[1])
            g.dma_start(out=ew_h[:, :], in_=st.eW2[e, fc * 128:(fc + 1) * 128, :]).then_inc(st.gd[0], dinc(st.gd))
        wh_ld = st.gd[1]

        @block.vector
        def _(v, wf_ld=wf_ld, wh_ld=wh_ld):
            wt(v, st.sy, wf_ld)
            wt(v, st.gd, wh_ld)
            wt(v, st.tt)
            inc(st.vt, v.tensor_tensor(out=ew_l[:, :], in0=ew_f[:, :], in1=ew_h[:, :], op=AL.subtract))
        v_wl = st.vt[1]

        @block.tensor
        def _(t, e=e, fc=fc, v_wl=v_wl):
            wt(t, st.vt, v_wl)
            last = None
            ug_h = ugh[:, fc:fc + 1]
            ug_l = ugl[:, fc:fc + 1]
            for oc in range(2):
                bnk = st.pb[6] if oc == 0 else st.pb[7]
                wh_ = ew_h[:, oc * 512:(oc + 1) * 512]
                wl_ = ew_l[:, oc * 512:(oc + 1) * 512]
                t.matmul(bnk[0:1, 0:512], ug_h, wh_, start=(fc == 0), stop=False)
                t.matmul(bnk[0:1, 0:512], ug_l, wh_, start=False, stop=False)
                last = t.matmul(bnk[0:1, 0:512], ug_h, wl_, start=False, stop=(fc == 31))
            inc(st.tt, last)
    t_y = st.tt[1]

    @block.vector
    def _(v, t_y=t_y):
        wt(v, st.tt, t_y)
        inc(st.vt, v.tensor_copy(out=AR[0:1, 19456:19968], in_=st.pb[6][0:1, 0:512]))
        inc(st.vt, v.tensor_copy(out=AR[0:1, 19968:20480], in_=st.pb[7][0:1, 0:512]))
    v_y = st.vt[1]

    if st.debug and e == 0:
        @block.scalar
        def _(s, v_y=v_y):
            wt(s, st.vt, v_y)
            s.dma_start(out=st.dbg["d_ey"], in_=AR[0:1, 19456:20480]).then_inc(st.sd[0], dinc(st.sd))

        @block.vector
        def _(v):
            wt(v, st.sd)
            inc(st.vt, v.engine_nop())

    yrow = AR[0:1, 19456:20480]
    for dc in range(8):
        @block.tensor
        def _(t, dc=dc, v_y=v_y):
            wt(t, st.vt, v_y)
            wt(t, st.vt)
            inc(st.tt, t.transpose(st.pb[6 if dc % 2 == 0 else 7][:, 0:1],
                                   yrow[:, dc * 128:(dc + 1) * 128], st.idf[0:1, 0:1]))
        t_tr = st.tt[1]

        @block.vector
        def _(v, e=e, dc=dc, t_tr=t_tr):
            wt(v, st.tt, t_tr)
            inc(st.vt, v.scalar_tensor_tensor(out=oT[:, dc * 16 + e:dc * 16 + e + 1],
                                              in0=st.pb[6 if dc % 2 == 0 else 7][:, 0:1],
                                              scalar=1.0, in1=oT[:, dc * 16 + e:dc * 16 + e + 1],
                                              op0=AL.mult, op1=AL.add))


def emit_cpath(st):
    nc, block = st.nc, st.block
    AR = st.AR
    wnat = st.wnat
    kho = AR[0:16, 23568:24592]
    vho = AR[0:16, 24592:25616]
    khoT2 = AR[0:64, 29184:29440]
    bqfm2 = AR[0:64, 29440:29456]
    sbrow = st.espr
    WqT2 = AR[0:64, 16384:18432]
    Mf = AR[:, 18432:18688]
    Mh = AR[:, 20480:21504].bitcast(BF16)
    Ml = AR[:, 21504:22528].bitcast(BF16)
    aCt4 = AR[:, 16384:20480]
    Pc = st.Pc
    negm = AR[:, 29712:29728]
    den = AR[:, 29728:29744]
    inv = AR[:, 29744:29760]
    pTs = AR[0:16, 27248:27376]
    cboB = AR[:, 22544:23568]
    brow = AR[0:1, 18432:19456]

    for (W, bvec, dst) in [(st.c_Wk, st.c_bk, kho), (st.c_Wv, st.c_bv, vho)]:
        @block.sync
        def _(s, W=W, bvec=bvec):
            wt(s, st.vt)
            with nc.allow_non_contiguous_dma(reason="bias row"):
                s.dma_start(out=brow[:, :], in_=bvec.unsqueeze(0)).then_inc(st.sy[0], dinc(st.sy))
        b_ld = st.sy[1]

        @block.tensor
        def _(t, b_ld=b_ld):
            wt(t, st.sy, b_ld)
            wt(t, st.vt)
            t.matmul(st.pb[2][0:16, 0:512], st.ones1[0:1, 0:16], brow[:, 0:512], start=True, stop=True)
            inc(st.tt, t.matmul(st.pb[3][0:16, 0:512], st.ones1[0:1, 0:16], brow[:, 512:1024],
                                start=True, stop=True))

        for dc in range(8):
            sl = (dc % 2) * 1024

            @block.sync
            def _(s, W=W, dc=dc, sl=sl):
                wt(s, st.vt)
                s.dma_start(out=wnat[:, sl:sl + 1024], in_=W[dc * 128:(dc + 1) * 128, :]).then_inc(st.sy[0], dinc(st.sy))
            w_ld = st.sy[1]

            @block.tensor
            def _(t, dc=dc, sl=sl, w_ld=w_ld):
                wt(t, st.sy, w_ld)
                last = None
                for oc in range(2):
                    last = t.matmul(st.pb[oc][0:16, 0:512], st.oT[:, dc * 16:(dc + 1) * 16],
                                    wnat[:, sl + oc * 512:sl + (oc + 1) * 512],
                                    start=(dc == 0), stop=(dc == 7))
                inc(st.tt, last)

            @block.vector
            def _(v):
                wt(v, st.tt)
                inc(st.vt, v.engine_nop())
        t_p = st.tt[1]

        @block.vector
        def _(v, dst=dst, t_p=t_p):
            wt(v, st.tt, t_p)
            for oc in range(2):
                v.tensor_copy(out=dst[:, oc * 512:(oc + 1) * 512], in_=st.pb[oc][0:16, 0:512])
            for oc in range(2):
                v.scalar_tensor_tensor(out=dst[:, oc * 512:(oc + 1) * 512],
                                       in0=st.pb[2 + oc][0:16, 0:512], scalar=1.0,
                                       in1=dst[:, oc * 512:(oc + 1) * 512], op0=AL.mult, op1=AL.add)
            inc(st.vt, v.engine_nop())
    v_kv = st.vt[1]

    if KCA < 2:
        st.v_cp = st.vt[1]
        return
    @block.sync
    def _(s):
        wt(s, st.vt, v_kv)
        with nc.allow_non_contiguous_dma(reason="bq per-head fm"):
            s.dma_start(out=bqfm2[:, :], in_=st.c_bq.rearrange("(h o) -> o h", o=64),
                        allow_slow_non_contiguous=True).then_inc(st.sy[0], dinc(st.sy))
    bq_ld = st.sy[1]

    @block.tensor
    def _(t, bq_ld=bq_ld):
        wt(t, st.sy, bq_ld)
        wt(t, st.vt)
        for h in range(15):
            t.transpose(st.pb[4][0:64, h * 16:(h + 1) * 16], kho[:, h * 64:(h + 1) * 64],
                        st.idf[0:16, 0:16])
        inc(st.tt, t.transpose(st.pb[4][0:64, 240:256], kho[:, 960:1024], st.idf[0:16, 0:16]))
    t_kT = st.tt[1]

    @block.vector
    def _(v, t_kT=t_kT):
        wt(v, st.tt, t_kT)
        inc(st.vt, v.tensor_copy(out=khoT2[:, :], in_=st.pb[4][0:64, 0:256]))
    v_kT = st.vt[1]

    if KCB < 2:
        st.v_cp = st.vt[1]
        return
    @block.tensor
    def _(t, v_kT=v_kT):
        wt(t, st.vt, v_kT)
        last = None
        for h in range(16):
            last = t.matmul(st.pb[5][0:1, h * 16:(h + 1) * 16],
                            bqfm2[:, h:h + 1],
                            khoT2[:, h * 16:(h + 1) * 16],
                            start=True, stop=True)
        inc(st.tt, last)
    t_sb = st.tt[1]

    @block.vector
    def _(v, t_sb=t_sb):
        wt(v, st.tt, t_sb)
        inc(st.vt, v.tensor_scalar_mul(sbrow[:, 0:256], st.pb[5][0:1, 0:256], 0.125))
    v_sb = st.vt[1]

    if KCA < 3:
        st.v_cp = st.vt[1]
        return
    @block.sync
    def _(s):
        wt(s, st.vt, v_sb)
        with nc.allow_non_contiguous_dma(reason="bias row"):
            s.dma_start(out=brow[:, :], in_=st.c_bo.unsqueeze(0)).then_inc(st.sy[0], dinc(st.sy))
    bo_ld = st.sy[1]

    @block.tensor
    def _(t, bo_ld=bo_ld):
        wt(t, st.sy, bo_ld)
        wt(t, st.vt)
        t.matmul(st.pb[6][:, 0:512], st.ones1[:, :], brow[:, 0:512], start=True, stop=True)
        inc(st.tt, t.matmul(st.pb[7][:, 0:512], st.ones1[:, :], brow[:, 512:1024], start=True, stop=True))
    t_bo = st.tt[1]

    @block.vector
    def _(v, t_bo=t_bo):
        wt(v, st.tt, t_bo)
        v.tensor_copy(out=cboB[:, 0:512], in_=st.pb[6][:, 0:512])
        inc(st.vt, v.tensor_copy(out=cboB[:, 512:1024], in_=st.pb[7][:, 0:512]))
    v_bo = st.vt[1]

    if KCA < 4:
        st.v_cp = st.vt[1]
        return
    for dcd in range(8):
        sl = (dcd % 2) * 1024

        @block.sync
        def _(s, dcd=dcd, sl=sl):
            wt(s, st.vt)
            s.dma_start(out=wnat[:, sl:sl + 1024], in_=st.c_Wq[dcd * 128:(dcd + 1) * 128, :]).then_inc(st.sy[0], dinc(st.sy))
        wq_ld = st.sy[1]

        for h2 in range(8):
            @block.tensor
            def _(t, h2=h2, sl=sl, wq_ld=wq_ld):
                wt(t, st.sy, wq_ld)
                wt(t, st.vt)
                for k in range(2):
                    h = h2 * 2 + k
                    inc(st.tt, t.transpose(st.pb[4 + k][0:64, 0:128],
                                           wnat[:, sl + h * 64:sl + (h + 1) * 64], st.idf[:, :]))
            t_tp = st.tt[1]

            @block.vector
            def _(v, h2=h2, t_tp=t_tp):
                wt(v, st.tt, t_tp)
                for k in range(2):
                    h = h2 * 2 + k
                    v.tensor_copy(out=WqT2[:, h * 128:(h + 1) * 128], in_=st.pb[4 + k][0:64, 0:128])
                inc(st.vt, v.engine_nop())
        v_wT = st.vt[1]

        @block.tensor
        def _(t, v_wT=v_wT):
            wt(t, st.vt, v_wT)
            last = None
            for h in range(16):
                last = t.matmul(st.pb[6][:, h * 16:(h + 1) * 16],
                                WqT2[:, h * 128:(h + 1) * 128],
                                khoT2[:, h * 16:(h + 1) * 16],
                                start=True, stop=True)
            inc(st.tt, last)
        t_M = st.tt[1]

        @block.vector
        def _(v, dcd=dcd, t_M=t_M):
            wt(v, st.tt, t_M)
            inc(st.vt, v.tensor_scalar_mul(Mf[:, :], st.pb[6][:, 0:256], 0.125))
            wt(v, st.vt)
            v.tensor_copy(out=Mh[:, dcd * 256:(dcd + 1) * 256], in_=Mf[:, :])
            inc(st.vt, v.tensor_tensor(out=Ml[:, dcd * 256:(dcd + 1) * 256], in0=Mf[:, :],
                                       in1=Mh[:, dcd * 256:(dcd + 1) * 256], op=AL.subtract))
    v_M = st.vt[1]

    if KCSUB < 2:
        st.v_cp = st.vt[1]
        return
    for itg in range(4):
        for itl in range(4):
            it = itg * 4 + itl

            @block.tensor
            def _(t, it=it, v_M=v_M):
                wt(t, st.vt)
                last = None
                for dc in range(8):
                    xh = st.xn1Th[:, dc * 2048 + it * 128: dc * 2048 + (it + 1) * 128]
                    xl = st.xn1Tl[:, dc * 2048 + it * 128: dc * 2048 + (it + 1) * 128]
                    mh = Mh[:, dc * 256:(dc + 1) * 256]
                    ml = Ml[:, dc * 256:(dc + 1) * 256]
                    t.matmul(st.pb[0][:, 0:256], xh, mh, start=(dc == 0), stop=False)
                    t.matmul(st.pb[0][:, 0:256], xl, mh, start=False, stop=False)
                    t.matmul(st.pb[0][:, 0:256], xh, ml, start=False, stop=False)
                last = t.matmul(st.pb[0][:, 0:256], st.ones1[:, :], sbrow[:, 0:256],
                                start=False, stop=True)
                inc(st.tt, last)
            t_S = st.tt[1]

            @block.vector
            def _(v, t_S=t_S):
                wt(v, st.tt, t_S)
                inc(st.vt, v.tensor_copy(out=Pc[:, :], in_=st.pb[0][:, 0:256]))
                wt(v, st.vt)
                inc(st.vt, v.tensor_reduce(out=st.sm2[:, 0:16],
                                           in_=Pc[:, :].rearrange("p (a b) -> p a b", b=16),
                                           axis=AX.X, op=AL.max))
                wt(v, st.vt)
                inc(st.vt, v.tensor_scalar_mul(negm[:, :], st.sm2[:, 0:16], -1.0))
                wt(v, st.vt)
                inc(st.vt, v.engine_nop())
            v_m = st.vt[1]

            @block.scalar
            def _(s, v_m=v_m):
                wt(s, st.vt, v_m)
                last = None
                for h in range(16):
                    last = s.activation(out=Pc[:, h * 16:(h + 1) * 16], in_=Pc[:, h * 16:(h + 1) * 16],
                                        func=AF.Exp, bias=negm[:, h:h + 1], accum_out=den[:, h:h + 1])
                inc(st.ct, last)
            s_e = st.ct[1]

            @block.scalar
            def _(s, s_e=s_e):
                wt(s, st.ct, s_e)
                s.activation(out=inv[:, :], in_=den[:, :], func=AF.Ln)
                inc(st.ct, s.activation(out=inv[:, :], in_=inv[:, :], func=AF.Exp, scale=-1.0))
            s_i = st.ct[1]

            @block.vector
            def _(v, s_i=s_i):
                wt(v, st.ct, s_i)
                w = st.sm2[:, 0:16]
                for _i in range(2):
                    inc(st.vt, v.tensor_tensor(out=w, in0=inv[:, :], in1=den[:, :], op=AL.mult))
                    wt(v, st.vt)
                    inc(st.vt, v.tensor_scalar(out=w, in0=w, scalar1=-1.0, scalar2=2.0,
                                               op0=AL.mult, op1=AL.add))
                    wt(v, st.vt)
                    inc(st.vt, v.tensor_tensor(out=inv[:, :], in0=inv[:, :], in1=w, op=AL.mult))
                    wt(v, st.vt)
                inc(st.vt, v.tensor_tensor(
                    out=Pc[:, :].rearrange("p (a b) -> p a b", b=16),
                    in0=Pc[:, :].rearrange("p (a b) -> p a b", b=16),
                    in1=inv[:, :].unsqueeze(2).broadcast_to([128, 16, 16]), op=AL.mult))
            v_P = st.vt[1]

            for h in (range(16) if KCSUB >= 3 else []):
                b0 = (h % 2) * 64

                @block.tensor
                def _(t, h=h, v_P=v_P):
                    wt(t, st.vt, v_P)
                    wt(t, st.vt)
                    inc(st.tt, t.transpose(st.pb[4 + (h % 2)][0:16, 0:128],
                                           Pc[:, h * 16:(h + 1) * 16], st.idf[:, :]))
                t_pT = st.tt[1]

                @block.vector
                def _(v, h=h, t_pT=t_pT):
                    wt(v, st.tt, t_pT)
                    inc(st.vt, v.tensor_copy(out=pTs[:, :], in_=st.pb[4 + (h % 2)][0:16, 0:128]))
                v_pT = st.vt[1]

                @block.tensor
                def _(t, h=h, b0=b0, v_pT=v_pT):
                    wt(t, st.vt, v_pT)
                    inc(st.tt, t.matmul(st.pb[7][b0:b0 + 64, (h % 2) * 128:(h % 2) * 128 + 128],
                                        vho[:, h * 64:(h + 1) * 64], pTs[:, :],
                                        start=True, stop=True))
                t_av = st.tt[1]

                @block.vector
                def _(v, h=h, b0=b0, itl=itl, t_av=t_av):
                    wt(v, st.tt, t_av)
                    inc(st.vt, v.tensor_copy(
                        out=aCt4[b0:b0 + 64, itl * 1024 + (h // 2) * 128: itl * 1024 + (h // 2) * 128 + 128],
                        in_=st.pb[7][b0:b0 + 64, (h % 2) * 128:(h % 2) * 128 + 128]))
        v_g = st.vt[1]

        for dc in (range(8) if KCSUB >= 4 else []):
            sl = (dc % 2) * 1024

            @block.sync
            def _(s, dc=dc, sl=sl, v_g=v_g):
                wt(s, st.vt)
                s.dma_start(out=wnat[:, sl:sl + 1024], in_=st.c_Wo[dc * 128:(dc + 1) * 128, :]).then_inc(st.sy[0], dinc(st.sy))
            wo_ld = st.sy[1]

            @block.tensor
            def _(t, dc=dc, sl=sl, wo_ld=wo_ld, v_g=v_g):
                wt(t, st.sy, wo_ld)
                wt(t, st.vt)
                last = None
                for itl in range(4):
                    for oc in range(2):
                        last = t.matmul(st.pb[itl * 2 + oc][:, 0:512],
                                        aCt4[:, itl * 1024 + dc * 128: itl * 1024 + (dc + 1) * 128],
                                        wnat[:, sl + oc * 512:sl + (oc + 1) * 512],
                                        start=(dc == 0), stop=(dc == 7))
                inc(st.tt, last)

            @block.vector
            def _(v):
                wt(v, st.tt)
                inc(st.vt, v.engine_nop())
        t_op = st.tt[1]

        @block.vector
        def _(v, itg=itg, t_op=t_op):
            if KCSUB < 4:
                inc(st.vt, v.engine_nop())
                return
            wt(v, st.tt, t_op)
            for itl in range(4):
                it = itg * 4 + itl
                for oc in range(2):
                    xa = st.X[:, it * D + oc * 512: it * D + (oc + 1) * 512]
                    v.scalar_tensor_tensor(out=xa, in0=st.pb[itl * 2 + oc][:, 0:512], scalar=1.0,
                                           in1=xa, op0=AL.mult, op1=AL.add)
            for itl in range(4):
                it = itg * 4 + itl
                for oc in range(2):
                    xa = st.X[:, it * D + oc * 512: it * D + (oc + 1) * 512]
                    v.tensor_tensor(out=xa, in0=xa, in1=cboB[:, oc * 512:(oc + 1) * 512], op=AL.add)
            inc(st.vt, v.engine_nop())
    st.v_cp = st.vt[1]


def kernel(**inputs):
    nc = build_program(debug=False)
    in_maps = make_in_maps(inputs)
    res = run_bass_kernel_spmd(nc, in_maps, core_ids=list(range(8)))
    outs = [r["out"].reshape(T, D) for r in res.results]
    return np.stack(outs, axis=0)


def make_in_maps(inputs):
    iota = np.arange(256, dtype=np.float32)
    jcol = np.arange(128, dtype=np.float32)
    consts = {
        "c_identf": np.eye(128, dtype=np.float32),
        "c_identb": np.eye(128, dtype=np.float32).astype(ml_dtypes.bfloat16),
        "c_mask0": (iota[None, :] >= jcol[:, None]).astype(np.float32),
        "c_mask1": (iota[None, :] >= (jcol[:, None] + 128.0)).astype(np.float32),
        "c_sel16": np.repeat(np.eye(16, dtype=np.float32), 128, axis=1),
        "c_sel64": np.concatenate([np.zeros((64, 64), np.float32),
                                   np.eye(64, dtype=np.float32)], axis=0),
    }
    in_maps = []
    for b in range(8):
        m = dict(consts)
        for k, v in inputs.items():
            if k == "L0_attn_u":
                m[k] = np.ascontiguousarray(np.asarray(v)[b])
            else:
                m[k] = np.asarray(v)
        in_maps.append(m)
    return in_maps


# version 03
# speedup vs baseline: 1.0127x; 1.0127x over previous
import contextlib
import os
KPHASES = int(os.environ.get("KPHASES", "5"))
KCSUB = int(os.environ.get("KCSUB", "4"))
KCA = int(os.environ.get("KCA", "4"))
KCB = int(os.environ.get("KCB", "3"))
import numpy as np
import ml_dtypes
import concourse.bass as bass
import concourse.mybir as mybir
from concourse.bass_utils import run_bass_kernel_spmd

F32 = mybir.dt.float32
BF16 = mybir.dt.bfloat16
AL = mybir.AluOpType
AF = mybir.ActivationFunctionType
AX = mybir.AxisListType

T, NT, D, DI, NH, DS, CONVD, DIP = 2048, 16, 1024, 2048, 16, 64, 2176, 4240
LN2 = float(np.log(2.0))
LN_EPS = 1e-5
RMS_EPS = 1e-5


class St:
    pass


def inc(rec, inst, k=1):
    rec[1] += k
    inst.then_inc(rec[0], k)
    return rec[1]


def wt(eng, rec, val=None):
    eng.wait_ge(rec[0], rec[1] if val is None else val)


def dinc(rec):
    rec[1] += 16
    return 16


def mxfp4_quant_emit(v, st, x_ap, nrows, ncols, kmin, kmax, scl2d, iscl2d, out_ap):
    nb = ncols // 32
    x3 = x_ap.rearrange("p (b k) -> p b k", k=32)
    amax = st.qs[0:nrows, 0:nb]
    pexp = st.qs[0:nrows, nb:2 * nb]
    ipexp = st.qs[0:nrows, 2 * nb:3 * nb]
    tmpb = st.qs[0:nrows, 3 * nb:4 * nb]
    tmpb2 = st.qs[0:nrows, 4 * nb:5 * nb]
    inc(st.vt, v.tensor_reduce(out=amax, in_=x3, axis=AX.X, op=AL.max, apply_absolute_value=True))
    wt(v, st.vt)
    v.tensor_scalar(out=pexp, in0=amax, scalar1=float(2.0 ** kmin), scalar2=float(2.0 ** kmin),
                    op0=AL.is_ge, op1=AL.mult)
    v.tensor_scalar(out=ipexp, in0=amax, scalar1=float(2.0 ** kmin), scalar2=float(2.0 ** -kmin),
                    op0=AL.is_ge, op1=AL.mult)
    for k in range(kmin + 1, kmax + 1):
        v.tensor_scalar(out=tmpb, in0=amax, scalar1=float(2.0 ** k),
                        scalar2=float(2.0 ** k - 2.0 ** (k - 1)), op0=AL.is_ge, op1=AL.mult)
        v.tensor_scalar(out=tmpb2, in0=amax, scalar1=float(2.0 ** k),
                        scalar2=float(2.0 ** -k - 2.0 ** (-k + 1)), op0=AL.is_ge, op1=AL.mult)
        v.tensor_tensor(out=pexp, in0=pexp, in1=tmpb, op=AL.add)
        v.tensor_tensor(out=ipexp, in0=ipexp, in1=tmpb2, op=AL.add)
    v.tensor_tensor(out=ipexp, in0=ipexp, in1=iscl2d, op=AL.mult)
    v.tensor_tensor(out=pexp, in0=pexp, in1=scl2d, op=AL.mult)
    xs = st.q1[0:nrows, 0:ncols]
    v.tensor_tensor(out=xs.rearrange("p (b k) -> p b k", k=32), in0=x3,
                    in1=ipexp.unsqueeze(2).broadcast_to([nrows, nb, 32]), op=AL.mult)
    sgn = st.q2[0:nrows, 0:ncols]
    v.tensor_scalar(out=sgn, in0=xs, scalar1=0.0, scalar2=2.0, op0=AL.is_ge, op1=AL.mult)
    v.tensor_scalar(out=sgn, in0=sgn, scalar1=-1.0, scalar2=None, op0=AL.add)
    a = xs
    v.tensor_tensor(out=a, in0=xs, in1=sgn, op=AL.mult)
    v.tensor_scalar(out=a, in0=a, scalar1=6.0, scalar2=None, op0=AL.min)
    tq = out_ap
    v.tensor_scalar(out=tq, in0=a, scalar1=4.0, scalar2=0.5, op0=AL.is_lt, op1=AL.mult)
    v.tensor_scalar(out=tq, in0=tq, scalar1=0.5, scalar2=None, op0=AL.add)
    v.scalar_tensor_tensor(out=tq, in0=a, scalar=2.0, in1=tq, op0=AL.is_lt, op1=AL.add)
    v.tensor_tensor(out=tq, in0=tq, in1=a, op=AL.mult)
    r = st.lndump[0:nrows, 0:ncols]
    v.tensor_scalar(out=r, in0=tq, scalar1=0.5, scalar2=None, op0=AL.is_gt)
    for thr, op in [(1.5, AL.is_ge), (2.5, AL.is_gt), (3.5, AL.is_ge), (4.5, AL.is_gt), (5.5, AL.is_ge)]:
        v.scalar_tensor_tensor(out=r, in0=tq, scalar=float(thr), in1=r, op0=op, op1=AL.add)
    stp = tq
    v.tensor_scalar(out=stp, in0=a, scalar1=4.0, scalar2=-1.0, op0=AL.is_lt, op1=AL.mult)
    v.tensor_scalar(out=stp, in0=stp, scalar1=2.0, scalar2=None, op0=AL.add)
    v.tensor_scalar(out=a, in0=a, scalar1=2.0, scalar2=-0.5, op0=AL.is_lt, op1=AL.mult)
    v.tensor_tensor(out=stp, in0=stp, in1=a, op=AL.add)
    v.tensor_tensor(out=out_ap, in0=stp, in1=r, op=AL.mult)
    v.tensor_tensor(out=out_ap.rearrange("p (b k) -> p b k", k=32),
                    in0=out_ap.rearrange("p (b k) -> p b k", k=32),
                    in1=pexp.unsqueeze(2).broadcast_to([nrows, nb, 32]), op=AL.mult)
    v.tensor_tensor(out=out_ap, in0=out_ap, in1=sgn, op=AL.mult)


def build_program(debug=False):
    nc = bass.Bass("TRN2", target_bir_lowering=False, detect_race_conditions=False)
    st = St()
    st.nc = nc
    st.es = contextlib.ExitStack()
    st.debug = debug

    def di(name, shape, dtype=F32):
        return nc.dram_tensor(name, shape, dtype, kind="ExternalInput").ap()

    st.u = di("L0_attn_u", [T, D])
    st.ipw = di("L0_attn_in_proj_w", [DIP, D])
    st.convw = di("L0_attn_conv_w", [CONVD, 4])
    st.convb = di("L0_attn_conv_b", [CONVD])
    st.dt_bias = di("L0_attn_dt_bias", [NH])
    st.A_log = di("L0_attn_A_log", [NH])
    st.Dp = di("L0_attn_D", [NH])
    st.xnw = di("L0_attn_x_norm_w", [D])
    st.xnb = di("L0_attn_x_norm_b", [D])
    st.rmsw = di("L0_attn_rms_w", [DI])
    st.opw = di("L0_attn_out_proj_w", [D, DI])
    st.fw = di("L0_ffn_weight", [D, D])
    st.fb = di("L0_ffn_bias", [D])
    st.fdw = di("L0_ffn_delta_w", [D, 32])
    st.fda = di("L0_ffn_delta_act", [1])
    st.rot = di("L1_attn_rotation_params", [16, 64])
    st.ent = di("L1_attn_entangle_params", [16, 64])
    st.sq_in = di("L1_ffn_slot_query", [1, 16, D])
    for n in ["d_Wq", "d_Wk", "d_Wv", "d_Wo", "c_Wq", "c_Wk", "c_Wv", "c_Wo"]:
        setattr(st, n, di("L1_ffn_" + n, [D, D]))
    for n in ["d_bq", "d_bk", "d_bv", "d_bo", "c_bq", "c_bk", "c_bv", "c_bo"]:
        setattr(st, n, di("L1_ffn_" + n, [D]))
    st.elng = di("L1_ffn_e_ln_g", [16, D])
    st.elnb = di("L1_ffn_e_ln_b", [16, D])
    st.eW1 = di("L1_ffn_e_W1", [16, D, 4096])
    st.eb1 = di("L1_ffn_e_b1", [16, 4096])
    st.eWg = di("L1_ffn_e_Wg", [16, D, 4096])
    st.ebg = di("L1_ffn_e_bg", [16, 4096])
    st.eW2 = di("L1_ffn_e_W2", [16, 4096, D])
    st.eb2 = di("L1_ffn_e_b2", [16, D])
    st.identf = di("c_identf", [128, 128])
    st.identb = di("c_identb", [128, 128], BF16)
    st.mask0 = di("c_mask0", [128, 256])
    st.mask1 = di("c_mask1", [128, 256])
    st.sel16i = di("c_sel16", [16, 2048])
    st.sel64i = di("c_sel64", [128, 64])

    st.out = nc.dram_tensor("out", [T, D], F32, kind="ExternalOutput").ap()
    st.s_ipW = nc.dram_tensor("s_ipW", [D, DIP], F32, kind="Internal").ap()
    st.s_opW = nc.dram_tensor("s_opW", [DI, D], F32, kind="Internal").ap()
    st.s_fqW = nc.dram_tensor("s_fqW", [D, D], F32, kind="Internal").ap()

    st.dbg = {}
    if debug:
        for name, shape in [
            ("d_xn0", [128, 2 * D]), ("d_u", [128, 2 * D]),
            ("d_gT", [128, 16 * 256]), ("d_xBC", [128, 17 * 256]),
            ("d_xBCp", [128, 17 * 256]), ("d_dt1", [16, 2048]),
            ("d_xn1h", [128, 8192]), ("d_o", [16, 1024]), ("d_o2", [128, 128]),
            ("d_aX", [128, 16384]), ("d_negm", [1, 2048]), ("d_S0", [128, 512]),
            ("d_Qh", [65, 1024]), ("d_Ql", [65, 1024]), ("d_Kh", [65, 1024]), ("d_Kl", [65, 1024]),
            ("d_dqh", [16, 1024]), ("d_dkh", [128, 2048]), ("d_dvh", [128, 2048]),
            ("d_eh", [16, 1024]), ("d_eu", [128, 32]), ("d_eg", [128, 32]), ("d_ey", [1, 1024]),
            ("d_eur", [128, 32]),
            ("d_eb1T", [128, 512]),
            ("d_dS", [16, 2048]), ("d_daD2", [64, 256]),
            ("d_P0", [128, 512]), ("d_aO0", [65, 1024]), ("d_dd", [128, 16]),
            ("d_dt", [16, 2048]), ("d_xq", [128, 2 * D]),
            ("d_xL0", [128, NT * D]),
        ]:
            st.dbg[name] = nc.dram_tensor(name, shape, F32, kind="ExternalOutput").ap()

    with st.es:
        es = st.es
        sbt = lambda name, shape, dt=F32: es.enter_context(nc.sbuf_tensor(name, shape, dt))

        st.X = sbt("X", [128, NT * D])
        st.idf = sbt("idf", [128, 128])
        st.idb = sbt("idb", [128, 128], BF16)
        st.msk = sbt("msk", [128, 512])
        st.ones1 = sbt("ones1", [1, 128])
        st.onesc = sbt("onesc", [128, 1])
        st.xwb = sbt("xwb", [128, D])
        st.xbb = sbt("xbb", [128, D])
        st.fbb = sbt("fbb", [128, D])
        st.cwT = sbt("cwT", [128, 68])
        st.cbT = sbt("cbT", [128, 17])
        st.rwT = sbt("rwT", [128, 16])
        st.dtbc = sbt("dtbc", [16, 1])
        st.Acol = sbt("Acol", [16, 1])
        st.DB = sbt("DB", [128, 16])
        st.sm1 = sbt("sm1", [1, 528])
        st.sm2 = sbt("sm2", [128, 16])
        st.srow = sbt("srow", [1, 256])
        st.sel16 = sbt("sel16", [16, 2048])
        st.sel64 = sbt("sel64", [128, 64])
        st.AR = sbt("AR", [128, 29824])
        AR = st.AR
        st.xn0c = AR[:, 0:2048]
        st.gT = AR[:, 2048:6144]
        st.xBCc = AR[:, 6144:10496]
        st.q1 = AR[:, 10496:11520]
        st.q2 = AR[:, 11520:12544]
        st.uT = AR[:, 12544:14592]
        st.wip = AR[:, 14592:16640]
        st.ghl = AR[:, 16640:18688]
        st.wop = AR[:, 18688:20736]
        st.wfq = AR[:, 20736:22784]
        st.dtp = AR[0:16, 22784:24832]
        st.PS = AR[0:64, 24832:26880]
        st.attb = AR[:, 26880:27392]
        st.Eb = AR[:, 27392:27648]
        st.Cf32 = AR[0:64, 28160:28416]
        st.btok = AR[:, 28416:28544]
        st.xhT = AR[:, 28544:28800]
        st.wBh = AR[:, 28800:28928]
        st.sS = AR[:, 28928:29024]
        st.cdB = AR[:, 29024:29040]
        st.hist = AR[:, 29040:29092]
        st.cin = AR[:, 29092:29352]
        st.qs = AR[:, 29352:29512]
        st.rows = AR[0:1, 0:3088]
        st.uc = st.gT[:, 0:2048]
        st.lndump = st.gT[:, 2048:3072]
        st.pnat = st.gT
        st.pspl = st.uT
        st.ptp = AR[:, 6144:8192]
        st.psc = st.qs[:, 144:152]
        st.xqT = st.uT
        st.xn1T = AR[:, 0:16384]
        st.t_tok = AR[:, 16384:17408]
        st.ldmp1 = AR[:, 18432:19456]
        st.Sd = AR[0:16, 16384:18432]
        st.khT = AR[:, 23568:25616]
        st.vhT = AR[:, 20480:22528]
        st.wnat = AR[:, 27664:29712]
        st.sqT = AR[:, 25616:25744]
        st.qhC = AR[0:16, 17408:18432]
        st.o_tok = AR[0:16, 24592:25616]
        st.Q65 = AR[0:65, 16384:18432]
        st.K65 = AR[0:65, 18432:20480]
        st.aT = AR[:, 20480:21504]
        st.Vb = AR[:, 21504:22544]
        st.Pb = AR[:, 22544:23568]
        st.aO = AR[0:65, 23568:24592]
        st.b1T = AR[:, 25616:26128]
        st.bgT = AR[:, 26128:26640]
        st.hT = AR[:, 26640:26768]
        st.uTe = AR[:, 26896:26928]
        st.gTe = AR[:, 26928:26960]
        st.espr = AR[0:1, 26992:27248]
        st.eb2n = AR[0:16, 25616:26640]
        st.ebnat = AR[0:16, 20480:24576]
        st.kvoT = AR[:, 29184:29312]
        st.Pc = AR[:, 26640:26896]
        st.rotT = AR[0:64, 29760:29776]
        st.entT = AR[0:64, 29776:29792]
        st.dbT = AR[:, 29792:29808]

        st.pb = [es.enter_context(nc.psum_tensor(f"pb{i}", [128, 512], F32)) for i in range(8)]

        def sem(name):
            return [es.enter_context(nc.semaphore(name)), 0]
        st.sy = sem("sy")
        st.sd = sem("sd")
        st.gd = sem("gd")
        st.ot = sem("ot")
        st.tt = sem("tt")
        st.vt = sem("vt")
        st.ct = sem("ct")
        st.ipc = sem("ipc")
        st.ipe = sem("ipe")
        st.opc = sem("opc")
        st.fqc = sem("fqc")

        st.block = es.enter_context(nc.Block())

        emit_p0(st)
        emit_prep(st)
        for c in range(8):
            emit_l0_chunk(st, c)
        emit_p2(st)
        if KPHASES >= 3:
            emit_dpath(st)
        if KPHASES >= 4:
            emit_p4(st)
        if KPHASES >= 5:
            emit_cpath(st)

        block = st.block

        @block.sync
        def _(s):
            wt(s, st.vt)
            wt(s, st.tt)
            wt(s, st.ct)
            s.dma_start(out=st.out.rearrange("(t p) d -> p t d", p=128),
                        in_=st.X[:, :].rearrange("p (t d) -> p t d", d=D)).then_inc(st.ot[0], dinc(st.ot))
            s.wait_ge(st.ot[0], 16)

        if debug:
            @block.scalar
            def _(s):
                wt(s, st.vt)
                wt(s, st.tt)
                s.dma_start(out=st.dbg["d_xL0"], in_=st.X[:, :]).then_inc(st.sd[0], dinc(st.sd))
                s.wait_ge(st.sd[0], st.sd[1])

    return nc


def emit_p0(st):
    nc, block = st.nc, st.block

    @block.sync
    def _(s):
        s.dma_start(out=st.X[:, :].rearrange("p (t d) -> p t d", d=D),
                    in_=st.u.rearrange("(t p) d -> p t d", p=128)).then_inc(st.sy[0], dinc(st.sy))
        s.dma_start(out=st.idf[:, :], in_=st.identf).then_inc(st.sy[0], dinc(st.sy))
        s.dma_start(out=st.idb[:, :], in_=st.identb).then_inc(st.sy[0], dinc(st.sy))
        s.dma_start(out=st.msk[:, 0:256], in_=st.mask0).then_inc(st.sy[0], dinc(st.sy))
        s.dma_start(out=st.msk[:, 256:512], in_=st.mask1).then_inc(st.sy[0], dinc(st.sy))
        s.dma_start(out=st.sel16[:, :], in_=st.sel16i).then_inc(st.sy[0], dinc(st.sy))
        s.dma_start(out=st.sel64[:, :], in_=st.sel64i).then_inc(st.sy[0], dinc(st.sy))
        with nc.allow_non_contiguous_dma(reason="one-time small constant rearranges"):
            s.dma_start(out=st.rows[0:1, 0:1024], in_=st.xnw.unsqueeze(0)).then_inc(st.sy[0], dinc(st.sy))
            s.dma_start(out=st.rows[0:1, 1024:2048], in_=st.xnb.unsqueeze(0)).then_inc(st.sy[0], dinc(st.sy))
            s.dma_start(out=st.rows[0:1, 2048:3072], in_=st.fb.unsqueeze(0)).then_inc(st.sy[0], dinc(st.sy))
            s.dma_start(out=st.rows[0:1, 3072:3088], in_=st.Dp.unsqueeze(0)).then_inc(st.sy[0], dinc(st.sy))
            s.dma_start(out=st.cwT[:, :].rearrange("p (ct k) -> p ct k", k=4),
                        in_=st.convw.rearrange("(ct p) k -> p ct k", p=128)).then_inc(st.sy[0], dinc(st.sy))
            s.dma_start(out=st.cbT[:, :], in_=st.convb.rearrange("(ct p) -> p ct", p=128),
                        allow_slow_non_contiguous=True).then_inc(st.sy[0], dinc(st.sy))
            s.dma_start(out=st.rwT[:, :], in_=st.rmsw.rearrange("(i p) -> p i", p=128),
                        allow_slow_non_contiguous=True).then_inc(st.sy[0], dinc(st.sy))
            s.dma_start(out=st.dtbc[:, :], in_=st.dt_bias.unsqueeze(1)).then_inc(st.sy[0], dinc(st.sy))
            s.dma_start(out=st.Acol[:, :], in_=st.A_log.unsqueeze(1)).then_inc(st.sy[0], dinc(st.sy))
            s.dma_start(out=st.sm1[0:1, 0:1], in_=st.fda.unsqueeze(0)).then_inc(st.sy[0], dinc(st.sy))
    st.LOADS0 = st.sy[1]

    @block.vector
    def _(v):
        wt(v, st.sy, st.LOADS0)
        v.memset(st.ones1[:, :], 1.0)
        inc(st.vt, v.memset(st.onesc[:, :], 1.0))
    v_init = st.vt[1]

    @block.scalar
    def _(s):
        wt(s, st.sy, st.LOADS0)
        s.activation(out=st.Acol[:, :], in_=st.Acol[:, :], func=AF.Exp)
        s.activation(out=st.sm1[0:1, 1:2], in_=st.sm1[0:1, 0:1], func=AF.Tanh, scale=4.0)
        s.activation(out=st.sm1[0:1, 2:3], in_=st.sm1[0:1, 1:2], func=AF.Exp, scale=0.5 * LN2)
        inc(st.ct, s.activation(out=st.sm1[0:1, 3:4], in_=st.sm1[0:1, 1:2], func=AF.Exp, scale=-0.5 * LN2))
    s_da = st.ct[1]

    @block.vector
    def _(v):
        wt(v, st.ct, s_da)
        inc(st.vt, v.tensor_scalar_mul(st.Acol[:, :], st.Acol[:, :], -1.0))

    @block.tensor
    def _(t):
        wt(t, st.vt, v_init)
        wt(t, st.ct, s_da)
        t.matmul(st.pb[0][:, :], st.ones1[:, :], st.rows[0:1, 0:512], start=True, stop=True)
        t.matmul(st.pb[1][:, :], st.ones1[:, :], st.rows[0:1, 512:1024], start=True, stop=True)
        t.matmul(st.pb[2][:, :], st.ones1[:, :], st.rows[0:1, 1024:1536], start=True, stop=True)
        t.matmul(st.pb[3][:, :], st.ones1[:, :], st.rows[0:1, 1536:2048], start=True, stop=True)
        t.matmul(st.pb[4][:, :], st.ones1[:, :], st.rows[0:1, 2048:2560], start=True, stop=True)
        t.matmul(st.pb[5][:, :], st.ones1[:, :], st.rows[0:1, 2560:3072], start=True, stop=True)
        t.matmul(st.pb[6][:, 0:16], st.ones1[:, :], st.rows[0:1, 3072:3088], start=True, stop=True)
        inc(st.tt, t.matmul(st.pb[7][:, 0:2], st.ones1[:, :], st.sm1[0:1, 2:4], start=True, stop=True))
    t_bc = st.tt[1]
    st.t_bc = t_bc

    @block.vector
    def _(v):
        wt(v, st.tt, t_bc)
        v.tensor_copy(out=st.xwb[:, 0:512], in_=st.pb[0][:, :])
        v.tensor_copy(out=st.xwb[:, 512:1024], in_=st.pb[1][:, :])
        v.tensor_copy(out=st.xbb[:, 0:512], in_=st.pb[2][:, :])
        v.tensor_copy(out=st.xbb[:, 512:1024], in_=st.pb[3][:, :])
        v.tensor_copy(out=st.fbb[:, 0:512], in_=st.pb[4][:, :])
        v.tensor_copy(out=st.fbb[:, 512:1024], in_=st.pb[5][:, :])
        v.tensor_copy(out=st.DB[:, :], in_=st.pb[6][:, 0:16])
        v.tensor_scalar(out=st.sm2[:, 0:1], in0=st.pb[7][:, 0:1], scalar1=0.25, scalar2=None, op0=AL.mult)
        v.tensor_scalar(out=st.sm2[:, 1:2], in0=st.pb[7][:, 1:2], scalar1=4.0, scalar2=None, op0=AL.mult)
        v.memset(st.hist[:, :], 0.0)
        v.memset(st.dtp[:, :], 0.0)
        inc(st.vt, v.memset(st.PS[:, :], 0.0))


def emit_norm_scale(st, natv, nat_rows, two_cols, extra_mult, dma_done):
    nc, block = st.nc, st.block

    @block.scalar
    def _(s):
        wt(s, st.sy, dma_done)
        s.activation(out=st.q1[0:nat_rows, :], in_=natv[:, 0:1024], func=AF.Square,
                     accum_out=st.psc[0:nat_rows, 0:1])
        if two_cols:
            s.activation(out=st.q1[0:nat_rows, :], in_=natv[:, 1024:2048], func=AF.Square,
                         accum_out=st.psc[0:nat_rows, 6:7])
        inc(st.ct, s.activation(out=st.psc[0:nat_rows, 7:8], in_=st.psc[0:nat_rows, 0:1], func=AF.Identity))
    s_sq = st.ct[1]

    @block.vector
    def _(v):
        wt(v, st.ct, s_sq)
        if two_cols:
            inc(st.vt, v.tensor_tensor(out=st.psc[0:nat_rows, 0:1], in0=st.psc[0:nat_rows, 0:1],
                                       in1=st.psc[0:nat_rows, 6:7], op=AL.add))
            wt(v, st.vt)
        inc(st.vt, v.tensor_scalar(out=st.psc[0:nat_rows, 1:2], in0=st.psc[0:nat_rows, 0:1],
                                   scalar1=1e-40, scalar2=None, op0=AL.max))
    v_m = st.vt[1]

    @block.scalar
    def _(s):
        wt(s, st.vt, v_m)
        s.activation(out=st.psc[0:nat_rows, 2:3], in_=st.psc[0:nat_rows, 1:2], func=AF.Ln)
        inc(st.ct, s.activation(out=st.psc[0:nat_rows, 3:4], in_=st.psc[0:nat_rows, 2:3],
                                func=AF.Exp, scale=-0.5))
    s_seed = st.ct[1]

    @block.vector
    def _(v):
        wt(v, st.ct, s_seed)
        y = st.psc[0:nat_rows, 3:4]
        for _i in range(2):
            inc(st.vt, v.tensor_tensor(out=st.psc[0:nat_rows, 4:5], in0=y, in1=y, op=AL.mult))
            wt(v, st.vt)
            inc(st.vt, v.tensor_tensor(out=st.psc[0:nat_rows, 4:5], in0=st.psc[0:nat_rows, 4:5],
                                       in1=st.psc[0:nat_rows, 1:2], op=AL.mult))
            wt(v, st.vt)
            inc(st.vt, v.tensor_scalar(out=st.psc[0:nat_rows, 4:5], in0=st.psc[0:nat_rows, 4:5],
                                       scalar1=-0.5, scalar2=1.5, op0=AL.mult, op1=AL.add))
            wt(v, st.vt)
            inc(st.vt, v.tensor_tensor(out=y, in0=y, in1=st.psc[0:nat_rows, 4:5], op=AL.mult))
            wt(v, st.vt)
        inc(st.vt, v.tensor_scalar(out=st.psc[0:nat_rows, 5:6], in0=y,
                                   scalar1=1e6, scalar2=extra_mult, op0=AL.min, op1=AL.mult))


def emit_prep_store(st, src_ap, nat_rows, dcols, dst, col0):
    nc, block = st.nc, st.block
    ncolt = dcols // 128
    for dc in range(ncolt):
        bh = st.pb[4 + (dc % 2)]

        @block.tensor
        def _(t, dc=dc, bh=bh):
            wt(t, st.vt)
            inc(st.tt, t.transpose(bh[:, 0:nat_rows],
                                   src_ap[0:nat_rows, dc * 128:(dc + 1) * 128],
                                   st.idf[0:nat_rows, 0:nat_rows]))
        t_tp = st.tt[1]

        @block.vector
        def _(v, dc=dc, bh=bh, t_tp=t_tp):
            wt(v, st.tt, t_tp)
            if dc == 0:
                wt(v, st.sd, st.ptp_free)
            inc(st.vt, v.tensor_copy(out=st.ptp[:, dc * nat_rows:(dc + 1) * nat_rows],
                                     in_=bh[:, 0:nat_rows]))
    v_done = st.vt[1]

    @block.scalar
    def _(s):
        wt(s, st.vt, v_done)
        s.dma_start(out=dst[0:dcols, col0:col0 + nat_rows].rearrange("(dc p) o -> p dc o", p=128),
                    in_=st.ptp[:, 0:ncolt * nat_rows]).then_inc(st.sd[0], dinc(st.sd))
    st.ptp_free = st.sd[1]


def emit_prep(st):
    nc, block = st.nc, st.block
    st.ptp_free = 0

    for ot in range(34):
        nat_rows = 128 if ot < 33 else 16
        par = ot % 2
        natv = st.pnat[0:nat_rows, par * 2048: par * 2048 + 1024]

        @block.sync
        def _(s, ot=ot, natv=natv, nat_rows=nat_rows):
            if ot == 0:
                wt(s, st.tt, st.t_bc)
            if ot >= 2:
                wt(s, st.vt)
            s.dma_start(out=natv, in_=st.ipw[ot * 128: ot * 128 + nat_rows, :]).then_inc(st.sy[0], dinc(st.sy))
        emit_norm_scale(st, natv, nat_rows, False, float(2.0 ** -5), st.sy[1])

        @block.vector
        def _(v, natv=natv, nat_rows=nat_rows):
            wt(v, st.vt)
            inc(st.vt, v.tensor_scalar(out=natv, in0=natv, scalar1=st.psc[0:nat_rows, 5:6],
                                       scalar2=None, op0=AL.mult))
        emit_prep_store(st, natv, nat_rows, 1024, st.s_ipW, ot * 128)

    for ot in range(8):
        par = ot % 2
        natv = st.pnat[:, par * 2048:(par + 1) * 2048]

        @block.sync
        def _(s, ot=ot, natv=natv):
            wt(s, st.vt)
            s.dma_start(out=natv, in_=st.opw[ot * 128:(ot + 1) * 128, :]).then_inc(st.sy[0], dinc(st.sy))
        emit_norm_scale(st, natv, 128, True, 1.0, st.sy[1])

        @block.vector
        def _(v, natv=natv):
            inc(st.vt, v.tensor_scalar(out=natv, in0=natv, scalar1=st.psc[:, 5:6],
                                       scalar2=None, op0=AL.mult))
        emit_prep_store(st, natv, 128, 2048, st.s_opW, ot * 128)

    for ot in range(8):
        par = ot % 2
        natv = st.pnat[:, par * 2048: par * 2048 + 1024]

        @block.sync
        def _(s, ot=ot, natv=natv):
            wt(s, st.vt)
            s.dma_start(out=natv, in_=st.fw[ot * 128:(ot + 1) * 128, :]).then_inc(st.sy[0], dinc(st.sy))
            s.dma_start(out=st.qs[:, 96:128], in_=st.fdw[ot * 128:(ot + 1) * 128, :]).then_inc(st.sy[0], dinc(st.sy))
        dma_d = st.sy[1]

        @block.scalar
        def _(s, dma_d=dma_d):
            wt(s, st.sy, dma_d)
            s.activation(out=st.qs[:, 128:160] if False else st.q1[:, 0:32],
                         in_=st.qs[:, 96:128], func=AF.Tanh, scale=4.0)
            s.activation(out=st.q2[:, 0:32], in_=st.q1[:, 0:32], func=AF.Exp, scale=0.5 * LN2)
            inc(st.ct, s.activation(out=st.q2[:, 32:64], in_=st.q1[:, 0:32], func=AF.Exp, scale=-0.5 * LN2))
        s_dd = st.ct[1]

        @block.vector
        def _(v, natv=natv, s_dd=s_dd):
            wt(v, st.ct, s_dd)
            v.tensor_scalar(out=st.cin[:, 32:64], in0=st.q2[:, 0:32], scalar1=0.25,
                            scalar2=None, op0=AL.mult)
            v.tensor_scalar(out=st.cin[:, 0:32], in0=st.q2[:, 32:64], scalar1=4.0,
                            scalar2=None, op0=AL.mult)
            mxfp4_quant_emit(v, st, natv, 128, 1024, -16, -1,
                             st.cin[:, 32:64], st.cin[:, 0:32], st.xn0c[:, 0:1024])
            inc(st.vt, v.engine_nop())
        emit_prep_store(st, st.xn0c[:, 0:1024], 128, 1024, st.s_fqW, ot * 128)

    st.prep_sd = st.sd[1]


def emit_ln(st, src_ap, dst_ap, w_bcast, b_bcast, ncols, dump=None):
    nc, block = st.nc, st.block

    if dump is None:
        dump = st.lndump

    @block.scalar
    def _(s):
        wt(s, st.vt)
        inc(st.ct, s.activation(out=dump[:, 0:ncols], in_=src_ap, func=AF.Identity,
                                accum_out=st.sm2[:, 4:5]))
    s_sum = st.ct[1]

    @block.vector
    def _(v):
        wt(v, st.ct, s_sum)
        inc(st.vt, v.tensor_scalar_mul(st.sm2[:, 4:5], st.sm2[:, 4:5], -1.0 / ncols))
    v_mu = st.vt[1]

    @block.scalar
    def _(s):
        wt(s, st.vt, v_mu)
        inc(st.ct, s.activation(out=dump[:, 0:ncols], in_=src_ap, func=AF.Square,
                                bias=st.sm2[:, 4:5], accum_out=st.sm2[:, 5:6]))
    s_sq = st.ct[1]

    @block.vector
    def _(v):
        wt(v, st.ct, s_sq)
        inc(st.vt, v.tensor_scalar(out=st.sm2[:, 6:7], in0=st.sm2[:, 5:6], scalar1=1.0 / ncols,
                                   scalar2=LN_EPS, op0=AL.mult, op1=AL.add))
    v_v = st.vt[1]

    @block.scalar
    def _(s):
        wt(s, st.vt, v_v)
        s.activation(out=st.sm2[:, 7:8], in_=st.sm2[:, 6:7], func=AF.Ln)
        inc(st.ct, s.activation(out=st.sm2[:, 5:6], in_=st.sm2[:, 7:8], func=AF.Exp, scale=-0.5))
    s_seed = st.ct[1]

    @block.vector
    def _(v):
        wt(v, st.ct, s_seed)
        y = st.sm2[:, 5:6]
        for _i in range(2):
            inc(st.vt, v.tensor_tensor(out=st.sm2[:, 7:8], in0=y, in1=y, op=AL.mult))
            wt(v, st.vt)
            inc(st.vt, v.tensor_tensor(out=st.sm2[:, 7:8], in0=st.sm2[:, 7:8], in1=st.sm2[:, 6:7], op=AL.mult))
            wt(v, st.vt)
            inc(st.vt, v.tensor_scalar(out=st.sm2[:, 7:8], in0=st.sm2[:, 7:8], scalar1=-0.5, scalar2=1.5,
                                       op0=AL.mult, op1=AL.add))
            wt(v, st.vt)
            inc(st.vt, v.tensor_tensor(out=y, in0=y, in1=st.sm2[:, 7:8], op=AL.mult))
            wt(v, st.vt)
        v.tensor_scalar(out=dst_ap, in0=src_ap, scalar1=st.sm2[:, 4:5], scalar2=None, op0=AL.add)
        v.tensor_scalar(out=dst_ap, in0=dst_ap, scalar1=y, scalar2=None, op0=AL.mult)
        if w_bcast is not None:
            v.tensor_tensor(out=dst_ap, in0=dst_ap, in1=w_bcast, op=AL.mult)
            v.tensor_tensor(out=dst_ap, in0=dst_ap, in1=b_bcast, op=AL.add)
        inc(st.vt, v.engine_nop())


def emit_l0_chunk(st, c):
    nc, block = st.nc, st.block
    xsl = st.X[:, (2 * c) * D:(2 * c + 2) * D]

    @block.tensor
    def _(t):
        wt(t, st.vt)
        wt(t, st.ct)
        wt(t, st.ipe)
        inc(st.tt, t.matmul(st.pb[7][0:1, 4:5], st.ones1[:, 0:1], st.ones1[0:1, 0:1], start=True, stop=True))

    @block.vector
    def _(v):
        wt(v, st.ct)
        wt(v, st.tt)
        wt(v, st.ipc)
        wt(v, st.opc)
        wt(v, st.fqc)
        inc(st.vt, v.engine_nop())

    @block.scalar
    def _(s):
        wt(s, st.vt)
        wt(s, st.tt)
        inc(st.ct, s.activation(out=st.sm1[0:1, 8:9], in_=st.sm1[0:1, 0:1], func=AF.Identity))

    for tsub in range(2):
        emit_ln(st, xsl[:, tsub * D:(tsub + 1) * D], st.xn0c[:, tsub * D:(tsub + 1) * D], None, None, D)
        emit_ln(st, st.xn0c[:, tsub * D:(tsub + 1) * D], st.uc[:, tsub * D:(tsub + 1) * D],
                st.xwb[:, :], st.xbb[:, :], D)

    @block.vector
    def _(v):
        if c > 0:
            wt(v, st.ipc, 34 * c)
        inc(st.vt, v.engine_nop())
    v_usp = st.vt[1]

    if st.debug and c == 0:
        @block.scalar
        def _(s):
            wt(s, st.vt, v_usp)
            s.dma_start(out=st.dbg["d_xn0"], in_=st.xn0c[:, :]).then_inc(st.sd[0], dinc(st.sd))
            s.dma_start(out=st.dbg["d_u"], in_=st.uc[:, :]).then_inc(st.sd[0], dinc(st.sd))

        @block.vector
        def _(v):
            wt(v, st.sd)
            inc(st.vt, v.engine_nop())


    for i in range(16):
        tsub, dc = divmod(i, 8)
        bnk = st.pb[4 + (i % 2)]

        @block.tensor
        def _(t, i=i, tsub=tsub, dc=dc, bnk=bnk):
            wt(t, st.vt)
            inc(st.tt, t.transpose(bnk[:, 0:128],
                                   st.uc[:, tsub * 1024 + dc * 128: tsub * 1024 + (dc + 1) * 128],
                                   st.idf[:, :]))
        t_tp = st.tt[1]

        @block.vector
        def _(v, tsub=tsub, dc=dc, bnk=bnk, t_tp=t_tp):
            wt(v, st.tt, t_tp)
            inc(st.vt, v.tensor_copy(out=st.uT[:, dc * 256 + tsub * 128: dc * 256 + (tsub + 1) * 128],
                                     in_=bnk[:, 0:128]))
    v_uT = st.vt[1]

    for ot in range(34):
        nat_rows = 128 if ot < 33 else 16
        slot = ot % 2
        unit = c * 34 + ot

        @block.sync
        def _(s, ot=ot, slot=slot, unit=unit, nat_rows=nat_rows):
            if unit == 0:
                wt(s, st.sd, st.prep_sd)
            if unit >= 2:
                wt(s, st.ipc, unit - 1)
            s.dma_start(out=st.wip[:, slot * 1024: slot * 1024 + 8 * nat_rows],
                        in_=st.s_ipW[:, ot * 128: ot * 128 + nat_rows].rearrange("(dc p) o -> p dc o", p=128)).then_inc(st.sy[0], dinc(st.sy))
        w_done = st.sy[1]

        bnk = st.pb[ot % 2]

        @block.tensor
        def _(t, ot=ot, slot=slot, unit=unit, nat_rows=nat_rows, bnk=bnk, w_done=w_done):
            wt(t, st.sy, w_done)
            if ot == 0:
                wt(t, st.vt, v_uT)
            if unit >= 2:
                wt(t, st.ipe, unit - 1)
            last = None
            for dc in range(8):
                wh = st.wip[:, slot * 1024 + dc * nat_rows: slot * 1024 + (dc + 1) * nat_rows]
                uh = st.uT[:, dc * 256:(dc + 1) * 256]
                last = t.matmul(bnk[0:nat_rows, 0:256], wh, uh, start=(dc == 0), stop=(dc == 7))
            inc(st.ipc, last)

        if ot < 16:
            @block.scalar
            def _(s, ot=ot, unit=unit, bnk=bnk):
                wt(s, st.ipc, unit + 1)
                a = s.activation(out=st.gT[:, ot * 256:(ot + 1) * 256], in_=bnk[:, 0:256], func=AF.Silu)
                inc(st.ipe, a)
        elif ot < 33:
            @block.vector
            def _(v, ot=ot, unit=unit, bnk=bnk):
                wt(v, st.ipc, unit + 1)
                inc(st.ipe, v.tensor_copy(out=st.xBCc[:, (ot - 16) * 256:(ot - 15) * 256], in_=bnk[:, 0:256]))
        else:
            @block.vector
            def _(v, unit=unit, bnk=bnk):
                wt(v, st.ipc, unit + 1)
                inc(st.ipe, v.tensor_copy(out=st.dtp[0:16, 0:256], in_=bnk[0:16, 0:256]))
    ip_done = st.ipe[1]

    if st.debug and c == 1:
        @block.scalar
        def _(s):
            wt(s, st.ipe, ip_done)
            s.dma_start(out=st.dbg["d_xBC"], in_=st.xBCc[:, :]).then_inc(st.sd[0], dinc(st.sd))

        @block.vector
        def _(v):
            wt(v, st.sd)
            inc(st.vt, v.engine_nop())

    @block.vector
    def _(v):
        wt(v, st.ipe, ip_done)
        for ct in range(17):
            tile = st.xBCc[:, ct * 256:(ct + 1) * 256]
            v.tensor_copy(out=st.cin[:, 0:3], in_=st.hist[:, ct * 3:(ct + 1) * 3])
            inc(st.vt, v.tensor_copy(out=st.cin[:, 3:259], in_=tile))
            wt(v, st.vt)
            v.tensor_scalar(out=tile, in0=st.cin[:, 0:256], scalar1=st.cwT[:, 4 * ct:4 * ct + 1],
                            scalar2=None, op0=AL.mult)
            for k in range(1, 4):
                v.scalar_tensor_tensor(out=tile, in0=st.cin[:, k:k + 256],
                                       scalar=st.cwT[:, 4 * ct + k:4 * ct + k + 1],
                                       in1=tile, op0=AL.mult, op1=AL.add)
            v.tensor_copy(out=st.hist[:, ct * 3:(ct + 1) * 3], in_=st.cin[:, 256:259])
        inc(st.vt, v.engine_nop())
    v_conv = st.vt[1]

    @block.scalar
    def _(s):
        wt(s, st.vt, v_conv)
        for ct in range(17):
            tile = st.xBCc[:, ct * 256:(ct + 1) * 256]
            s.activation(out=tile, in_=tile, func=AF.Silu, bias=st.cbT[:, ct:ct + 1])
        inc(st.ct, s.activation(out=st.dtp[0:16, 256:512], in_=st.dtp[0:16, 0:256],
                                func=AF.Abs, bias=st.dtbc[:, :]))
    s_conv = st.ct[1]

    if st.debug and c == 0:
        @block.scalar
        def _(s):
            s.dma_start(out=st.dbg["d_gT"], in_=st.gT[:, :]).then_inc(st.sd[0], dinc(st.sd))

        @block.vector
        def _(v):
            wt(v, st.sd)
            inc(st.vt, v.engine_nop())

    if st.debug and c == 1:
        @block.vector
        def _(v):
            wt(v, st.ct, s_conv)
            inc(st.vt, v.engine_nop())

        @block.scalar
        def _(s):
            wt(s, st.vt)
            s.dma_start(out=st.dbg["d_xBCp"], in_=st.xBCc[:, :]).then_inc(st.sd[0], dinc(st.sd))

        @block.vector
        def _(v):
            wt(v, st.sd)
            inc(st.vt, v.engine_nop())

    @block.vector
    def _(v):
        wt(v, st.ct, s_conv)
        inc(st.vt, v.tensor_scalar(out=st.dtp[0:16, 0:256], in0=st.dtp[0:16, 0:256],
                                   scalar1=st.dtbc[:, :], scalar2=None, op0=AL.add))
    v_dtx = st.vt[1]

    @block.scalar
    def _(s):
        wt(s, st.vt, v_dtx)
        s.activation(out=st.dtp[0:16, 512:768], in_=st.dtp[0:16, 256:512], func=AF.Exp, scale=-1.0)
        inc(st.ct, s.activation(out=st.dtp[0:16, 512:768], in_=st.dtp[0:16, 512:768], func=AF.Ln, bias=1.0))
    s_dt1 = st.ct[1]

    @block.vector
    def _(v):
        wt(v, st.ct, s_dt1)
        v.scalar_tensor_tensor(out=st.dtp[0:16, 256:512], in0=st.dtp[0:16, 0:256], scalar=0.0,
                               in1=st.dtp[0:16, 512:768], op0=AL.max, op1=AL.add)
        v.tensor_scalar(out=st.dtp[0:16, 512:768], in0=st.dtp[0:16, 256:512],
                        scalar1=st.Acol[:, :], scalar2=None, op0=AL.mult)
        v.tensor_tensor_scan(out=st.dtp[0:16, 768:1024], data0=st.dtp[0:16, 512:768],
                             data1=st.dtp[0:16, 512:768], initial=0.0, op0=AL.add, op1=AL.bypass)
        inc(st.vt, v.tensor_scalar_mul(st.dtp[0:16, 1024:1280], st.dtp[0:16, 768:1024], -1.0))
    v_dt2 = st.vt[1]

    @block.scalar
    def _(s):
        wt(s, st.vt, v_dt2)
        s.activation(out=st.dtp[0:16, 1280:1536], in_=st.dtp[0:16, 1024:1280], func=AF.Exp,
                     bias=st.dtp[0:16, 1023:1024])
        s.activation(out=st.dtp[0:16, 1536:1792], in_=st.dtp[0:16, 768:1024], func=AF.Exp)
        inc(st.ct, s.activation(out=st.dtp[0:16, 1792:1793], in_=st.dtp[0:16, 1023:1024], func=AF.Exp))
    s_dt3 = st.ct[1]

    @block.vector
    def _(v):
        wt(v, st.ct, s_dt3)
        inc(st.vt, v.tensor_tensor(out=st.dtp[0:16, 1280:1536], in0=st.dtp[0:16, 1280:1536],
                                   in1=st.dtp[0:16, 256:512], op=AL.mult))
    v_dt4 = st.vt[1]

    @block.tensor
    def _(t):
        wt(t, st.vt, v_dt4)
        for js in range(2):
            t.transpose(st.pb[2 + js][:, 0:16], st.dtp[0:16, 256 + js * 128:256 + (js + 1) * 128],
                        st.idf[0:16, 0:16])
            t.transpose(st.pb[4 + js][:, 0:16], st.dtp[0:16, 1280 + js * 128:1280 + (js + 1) * 128],
                        st.idf[0:16, 0:16])
            t.transpose(st.pb[6 + js][:, 0:16], st.dtp[0:16, 1024 + js * 128:1024 + (js + 1) * 128],
                        st.idf[0:16, 0:16])
        inc(st.tt, t.transpose(st.pb[2][0:1, 16:32], st.dtp[0:16, 1792:1793], st.idf[0:16, 0:16]))
    t_dts = st.tt[1]

    @block.vector
    def _(v):
        wt(v, st.tt, t_dts)
        for js in range(2):
            v.tensor_copy(out=st.sS[:, js * 16:(js + 1) * 16], in_=st.pb[2 + js][:, 0:16])
            v.tensor_copy(out=st.sS[:, 32 + js * 16:32 + (js + 1) * 16], in_=st.pb[4 + js][:, 0:16])
            v.tensor_copy(out=st.sS[:, 64 + js * 16:64 + (js + 1) * 16], in_=st.pb[6 + js][:, 0:16])
        inc(st.vt, v.tensor_copy(out=st.srow[0:1, 0:16], in_=st.pb[2][0:1, 16:32]))
    v_dts = st.vt[1]

    @block.tensor
    def _(t):
        wt(t, st.vt, v_dts)
        inc(st.tt, t.matmul(st.pb[3][:, 16:32], st.ones1[:, :], st.srow[0:1, 0:16], start=True, stop=True))
    t_cd = st.tt[1]

    @block.tensor
    def _(t):
        wt(t, st.ct, s_conv)
        inc(st.tt, t.matmul(st.pb[6][0:64, 0:256], st.sel64[:, :],
                            st.xBCc[:, 16 * 256:17 * 256], start=True, stop=True))
    t_cf = st.tt[1]

    @block.vector
    def _(v):
        wt(v, st.tt, t_cf)
        wt(v, st.tt, t_cd)
        v.tensor_copy(out=st.Cf32[:, :], in_=st.pb[6][0:64, 0:256])
        inc(st.vt, v.tensor_copy(out=st.cdB[:, :], in_=st.pb[3][:, 16:32]))
    v_bc = st.vt[1]

    @block.tensor
    def _(t):
        wt(t, st.vt, v_bc)
        t.transpose(st.pb[4][:, 0:64], st.xBCc[0:64, 16 * 256 + 0 * 128:16 * 256 + 1 * 128], st.idf[0:64, 0:64])
        inc(st.tt, t.transpose(st.pb[5][:, 0:64], st.xBCc[0:64, 16 * 256 + 1 * 128:16 * 256 + 2 * 128],
                               st.idf[0:64, 0:64]))
    t_btok = st.tt[1]

    @block.vector
    def _(v):
        wt(v, st.tt, t_btok)
        v.tensor_copy(out=st.btok[:, 0:64], in_=st.pb[4][:, 0:64])
        inc(st.vt, v.tensor_copy(out=st.btok[:, 64:128], in_=st.pb[5][:, 0:64]))
    v_btok = st.vt[1]

    if st.debug and c == 0:
        @block.scalar
        def _(s):
            wt(s, st.vt, v_btok)
            s.dma_start(out=st.dbg["d_dt"], in_=st.dtp[:, :]).then_inc(st.sd[0], dinc(st.sd))

        @block.vector
        def _(v):
            wt(v, st.sd)
            inc(st.vt, v.engine_nop())

    if st.debug and c == 1:
        @block.scalar
        def _(s):
            wt(s, st.vt, v_btok)
            s.dma_start(out=st.dbg["d_dt1"], in_=st.dtp[:, :]).then_inc(st.sd[0], dinc(st.sd))

        @block.vector
        def _(v):
            wt(v, st.sd)
            inc(st.vt, v.engine_nop())

    for h in range(NH):
        xh = st.xBCc[:, h * 256:(h + 1) * 256]

        @block.vector
        def _(v, h=h):
            wt(v, st.vt)
            v.tensor_scalar(out=st.wBh[:, 0:64], in0=st.btok[:, 0:64],
                            scalar1=st.sS[:, 32 + h:32 + h + 1], scalar2=None, op0=AL.mult)
            inc(st.vt, v.tensor_scalar(out=st.wBh[:, 64:128], in0=st.btok[:, 64:128],
                                       scalar1=st.sS[:, 48 + h:48 + h + 1], scalar2=None, op0=AL.mult))
        v_wb = st.vt[1]

        @block.tensor
        def _(t, h=h, xh=xh, v_wb=v_wb):
            wt(t, st.vt, v_wb)
            t.transpose(st.pb[4][:, 0:128], xh[:, 0:128], st.idf[:, :])
            inc(st.tt, t.transpose(st.pb[4][:, 128:256], xh[:, 128:256], st.idf[:, :]))
            BT = st.xBCc[0:64, 16 * 256:17 * 256]
            for js in range(2):
                t.matmul(st.pb[js][:, 0:256], BT[:, js * 128:(js + 1) * 128],
                         st.Cf32[:, :], start=True, stop=True)
            t.matmul(st.pb[2][:, 0:256], st.sel16[:, h * 128:(h + 1) * 128],
                     st.dtp[0:16, 768:1024], start=True, stop=True)
            inc(st.tt, t.matmul(st.pb[3][:, 0:256], st.sel16[:, h * 128:(h + 1) * 128],
                                st.dtp[0:16, 1536:1792], start=True, stop=True))
        t_sd = st.tt[1]

        @block.vector
        def _(v, h=h, t_sd=t_sd):
            wt(v, st.tt, t_sd)
            v.tensor_copy(out=st.xhT[:, 0:128], in_=st.pb[4][:, 0:128])
            inc(st.vt, v.tensor_copy(out=st.xhT[:, 128:256], in_=st.pb[4][:, 128:256]))
        v_xh = st.vt[1]

        for js in range(2):
            @block.vector
            def _(v, h=h, js=js, t_sd=t_sd):
                wt(v, st.tt, t_sd)
                inc(st.vt, v.tensor_scalar(out=st.Eb[:, :], in0=st.pb[2][:, 0:256],
                                           scalar1=st.sS[:, 64 + js * 16 + h:64 + js * 16 + h + 1],
                                           scalar2=0.0, op0=AL.add, op1=AL.min))
            v_d2 = st.vt[1]

            @block.scalar
            def _(s, v_d2=v_d2):
                wt(s, st.vt, v_d2)
                inc(st.ct, s.activation(out=st.Eb[:, :], in_=st.Eb[:, :], func=AF.Exp))
            s_e = st.ct[1]

            @block.vector
            def _(v, h=h, js=js, s_e=s_e):
                wt(v, st.ct, s_e)
                v.scalar_tensor_tensor(out=st.attb[:, js * 256:(js + 1) * 256], in0=st.Eb[:, :],
                                       scalar=st.sS[:, js * 16 + h:js * 16 + h + 1],
                                       in1=st.pb[js][:, 0:256], op0=AL.mult, op1=AL.mult)
                inc(st.vt, v.tensor_tensor(out=st.attb[:, js * 256:(js + 1) * 256],
                                           in0=st.attb[:, js * 256:(js + 1) * 256],
                                           in1=st.msk[:, js * 256:(js + 1) * 256], op=AL.mult))
        v_att = st.vt[1]

        @block.tensor
        def _(t, h=h, v_att=v_att, v_xh=v_xh):
            wt(t, st.vt, v_att)
            for js in range(2):
                t.matmul(st.pb[5][:, 0:256], st.xhT[:, js * 128:(js + 1) * 128],
                         st.attb[:, js * 256:(js + 1) * 256], start=(js == 0), stop=(js == 1))
            for js in range(2):
                t.matmul(st.pb[6][0:64, 0:128], st.wBh[:, js * 64:(js + 1) * 64],
                         st.xhT[:, js * 128:(js + 1) * 128], start=(js == 0), stop=(js == 1))
            inc(st.tt, t.matmul(st.pb[7][:, 0:256], st.PS[0:64, h * 128:(h + 1) * 128],
                                st.Cf32[:, :], start=True, stop=True))
        t_y = st.tt[1]

        @block.vector
        def _(v, h=h, t_y=t_y):
            wt(v, st.tt, t_y)
            v.tensor_copy(out=st.attb[:, 0:256], in_=st.pb[3][:, 0:256])
            v.tensor_tensor(out=st.Eb[:, :], in0=st.pb[7][:, 0:256], in1=st.attb[:, 0:256], op=AL.mult)
            v.tensor_tensor(out=st.Eb[:, :], in0=st.Eb[:, :], in1=st.pb[5][:, 0:256], op=AL.add)
            v.scalar_tensor_tensor(out=st.Eb[:, :], in0=st.xBCc[:, h * 256:(h + 1) * 256],
                                   scalar=st.DB[:, h:h + 1], in1=st.Eb[:, :], op0=AL.mult, op1=AL.add)
            v.tensor_tensor(out=st.gT[:, h * 256:(h + 1) * 256], in0=st.Eb[:, :],
                            in1=st.gT[:, h * 256:(h + 1) * 256], op=AL.mult)
            inc(st.vt, v.scalar_tensor_tensor(out=st.PS[0:64, h * 128:(h + 1) * 128],
                                              in0=st.PS[0:64, h * 128:(h + 1) * 128],
                                              scalar=st.cdB[0:64, h:h + 1],
                                              in1=st.pb[6][0:64, 0:128], op0=AL.mult, op1=AL.add))
    v_scan = st.vt[1]

    for dc in range(16):
        bq = st.q1 if dc % 2 == 0 else st.q2

        @block.scalar
        def _(s, dc=dc, bq=bq, v_scan=v_scan):
            if dc == 0:
                wt(s, st.vt, v_scan)
            if dc >= 2:
                wt(s, st.tt)
            inc(st.ct, s.activation(out=bq[:, 0:256], in_=st.gT[:, dc * 256:(dc + 1) * 256], func=AF.Square))
        s_gsq = st.ct[1]

        @block.tensor
        def _(t, dc=dc, bq=bq, s_gsq=s_gsq):
            wt(t, st.ct, s_gsq)
            inc(st.tt, t.matmul(st.pb[4][0:1, 0:256], st.onesc[:, :], bq[:, 0:256],
                                start=(dc == 0), stop=(dc == 15)))
    t_ss = st.tt[1]

    @block.vector
    def _(v):
        wt(v, st.tt, t_ss)
        inc(st.vt, v.tensor_scalar(out=st.sm1[0:1, 16:272], in0=st.pb[4][0:1, 0:256],
                                   scalar1=1.0 / DI, scalar2=RMS_EPS, op0=AL.mult, op1=AL.add))
    v_rp = st.vt[1]

    @block.scalar
    def _(s):
        wt(s, st.vt, v_rp)
        s.activation(out=st.srow[0:1, 0:256], in_=st.sm1[0:1, 16:272], func=AF.Ln)
        inc(st.ct, s.activation(out=st.srow[0:1, 0:256], in_=st.srow[0:1, 0:256], func=AF.Exp, scale=-0.5))
    s_rs = st.ct[1]

    @block.vector
    def _(v):
        wt(v, st.ct, s_rs)
        y = st.srow[0:1, 0:256]
        a = st.sm1[0:1, 16:272]
        w2 = st.sm1[0:1, 272:528]
        for _i in range(2):
            v.tensor_tensor(out=w2, in0=y, in1=y, op=AL.mult)
            v.tensor_tensor(out=w2, in0=w2, in1=a, op=AL.mult)
            v.tensor_scalar(out=w2, in0=w2, scalar1=-0.5, scalar2=1.5, op0=AL.mult, op1=AL.add)
            v.tensor_tensor(out=y, in0=y, in1=w2, op=AL.mult)
        inc(st.vt, v.engine_nop())
    v_rms = st.vt[1]

    @block.tensor
    def _(t):
        wt(t, st.vt, v_rms)
        inc(st.tt, t.matmul(st.pb[3][:, 0:256], st.ones1[:, :], st.srow[0:1, 0:256], start=True, stop=True))
    t_sb = st.tt[1]

    for dcq in range(4):
        slot_g = dcq % 2

        @block.vector
        def _(v, dcq=dcq, slot_g=slot_g, t_sb=t_sb):
            if dcq == 0:
                wt(v, st.tt, t_sb)
            if dcq >= 2:
                wt(v, st.opc, c * 16 + (dcq - 1) * 4)
            for dcl in range(4):
                dc = dcq * 4 + dcl
                v.tensor_scalar(out=st.ghl[:, slot_g * 1024 + dcl * 256: slot_g * 1024 + (dcl + 1) * 256],
                                in0=st.gT[:, dc * 256:(dc + 1) * 256],
                                scalar1=st.rwT[:, dc:dc + 1], scalar2=None, op0=AL.mult)
            for dcl in range(4):
                v.tensor_tensor(out=st.ghl[:, slot_g * 1024 + dcl * 256: slot_g * 1024 + (dcl + 1) * 256],
                                in0=st.ghl[:, slot_g * 1024 + dcl * 256: slot_g * 1024 + (dcl + 1) * 256],
                                in1=st.pb[3][:, 0:256], op=AL.mult)
            inc(st.vt, v.engine_nop())
        v_spl = st.vt[1]

        for oc in range(2):
            for dh in range(2):
                unit = c * 16 + dcq * 4 + oc * 2 + dh
                slot_w = unit % 2
                r0 = dcq * 512 + dh * 256

                @block.sync
                def _(s, oc=oc, unit=unit, slot_w=slot_w, r0=r0):
                    if unit >= 2:
                        wt(s, st.opc, unit - 1)
                    s.dma_start(out=st.wop[:, slot_w * 1024: slot_w * 1024 + 1024],
                                in_=st.s_opW[r0:r0 + 256, oc * 512:(oc + 1) * 512]
                                .rearrange("(dc p) o -> p dc o", p=128)).then_inc(st.sy[0], dinc(st.sy))
                w_done = st.sy[1]

                @block.tensor
                def _(t, dcq=dcq, oc=oc, dh=dh, unit=unit, slot_w=slot_w, slot_g=slot_g,
                      w_done=w_done, v_spl=v_spl):
                    wt(t, st.sy, w_done)
                    wt(t, st.vt, v_spl)
                    last = None
                    for i in range(2):
                        dc = dcq * 4 + dh * 2 + i
                        dcl = dh * 2 + i
                        wh = st.wop[:, slot_w * 1024 + i * 512: slot_w * 1024 + (i + 1) * 512]
                        for tsub in range(2):
                            gh = st.ghl[:, slot_g * 1024 + dcl * 256 + tsub * 128:
                                        slot_g * 1024 + dcl * 256 + (tsub + 1) * 128]
                            bnk = st.pb[4 + 2 * oc + tsub]
                            last = t.matmul(bnk[:, 0:512], gh, wh, start=(dc == 0), stop=(dc == 15))
                    inc(st.opc, last)
    t_op = st.opc[1]

    @block.vector
    def _(v):
        wt(v, st.opc, t_op)
        for oc in range(2):
            for tsub in range(2):
                xa = st.X[:, (2 * c + tsub) * D + oc * 512:(2 * c + tsub) * D + (oc + 1) * 512]
                v.tensor_tensor(out=xa, in0=xa, in1=st.pb[4 + 2 * oc + tsub][:, 0:512], op=AL.add)
        inc(st.vt, v.engine_nop())
    v_op = st.vt[1]

    @block.vector
    def _(v):
        wt(v, st.vt, v_op)
        wt(v, st.ct)
        for tsub in range(2):
            mxfp4_quant_emit(v, st, st.xn0c[:, tsub * D:(tsub + 1) * D], 128, D, -8, 5,
                             st.sm2[:, 0:1].broadcast_to([128, 32]),
                             st.sm2[:, 1:2].broadcast_to([128, 32]),
                             st.uc[:, tsub * D:(tsub + 1) * D])
        inc(st.vt, v.engine_nop())
    v_xq = st.vt[1]

    if st.debug and c == 0:
        @block.scalar
        def _(s):
            wt(s, st.vt, v_xq)
            s.dma_start(out=st.dbg["d_xq"], in_=st.uc[:, :]).then_inc(st.sd[0], dinc(st.sd))

        @block.vector
        def _(v):
            wt(v, st.sd)
            inc(st.vt, v.engine_nop())

    for i in range(16):
        tsub, dc = divmod(i, 8)
        bnk = st.pb[4 + (i % 2)]

        @block.tensor
        def _(t, i=i, tsub=tsub, dc=dc, bnk=bnk, v_xq=v_xq):
            wt(t, st.vt)
            inc(st.tt, t.transpose(bnk[:, 0:128],
                                   st.uc[:, tsub * 1024 + dc * 128: tsub * 1024 + (dc + 1) * 128],
                                   st.idf[:, :]))
        t_tp = st.tt[1]

        @block.vector
        def _(v, tsub=tsub, dc=dc, bnk=bnk, t_tp=t_tp):
            wt(v, st.tt, t_tp)
            inc(st.vt, v.tensor_copy(out=st.xqT[:, dc * 256 + tsub * 128: dc * 256 + (tsub + 1) * 128],
                                     in_=bnk[:, 0:128]))
    v_xqT = st.vt[1]

    for oc in range(2):
        for dh in range(2):
            unit = c * 4 + oc * 2 + dh
            r0 = dh * 512

            @block.sync
            def _(s, oc=oc, dh=dh, unit=unit, r0=r0):
                if unit >= 1:
                    wt(s, st.fqc, unit)
                s.dma_start(out=st.wfq[:, 0:2048],
                            in_=st.s_fqW[r0:r0 + 512, oc * 512:(oc + 1) * 512]
                            .rearrange("(dc p) o -> p dc o", p=128)).then_inc(st.sy[0], dinc(st.sy))
            w_done = st.sy[1]

            @block.tensor
            def _(t, oc=oc, dh=dh, unit=unit, w_done=w_done, vcur=st.vt[1]):
                wt(t, st.sy, w_done)
                wt(t, st.vt, vcur)
                last = None
                for i in range(4):
                    dc = dh * 4 + i
                    wh = st.wfq[:, i * 512:(i + 1) * 512]
                    for tsub in range(2):
                        xh_ = st.xqT[:, dc * 256 + tsub * 128: dc * 256 + (tsub + 1) * 128]
                        bnk = st.pb[tsub]
                        last = t.matmul(bnk[:, 0:512], xh_, wh, start=(dc == 0), stop=(dc == 7))
                inc(st.fqc, last)
        t_fq = st.fqc[1]

        @block.vector
        def _(v, oc=oc, t_fq=t_fq):
            wt(v, st.fqc, t_fq)
            for tsub in range(2):
                xa = st.X[:, (2 * c + tsub) * D + oc * 512:(2 * c + tsub) * D + (oc + 1) * 512]
                v.tensor_tensor(out=st.q1[:, 0:512], in0=st.pb[tsub][:, 0:512],
                                in1=st.fbb[:, oc * 512:(oc + 1) * 512], op=AL.add)
                v.tensor_tensor(out=xa, in0=xa, in1=st.q1[:, 0:512], op=AL.add)
            inc(st.vt, v.engine_nop())




def emit_p2(st):
    nc, block = st.nc, st.block

    @block.vector
    def _(v):
        wt(v, st.ct)
        wt(v, st.tt)
        wt(v, st.ipc)
        wt(v, st.opc)
        wt(v, st.fqc)
        inc(st.vt, v.engine_nop())

    @block.tensor
    def _(t):
        wt(t, st.vt)
        wt(t, st.ct)
        inc(st.tt, t.matmul(st.pb[7][0:1, 4:5], st.ones1[:, 0:1], st.ones1[0:1, 0:1], start=True, stop=True))

    @block.scalar
    def _(s):
        wt(s, st.vt)
        wt(s, st.tt)
        inc(st.ct, s.activation(out=st.sm1[0:1, 8:9], in_=st.sm1[0:1, 0:1], func=AF.Identity))

    @block.sync
    def _(s):
        wt(s, st.vt)
        with nc.allow_non_contiguous_dma(reason="small L1 constants"):
            s.dma_start(out=st.rotT[:, :], in_=st.rot.rearrange("h d -> d h"),
                        allow_slow_non_contiguous=True).then_inc(st.sy[0], dinc(st.sy))
            s.dma_start(out=st.entT[:, :], in_=st.ent.rearrange("h d -> d h"),
                        allow_slow_non_contiguous=True).then_inc(st.sy[0], dinc(st.sy))
            s.dma_start(out=st.dbT[:, 0:8], in_=st.d_bk.rearrange("(a p) -> p a", p=128),
                        allow_slow_non_contiguous=True).then_inc(st.sy[0], dinc(st.sy))
            s.dma_start(out=st.dbT[:, 8:16], in_=st.d_bv.rearrange("(a p) -> p a", p=128),
                        allow_slow_non_contiguous=True).then_inc(st.sy[0], dinc(st.sy))
    st.L1LOADS = st.sy[1]

    @block.vector
    def _(v):
        wt(v, st.sy, st.L1LOADS)
        inc(st.vt, v.tensor_scalar_mul(st.rotT[:, :], st.rotT[:, :], 0.125))

    for tt2 in range(16):
        xsl = st.X[:, tt2 * D:(tt2 + 1) * D]
        emit_ln(st, xsl, st.t_tok[:, :], None, None, D, dump=st.ldmp1)

        for dc in range(8):
            bnk = st.pb[4 + (dc % 2)]

            @block.tensor
            def _(t, tt2=tt2, dc=dc, bnk=bnk):
                wt(t, st.vt)
                inc(st.tt, t.transpose(bnk[:, 0:128], st.t_tok[:, dc * 128:(dc + 1) * 128],
                                       st.idf[:, :]))
            t_tp = st.tt[1]

            @block.vector
            def _(v, tt2=tt2, dc=dc, bnk=bnk, t_tp=t_tp):
                wt(v, st.tt, t_tp)
                inc(st.vt, v.tensor_copy(out=st.xn1T[:, dc * 2048 + tt2 * 128: dc * 2048 + (tt2 + 1) * 128],
                                         in_=bnk[:, 0:128]))
    st.v_xn1 = st.vt[1]

    if st.debug:
        @block.scalar
        def _(s):
            wt(s, st.vt, st.v_xn1)
            s.dma_start(out=st.dbg["d_xn1h"], in_=st.xn1T[:, 0:8192]).then_inc(st.sd[0], dinc(st.sd))

        @block.vector
        def _(v):
            wt(v, st.sd)
            inc(st.vt, v.engine_nop())



def emit_dpath(st):
    nc, block = st.nc, st.block
    dbrow = st.AR[0:1, 18432:19456]
    pT = st.AR[:, 19584:19600]
    bbv = st.AR[:, 19600:19728]
    st.aD2 = st.AR[0:64, 19728:19984]

    @block.sync
    def _(s):
        wt(s, st.vt)
        s.dma_start(out=st.t_tok[0:16, :], in_=st.sq_in[0, :, :]).then_inc(st.sy[0], dinc(st.sy))
        with nc.allow_non_contiguous_dma(reason="bias row"):
            s.dma_start(out=dbrow[:, :], in_=st.d_bq.unsqueeze(0)).then_inc(st.sy[0], dinc(st.sy))
    sq_ld = st.sy[1]

    @block.tensor
    def _(t):
        wt(t, st.sy, sq_ld)
        wt(t, st.vt)
        for dc in range(8):
            t.transpose(st.pb[4][:, dc * 16:(dc + 1) * 16], st.t_tok[0:16, dc * 128:(dc + 1) * 128],
                        st.idf[0:16, 0:16])
        inc(st.tt, t.matmul(st.pb[2][0:16, 0:512], st.ones1[0:1, 0:16], dbrow[:, 0:512],
                            start=True, stop=True))
        inc(st.tt, t.matmul(st.pb[3][0:16, 0:512], st.ones1[0:1, 0:16], dbrow[:, 512:1024],
                            start=True, stop=True))
    t_sq = st.tt[1]

    @block.vector
    def _(v):
        wt(v, st.tt, t_sq)
        inc(st.vt, v.tensor_copy(out=st.sqT[:, :], in_=st.pb[4][:, 0:128]))
    v_sq = st.vt[1]

    for dc in range(8):
        sl_w = (dc % 2) * 1024

        @block.sync
        def _(s, dc=dc, sl_w=sl_w):
            wt(s, st.vt)
            s.dma_start(out=st.wnat[:, sl_w:sl_w + 1024], in_=st.d_Wq[dc * 128:(dc + 1) * 128, :]).then_inc(st.sy[0], dinc(st.sy))
        w_ld = st.sy[1]

        @block.tensor
        def _(t, dc=dc, sl_w=sl_w, w_ld=w_ld, v_sq=v_sq):
            wt(t, st.sy, w_ld)
            wt(t, st.vt)
            last = None
            for oc in range(2):
                sh = st.sqT[:, dc * 16:(dc + 1) * 16]
                last = t.matmul(st.pb[oc][0:16, 0:512], sh,
                                st.wnat[:, sl_w + oc * 512:sl_w + (oc + 1) * 512],
                                start=(dc == 0), stop=(dc == 7))
            inc(st.tt, last)

        @block.vector
        def _(v):
            wt(v, st.tt)
            inc(st.vt, v.engine_nop())
    t_qh = st.tt[1]

    @block.vector
    def _(v):
        wt(v, st.tt, t_qh)
        for oc in range(2):
            v.tensor_scalar(out=st.qhC[0:16, oc * 512:(oc + 1) * 512], in0=st.pb[oc][0:16, 0:512],
                            scalar1=0.125, scalar2=None, op0=AL.mult)
            v.scalar_tensor_tensor(out=st.qhC[0:16, oc * 512:(oc + 1) * 512],
                                   in0=st.pb[2 + oc][0:16, 0:512], scalar=0.125,
                                   in1=st.qhC[0:16, oc * 512:(oc + 1) * 512], op0=AL.mult, op1=AL.add)
        inc(st.vt, v.engine_nop())
    v_qh = st.vt[1]

    if st.debug:
        @block.scalar
        def _(s):
            wt(s, st.vt, v_qh)
            s.dma_start(out=st.dbg["d_dqh"], in_=st.qhC[0:16, :]).then_inc(st.sd[0], dinc(st.sd))

        @block.vector
        def _(v):
            wt(v, st.sd)
            inc(st.vt, v.engine_nop())

    @block.tensor
    def _(t):
        wt(t, st.vt, v_qh)
        for dc in range(8):
            inc(st.tt, t.transpose(st.pb[4][:, dc * 16:(dc + 1) * 16],
                                   st.qhC[0:16, dc * 128:(dc + 1) * 128], st.idf[0:16, 0:16]))
    t_qT = st.tt[1]

    @block.vector
    def _(v):
        wt(v, st.tt, t_qT)
        inc(st.vt, v.tensor_copy(out=st.sqT[:, :], in_=st.pb[4][:, 0:128]))
    v_qT = st.vt[1]

    for p in range(8):
        @block.sync
        def _(s, p=p):
            wt(s, st.vt)
            wt(s, st.tt)
            with nc.allow_non_contiguous_dma(reason="column slice of d_Wk"):
                s.dma_start(out=st.wnat[:, 0:1024].rearrange("p (dc o) -> p dc o", o=128),
                            in_=st.d_Wk[:, p * 128:(p + 1) * 128].rearrange("(dc p) o -> p dc o", p=128)).then_inc(st.sy[0], dinc(st.sy))
        w_ld = st.sy[1]

        for ncol in range(4):
            bnk = st.pb[ncol % 2]

            @block.tensor
            def _(t, ncol=ncol, bnk=bnk, w_ld=w_ld):
                wt(t, st.sy, w_ld)
                wt(t, st.vt)
                if ncol >= 2:
                    wt(t, st.ct)
                last = None
                for dc in range(8):
                    wh = st.wnat[:, dc * 128:(dc + 1) * 128]
                    xh = st.xn1T[:, dc * 2048 + ncol * 512: dc * 2048 + (ncol + 1) * 512]
                    last = t.matmul(bnk[:, 0:512], wh, xh, start=(dc == 0), stop=(dc == 7))
                inc(st.tt, last)
            t_kh = st.tt[1]

            @block.scalar
            def _(s, p=p, ncol=ncol, bnk=bnk, t_kh=t_kh):
                wt(s, st.tt, t_kh)
                inc(st.ct, s.activation(out=st.khT[:, ncol * 512:(ncol + 1) * 512], in_=bnk[:, 0:512],
                                        func=AF.Identity, bias=st.dbT[:, p:p + 1]))
        s_kh = st.ct[1]

        @block.vector
        def _(v, s_kh=s_kh):
            wt(v, st.ct, s_kh)
            inc(st.vt, v.engine_nop())
        v_khs = st.vt[1]

        if st.debug and p == 0:
            @block.scalar
            def _(s, v_khs=v_khs):
                wt(s, st.vt, v_khs)
                s.dma_start(out=st.dbg["d_dkh"], in_=st.khT[:, :]).then_inc(st.sd[0], dinc(st.sd))

            @block.vector
            def _(v):
                wt(v, st.sd)
                inc(st.vt, v.engine_nop())

        @block.sync
        def _(s, p=p, v_khs=v_khs):
            wt(s, st.vt, v_khs)
            with nc.allow_non_contiguous_dma(reason="column slice of d_Wv"):
                s.dma_start(out=st.wnat[:, 0:1024].rearrange("p (dc o) -> p dc o", o=128),
                            in_=st.d_Wv[:, p * 128:(p + 1) * 128].rearrange("(dc p) o -> p dc o", p=128)).then_inc(st.sy[0], dinc(st.sy))
        wv_ld = st.sy[1]

        @block.tensor
        def _(t, p=p, wv_ld=wv_ld):
            wt(t, st.sy, wv_ld)
            wt(t, st.vt)
            inc(st.tt, t.transpose(st.pb[2][0:1, 0:128], st.dbT[:, 8 + p:9 + p], st.idf[:, :]))
        t_bv = st.tt[1]

        @block.vector
        def _(v, t_bv=t_bv):
            wt(v, st.tt, t_bv)
            inc(st.vt, v.tensor_copy(out=st.espr[:, 0:128], in_=st.pb[2][0:1, 0:128]))
        v_bvr = st.vt[1]

        @block.tensor
        def _(t, v_bvr=v_bvr):
            wt(t, st.vt, v_bvr)
            inc(st.tt, t.matmul(st.pb[2][:, 0:128], st.ones1[:, :], st.espr[:, 0:128], start=True, stop=True))
        t_bb = st.tt[1]

        @block.vector
        def _(v, t_bb=t_bb):
            wt(v, st.tt, t_bb)
            inc(st.vt, v.tensor_copy(out=bbv[:, :], in_=st.pb[2][:, 0:128]))
        v_bb = st.vt[1]

        for tt2 in range(16):
            bnk = st.pb[(tt2 % 2)]

            @block.tensor
            def _(t, tt2=tt2, bnk=bnk, v_bb=v_bb):
                wt(t, st.vt)
                last = None
                for dc in range(8):
                    sh = st.xn1T[:, dc * 2048 + tt2 * 128: dc * 2048 + (tt2 + 1) * 128]
                    wh = st.wnat[:, dc * 128:(dc + 1) * 128]
                    last = t.matmul(bnk[:, 0:128], sh, wh, start=(dc == 0), stop=(dc == 7))
                inc(st.tt, last)
            t_vh = st.tt[1]

            @block.vector
            def _(v, tt2=tt2, bnk=bnk, t_vh=t_vh):
                wt(v, st.tt, t_vh)
                inc(st.vt, v.tensor_tensor(out=st.vhT[:, tt2 * 128:(tt2 + 1) * 128], in0=bnk[:, 0:128],
                                           in1=bbv[:, :], op=AL.add))
        v_vh = st.vt[1]

        if st.debug and p == 0:
            @block.scalar
            def _(s, v_vh=v_vh):
                wt(s, st.vt, v_vh)
                s.dma_start(out=st.dbg["d_dvh"], in_=st.vhT[:, :]).then_inc(st.sd[0], dinc(st.sd))

            @block.vector
            def _(v):
                wt(v, st.sd)
                inc(st.vt, v.engine_nop())

        for hh in range(2):
            h = 2 * p + hh
            b0 = hh * 64

            for ncol in range(4):
                bnk = st.pb[2 + (ncol % 2)]

                @block.tensor
                def _(t, h=h, b0=b0, ncol=ncol, bnk=bnk, v_vh=v_vh):
                    wt(t, st.vt)
                    qh_ = st.sqT[b0:b0 + 64, (h // 2) * 16:(h // 2) * 16 + 16]
                    kh_ = st.khT[b0:b0 + 64, ncol * 512:(ncol + 1) * 512]
                    inc(st.tt, t.matmul(bnk[0:16, 0:512], qh_, kh_, start=True, stop=True))
                t_s = st.tt[1]

                @block.vector
                def _(v, ncol=ncol, bnk=bnk, t_s=t_s):
                    wt(v, st.tt, t_s)
                    inc(st.vt, v.tensor_copy(out=st.Sd[:, ncol * 512:(ncol + 1) * 512], in_=bnk[0:16, 0:512]))

            @block.vector
            def _(v):
                wt(v, st.vt)
                inc(st.vt, v.reduce_max(out=st.sm2[0:16, 8:9], in_=st.Sd[:, :], axis=AX.X, negate=True))
            v_m = st.vt[1]

            @block.scalar
            def _(s, v_m=v_m):
                wt(s, st.vt, v_m)
                s.activation(out=st.Sd[:, :], in_=st.Sd[:, :], func=AF.Exp,
                             bias=st.sm2[0:16, 8:9], accum_out=st.sm2[0:16, 9:10])
                s.activation(out=st.sm2[0:16, 10:11], in_=st.sm2[0:16, 9:10], func=AF.Ln)
                inc(st.ct, s.activation(out=st.sm2[0:16, 11:12], in_=st.sm2[0:16, 10:11],
                                        func=AF.Exp, scale=-1.0))
            s_r = st.ct[1]

            @block.vector
            def _(v, s_r=s_r):
                wt(v, st.ct, s_r)
                y = st.sm2[0:16, 11:12]
                for _i in range(2):
                    inc(st.vt, v.tensor_tensor(out=st.sm2[0:16, 12:13], in0=y, in1=st.sm2[0:16, 9:10], op=AL.mult))
                    wt(v, st.vt)
                    inc(st.vt, v.tensor_scalar(out=st.sm2[0:16, 12:13], in0=st.sm2[0:16, 12:13],
                                               scalar1=-1.0, scalar2=2.0, op0=AL.mult, op1=AL.add))
                    wt(v, st.vt)
                    inc(st.vt, v.tensor_tensor(out=y, in0=y, in1=st.sm2[0:16, 12:13], op=AL.mult))
                    wt(v, st.vt)
                inc(st.vt, v.tensor_scalar(out=st.Sd[:, :], in0=st.Sd[:, :], scalar1=y,
                                           scalar2=None, op0=AL.mult))
            v_P = st.vt[1]

            if st.debug and h == 0:
                @block.scalar
                def _(s, v_P=v_P):
                    wt(s, st.vt, v_P)
                    s.dma_start(out=st.dbg["d_dS"], in_=st.Sd[:, :]).then_inc(st.sd[0], dinc(st.sd))

                @block.vector
                def _(v):
                    wt(v, st.sd)
                    inc(st.vt, v.engine_nop())

            for jt in range(16):
                bnk = st.pb[4 + (jt % 2)]

                @block.tensor
                def _(t, jt=jt, bnk=bnk, v_P=v_P):
                    wt(t, st.vt)
                    inc(st.tt, t.transpose(bnk[:, 0:16], st.Sd[:, jt * 128:(jt + 1) * 128],
                                           st.idf[0:16, 0:16]))
                t_pt = st.tt[1]

                @block.vector
                def _(v, bnk=bnk, t_pt=t_pt):
                    wt(v, st.tt, t_pt)
                    inc(st.vt, v.tensor_copy(out=pT[:, :], in_=bnk[:, 0:16]))
                v_pt = st.vt[1]

                @block.tensor
                def _(t, b0=b0, jt=jt, v_pt=v_pt):
                    wt(t, st.vt, v_pt)
                    inc(st.tt, t.matmul(st.pb[6][0:64, 0:16], st.vhT[:, jt * 128 + b0: jt * 128 + b0 + 64],
                                        pT[:, :], start=(jt == 0), stop=(jt == 15)))
            t_av = st.tt[1]

            @block.vector
            def _(v, h=h, t_av=t_av):
                wt(v, st.tt, t_av)
                inc(st.vt, v.tensor_copy(out=st.aD2[:, h * 16:(h + 1) * 16], in_=st.pb[6][0:64, 0:16]))

    if st.debug:
        @block.scalar
        def _(s):
            wt(s, st.vt)
            s.dma_start(out=st.dbg["d_daD2"], in_=st.aD2[:, :]).then_inc(st.sd[0], dinc(st.sd))

        @block.vector
        def _(v):
            wt(v, st.sd)
            inc(st.vt, v.engine_nop())

    for h in range(16):
        @block.sync
        def _(s, h=h):
            wt(s, st.vt)
            s.dma_start(out=st.wnat[0:64, 0:1024], in_=st.d_Wo[h * 64:(h + 1) * 64, :]).then_inc(st.sy[0], dinc(st.sy))
        w_ld = st.sy[1]

        @block.tensor
        def _(t, h=h, w_ld=w_ld):
            wt(t, st.sy, w_ld)
            wt(t, st.vt)
            last = None
            for oc in range(2):
                last = t.matmul(st.pb[oc][0:16, 0:512], st.aD2[:, h * 16:(h + 1) * 16],
                                st.wnat[0:64, oc * 512:(oc + 1) * 512],
                                start=(h == 0), stop=(h == 15))
            inc(st.tt, last)
        @block.vector
        def _(v):
            wt(v, st.tt)
            inc(st.vt, v.engine_nop())
    t_o = st.tt[1]

    @block.sync
    def _(s):
        wt(s, st.vt)
        with nc.allow_non_contiguous_dma(reason="bias row"):
            s.dma_start(out=dbrow[:, :], in_=st.d_bo.unsqueeze(0)).then_inc(st.sy[0], dinc(st.sy))
        s.dma_start(out=st.AR[0:16, 20480:21504], in_=st.sq_in[0, :, :]).then_inc(st.sy[0], dinc(st.sy))
    o_ld = st.sy[1]

    @block.tensor
    def _(t, o_ld=o_ld):
        wt(t, st.sy, o_ld)
        t.matmul(st.pb[2][0:16, 0:512], st.ones1[0:1, 0:16], dbrow[:, 0:512], start=True, stop=True)
        inc(st.tt, t.matmul(st.pb[3][0:16, 0:512], st.ones1[0:1, 0:16], dbrow[:, 512:1024],
                            start=True, stop=True))
    t_ob = st.tt[1]

    @block.vector
    def _(v, t_ob=t_ob):
        wt(v, st.tt, t_ob)
        sq2 = st.AR[0:16, 20480:21504]
        for oc in range(2):
            osl = st.o_tok[:, oc * 512:(oc + 1) * 512]
            v.tensor_copy(out=osl, in_=st.pb[oc][0:16, 0:512])
            v.tensor_tensor(out=osl, in0=osl, in1=st.pb[2 + oc][0:16, 0:512], op=AL.add)
            v.tensor_tensor(out=osl, in0=osl, in1=sq2[:, oc * 512:(oc + 1) * 512], op=AL.add)
        inc(st.vt, v.engine_nop())
    st.v_o = st.vt[1]

    if st.debug:
        @block.scalar
        def _(s):
            wt(s, st.vt, st.v_o)
            s.dma_start(out=st.dbg["d_o"], in_=st.o_tok[:, :]).then_inc(st.sd[0], dinc(st.sd))

        @block.vector
        def _(v):
            wt(v, st.sd)
            inc(st.vt, v.engine_nop())


DBGH = 0


def emit_p4(st):
    nc, block = st.nc, st.block
    AR = st.AR
    oT = AR[:, 26768:26896]
    uTe, gTe = st.uTe, st.gTe
    espr = st.espr
    ew_f = AR[:, 27664:28688]
    dd = AR[:, 29712:29728]

    @block.sync
    def _(s):
        wt(s, st.vt, st.v_o)
        s.dma_start(out=st.ebnat[:, 0:4096], in_=st.eb1[:, :]).then_inc(st.sy[0], dinc(st.sy))
        s.dma_start(out=st.AR[0:16, 16384:17408], in_=st.elng[:, :]).then_inc(st.sy[0], dinc(st.sy))
        s.dma_start(out=st.AR[0:16, 17408:18432], in_=st.elnb[:, :]).then_inc(st.sy[0], dinc(st.sy))
    p4_ld = st.sy[1]

    hrow = AR[0:16, 18432:19456]
    elg = st.AR[0:16, 16384:17408]
    elb = st.AR[0:16, 17408:18432]

    @block.scalar
    def _(s):
        wt(s, st.sy, p4_ld)
        inc(st.ct, s.activation(out=hrow[:, :], in_=st.o_tok[:, :], func=AF.Identity,
                                accum_out=st.sm2[0:16, 8:9]))
    s_su = st.ct[1]

    @block.vector
    def _(v):
        wt(v, st.ct, s_su)
        inc(st.vt, v.tensor_scalar_mul(st.sm2[0:16, 8:9], st.sm2[0:16, 8:9], -1.0 / 1024))
    v_mu = st.vt[1]

    @block.scalar
    def _(s):
        wt(s, st.vt, v_mu)
        inc(st.ct, s.activation(out=hrow[:, :], in_=st.o_tok[:, :], func=AF.Square,
                                bias=st.sm2[0:16, 8:9], accum_out=st.sm2[0:16, 9:10]))
    s_sq = st.ct[1]

    @block.vector
    def _(v):
        wt(v, st.ct, s_sq)
        inc(st.vt, v.tensor_scalar(out=st.sm2[0:16, 10:11], in0=st.sm2[0:16, 9:10],
                                   scalar1=1.0 / 1024, scalar2=LN_EPS, op0=AL.mult, op1=AL.add))
    v_v = st.vt[1]

    @block.scalar
    def _(s):
        wt(s, st.vt, v_v)
        s.activation(out=st.sm2[0:16, 11:12], in_=st.sm2[0:16, 10:11], func=AF.Ln)
        inc(st.ct, s.activation(out=st.sm2[0:16, 11:12], in_=st.sm2[0:16, 11:12], func=AF.Exp, scale=-0.5))
    s_sd = st.ct[1]

    @block.vector
    def _(v):
        wt(v, st.ct, s_sd)
        y = st.sm2[0:16, 11:12]
        for _i in range(2):
            inc(st.vt, v.tensor_tensor(out=st.sm2[0:16, 12:13], in0=y, in1=y, op=AL.mult))
            wt(v, st.vt)
            inc(st.vt, v.tensor_tensor(out=st.sm2[0:16, 12:13], in0=st.sm2[0:16, 12:13],
                                       in1=st.sm2[0:16, 10:11], op=AL.mult))
            wt(v, st.vt)
            inc(st.vt, v.tensor_scalar(out=st.sm2[0:16, 12:13], in0=st.sm2[0:16, 12:13],
                                       scalar1=-0.5, scalar2=1.5, op0=AL.mult, op1=AL.add))
            wt(v, st.vt)
            inc(st.vt, v.tensor_tensor(out=y, in0=y, in1=st.sm2[0:16, 12:13], op=AL.mult))
            wt(v, st.vt)
        v.tensor_scalar(out=hrow[:, :], in0=st.o_tok[:, :], scalar1=st.sm2[0:16, 8:9],
                        scalar2=y, op0=AL.add, op1=AL.mult)
        v.tensor_tensor(out=hrow[:, :], in0=hrow[:, :], in1=elg[:, :], op=AL.mult)
        inc(st.vt, v.tensor_tensor(out=hrow[:, :], in0=hrow[:, :], in1=elb[:, :], op=AL.add))
    v_h = st.vt[1]

    if st.debug:
        @block.scalar
        def _(s, v_h=v_h):
            wt(s, st.vt, v_h)
            s.dma_start(out=st.dbg["d_eh"], in_=hrow[:, :]).then_inc(st.sd[0], dinc(st.sd))

        @block.vector
        def _(v):
            wt(v, st.sd)
            inc(st.vt, v.engine_nop())

    @block.tensor
    def _(t):
        wt(t, st.vt, v_h)
        for dc in range(8):
            t.transpose(st.pb[6][:, dc * 16:(dc + 1) * 16], hrow[:, dc * 128:(dc + 1) * 128],
                        st.idf[0:16, 0:16])
        for dc in range(8):
            inc(st.tt, t.transpose(st.pb[7][:, dc * 16:(dc + 1) * 16],
                                   st.o_tok[:, dc * 128:(dc + 1) * 128], st.idf[0:16, 0:16]))
    t_h = st.tt[1]

    @block.vector
    def _(v):
        wt(v, st.tt, t_h)
        inc(st.vt, v.tensor_copy(out=st.hT[:, :], in_=st.pb[6][:, 0:128]))
        inc(st.vt, v.tensor_copy(out=oT[:, :], in_=st.pb[7][:, 0:128]))
    v_hs = st.vt[1]

    for fc2 in range(16):
        @block.tensor
        def _(t, fc2=fc2):
            wt(t, st.vt)
            t.transpose(st.pb[6][:, 0:16], st.ebnat[:, (2 * fc2) * 128:(2 * fc2 + 1) * 128],
                        st.idf[0:16, 0:16])
            inc(st.tt, t.transpose(st.pb[7][:, 0:16], st.ebnat[:, (2 * fc2 + 1) * 128:(2 * fc2 + 2) * 128],
                                   st.idf[0:16, 0:16]))
        t_p = st.tt[1]

        @block.vector
        def _(v, fc2=fc2, t_p=t_p):
            wt(v, st.tt, t_p)
            v.tensor_copy(out=st.b1T[:, (2 * fc2) * 16:(2 * fc2 + 1) * 16], in_=st.pb[6][:, 0:16])
            inc(st.vt, v.tensor_copy(out=st.b1T[:, (2 * fc2 + 1) * 16:(2 * fc2 + 2) * 16],
                                     in_=st.pb[7][:, 0:16]))
    v_b1 = st.vt[1]

    if st.debug:
        @block.scalar
        def _(s):
            wt(s, st.vt, v_b1)
            s.dma_start(out=st.dbg["d_eb1T"], in_=st.b1T[:, :]).then_inc(st.sd[0], dinc(st.sd))

        @block.vector
        def _(v):
            wt(v, st.sd)
            inc(st.vt, v.engine_nop())

    @block.sync
    def _(s):
        wt(s, st.vt, v_b1)
        s.dma_start(out=st.ebnat[:, 0:4096], in_=st.ebg[:, :]).then_inc(st.sy[0], dinc(st.sy))
    bg_ld = st.sy[1]

    for fc2 in range(16):
        @block.tensor
        def _(t, fc2=fc2, bg_ld=bg_ld):
            wt(t, st.sy, bg_ld)
            wt(t, st.vt)
            t.transpose(st.pb[6][:, 0:16], st.ebnat[:, (2 * fc2) * 128:(2 * fc2 + 1) * 128],
                        st.idf[0:16, 0:16])
            inc(st.tt, t.transpose(st.pb[7][:, 0:16], st.ebnat[:, (2 * fc2 + 1) * 128:(2 * fc2 + 2) * 128],
                                   st.idf[0:16, 0:16]))
        t_p = st.tt[1]

        @block.vector
        def _(v, fc2=fc2, t_p=t_p):
            wt(v, st.tt, t_p)
            v.tensor_copy(out=st.bgT[:, (2 * fc2) * 16:(2 * fc2 + 1) * 16], in_=st.pb[6][:, 0:16])
            inc(st.vt, v.tensor_copy(out=st.bgT[:, (2 * fc2 + 1) * 16:(2 * fc2 + 2) * 16],
                                     in_=st.pb[7][:, 0:16]))
    v_bg = st.vt[1]

    for h in range(16):
        emit_attn_head(st, h)
        emit_expert(st, h, uTe, gTe, espr, ew_f, oT)

    @block.sync
    def _(s):
        wt(s, st.vt)
        s.dma_start(out=st.eb2n[:, :], in_=st.eb2[:, :]).then_inc(st.sy[0], dinc(st.sy))
    b2_ld = st.sy[1]

    @block.tensor
    def _(t):
        wt(t, st.sy, b2_ld)
        wt(t, st.vt)
        for dc in range(8):
            inc(st.tt, t.transpose(st.pb[6 if dc % 2 == 0 else 7][:, 0:16],
                                   st.eb2n[:, dc * 128:(dc + 1) * 128], st.idf[0:16, 0:16]))
    t_b2 = st.tt[1]

    @block.vector
    def _(v):
        base = t_b2 - 8
        for dc in range(8):
            wt(v, st.tt, base + dc + 1)
            inc(st.vt, v.scalar_tensor_tensor(out=oT[:, dc * 16:(dc + 1) * 16],
                                              in0=st.pb[6 if dc % 2 == 0 else 7][:, 0:16],
                                              scalar=1.0, in1=oT[:, dc * 16:(dc + 1) * 16],
                                              op0=AL.mult, op1=AL.add))
    st.v_oT = st.vt[1]
    st.oT = oT

    if st.debug:
        @block.scalar
        def _(s):
            wt(s, st.vt, st.v_oT)
            s.dma_start(out=st.dbg["d_o2"], in_=oT[:, :]).then_inc(st.sd[0], dinc(st.sd))
            s.dma_start(out=st.dbg["d_aX"], in_=st.X[:, :]).then_inc(st.sd[0], dinc(st.sd))

        @block.vector
        def _(v):
            wt(v, st.sd)
            inc(st.vt, v.engine_nop())


def emit_attn_head(st, h):
    nc, block = st.nc, st.block
    AR = st.AR
    dc_h = h // 2
    b0 = (h % 2) * 64
    dd = AR[:, 29712:29728]
    negmrow = AR[0:1, 27664:29712]

    for (dst, scol) in [(st.Q65, st.rotT), (st.K65, st.entT)]:
        for nc2 in range(2):
            for nq in range(2):
                ncol = nc2 * 2 + nq
                bnk = st.pb[nq]

                @block.tensor
                def _(t, ncol=ncol, bnk=bnk):
                    wt(t, st.vt)
                    mv = st.idf[:, 0:64] if b0 == 0 else st.idf[:, 64:128]
                    xs = st.xn1T[:, dc_h * 2048 + ncol * 512: dc_h * 2048 + (ncol + 1) * 512]
                    inc(st.tt, t.matmul(bnk[0:64, 0:512], mv, xs, start=True, stop=True))
            t_mv = st.tt[1]

            @block.vector
            def _(v, dst=dst, scol=scol, nc2=nc2, t_mv=t_mv):
                wt(v, st.tt, t_mv)
                for nq in range(2):
                    ncol = nc2 * 2 + nq
                    v.tensor_scalar(out=dst[0:64, ncol * 512:(ncol + 1) * 512],
                                    in0=st.pb[nq][0:64, 0:512],
                                    scalar1=scol[:, h:h + 1], scalar2=None, op0=AL.mult)
                inc(st.vt, v.engine_nop())
    v_qk = st.vt[1]

    @block.vector
    def _(v):
        inc(st.vt, v.memset(st.Vb[:, :].rearrange("p (a b) -> p a b", b=65)[:, :, 64:65], 1.0))
    v_vones = st.vt[1]

    identV = st.idf[0:64, 0:64] if b0 == 0 else st.idf[64:128, 64:128]
    for jt2 in range(8):
        @block.tensor
        def _(t, jt2=jt2, identV=identV):
            wt(t, st.vt)
            for k in range(2):
                jt = jt2 * 2 + k
                inc(st.tt, t.transpose(st.pb[4 + k][:, 0:64],
                                       st.xn1T[b0:b0 + 64, dc_h * 2048 + jt * 128: dc_h * 2048 + (jt + 1) * 128],
                                       identV))
        t_v = st.tt[1]

        @block.vector
        def _(v, jt2=jt2, t_v=t_v):
            wt(v, st.tt, t_v)
            for k in range(2):
                jt = jt2 * 2 + k
                v.tensor_copy(out=st.Vb[:, jt * 65:jt * 65 + 64], in_=st.pb[4 + k][:, 0:64])
            inc(st.vt, v.engine_nop())
    v_V = st.vt[1]

    for it in range(16):
        @block.tensor
        def _(t, it=it):
            wt(t, st.vt)
            for jc in range(4):
                inc(st.tt, t.matmul(st.pb[jc][:, 0:512], st.Q65[0:64, it * 128:(it + 1) * 128],
                                    st.K65[0:64, jc * 512:(jc + 1) * 512], start=True, stop=True))
        t_st = st.tt[1]

        @block.vector
        def _(v, it=it, t_st=t_st):
            wt(v, st.tt, t_st)
            for jc in range(4):
                v.tensor_reduce(out=st.sm2[:, 8 + jc:9 + jc], in_=st.pb[jc][:, 0:512],
                                axis=AX.X, op=AL.max)
            inc(st.vt, v.engine_nop())
            wt(v, st.vt)
            inc(st.vt, v.tensor_reduce(out=st.sm2[:, 12:13], in_=st.sm2[:, 8:12], axis=AX.X, op=AL.max))
            wt(v, st.vt)
            inc(st.vt, v.tensor_scalar_mul(st.sm2[:, 13:14], st.sm2[:, 12:13], -1.0))
        v_m = st.vt[1]

        @block.tensor
        def _(t, it=it, v_m=v_m):
            wt(t, st.vt, v_m)
            inc(st.tt, t.transpose(st.pb[4][0:1, 0:128], st.sm2[:, 13:14], st.idf[:, :]))
        t_nm = st.tt[1]

        @block.vector
        def _(v, it=it, t_nm=t_nm, negmrow=negmrow):
            wt(v, st.tt, t_nm)
            inc(st.vt, v.tensor_copy(out=negmrow[:, it * 128:(it + 1) * 128],
                                     in_=st.pb[4][0:1, 0:128]))
    v_stats = st.vt[1]

    if st.debug and h == DBGH:
        @block.scalar
        def _(s):
            wt(s, st.vt, v_stats)
            s.dma_start(out=st.dbg["d_negm"], in_=negmrow[:, :]).then_inc(st.sd[0], dinc(st.sd))

        @block.vector
        def _(v):
            wt(v, st.sd)
            inc(st.vt, v.engine_nop())

    for ic in range(2):
        for nc2 in range(2):
            icol = ic * 2 + nc2

            def st_mm(jt, icol=icol):
                sb = st.pb[jt % 2]

                @block.tensor
                def _(t, jt=jt, sb=sb, icol=icol):
                    wt(t, st.vt)
                    Ks = st.K65[0:64, jt * 128:(jt + 1) * 128]
                    Qs = st.Q65[0:64, icol * 512:(icol + 1) * 512]
                    t.matmul(sb[:, 0:512], Ks, Qs, start=True, stop=False)
                    inc(st.tt, t.matmul(sb[:, 0:512], st.ones1[:, :],
                                        negmrow[:, icol * 512:(icol + 1) * 512],
                                        start=False, stop=True))
                return st.tt[1]

            def exp_op(jt, t_s):
                sb = st.pb[jt % 2]

                @block.scalar
                def _(s, jt=jt, sb=sb, t_s=t_s):
                    wt(s, st.tt, t_s)
                    inc(st.ct, s.activation(out=st.Pb[:, (jt % 2) * 512:(jt % 2) * 512 + 512],
                                            in_=sb[:, 0:512], func=AF.Exp))
                return st.ct[1]

            def av_mm(jt, s_e):
                @block.tensor
                def _(t, jt=jt, s_e=s_e):
                    wt(t, st.ct, s_e)
                    inc(st.tt, t.matmul(st.pb[2][0:65, 0:512], st.Vb[:, jt * 65:(jt + 1) * 65],
                                        st.Pb[:, (jt % 2) * 512:(jt % 2) * 512 + 512],
                                        start=(jt == 0), stop=(jt == 15)))

            t_s = st_mm(0)
            for jt in range(16):
                s_e = exp_op(jt, t_s)
                if jt < 15:
                    t_s = st_mm(jt + 1)
                av_mm(jt, s_e)
            t_av = st.tt[1]

            @block.vector
            def _(v, icol=icol, nc2=nc2, t_av=t_av):
                wt(v, st.tt, t_av)
                inc(st.vt, v.tensor_copy(out=st.aO[:, nc2 * 512:(nc2 + 1) * 512], in_=st.pb[2][0:65, 0:512]))
        v_ao = st.vt[1]

        for itl2 in range(4):
            @block.tensor
            def _(t, ic=ic, itl2=itl2, v_ao=v_ao):
                wt(t, st.vt)
                for k in range(2):
                    itl = itl2 * 2 + k
                    inc(st.tt, t.transpose(st.pb[3 + k][:, 0:65], st.aO[:, itl * 128:(itl + 1) * 128],
                                           st.idf[0:65, 0:65]))
            t_at = st.tt[1]

            @block.vector
            def _(v, ic=ic, itl2=itl2, t_at=t_at):
                wt(v, st.tt, t_at)
                for k in range(2):
                    itl = itl2 * 2 + k
                    it = ic * 8 + itl
                    v.tensor_copy(out=st.aT[:, it * 64:(it + 1) * 64], in_=st.pb[3 + k][:, 0:64])
                    inc(st.vt, v.tensor_copy(out=dd[:, it:it + 1], in_=st.pb[3 + k][:, 64:65]))
    v_aT = st.vt[1]

    @block.scalar
    def _(s, v_aT=v_aT):
        wt(s, st.vt, v_aT)
        s.activation(out=AR[:, 29728:29744], in_=dd[:, :], func=AF.Ln)
        inc(st.ct, s.activation(out=AR[:, 29728:29744], in_=AR[:, 29728:29744], func=AF.Exp, scale=-1.0))
    s_i = st.ct[1]

    @block.vector
    def _(v, s_i=s_i):
        wt(v, st.ct, s_i)
        y = AR[:, 29728:29744]
        w2 = AR[:, 29744:29760]
        for _i in range(2):
            inc(st.vt, v.tensor_tensor(out=w2, in0=y, in1=dd[:, :], op=AL.mult))
            wt(v, st.vt)
            inc(st.vt, v.tensor_scalar(out=w2, in0=w2, scalar1=-1.0, scalar2=2.0, op0=AL.mult, op1=AL.add))
            wt(v, st.vt)
            inc(st.vt, v.tensor_tensor(out=y, in0=y, in1=w2, op=AL.mult))
            wt(v, st.vt)
        for it in range(16):
            v.scalar_tensor_tensor(out=st.X[:, it * D + h * 64: it * D + h * 64 + 64],
                                   in0=st.aT[:, it * 64:(it + 1) * 64], scalar=y[:, it:it + 1],
                                   in1=st.X[:, it * D + h * 64: it * D + h * 64 + 64],
                                   op0=AL.mult, op1=AL.add)
        inc(st.vt, v.engine_nop())


def emit_expert(st, e, uTe, gTe, espr, ew_f, oT):
    nc, block = st.nc, st.block
    AR = st.AR

    for mi, (W, dst) in enumerate([(st.eW1, uTe), (st.eWg, gTe)]):
        for fc in range(32):
            @block.sync
            def _(s, e=e, W=W, fc=fc):
                wt(s, st.vt)
                wt(s, st.tt)
                with nc.allow_non_contiguous_dma(reason="expert W1 column chunk"):
                    s.dma_start(out=ew_f[:, :].rearrange("p (dc o) -> p dc o", o=128),
                                in_=W[e, :, fc * 128:(fc + 1) * 128].rearrange("(dc p) o -> p dc o", p=128)).then_inc(st.sy[0], dinc(st.sy))
            wf_ld = st.sy[1]

            @block.tensor
            def _(t, e=e, fc=fc, wf_ld=wf_ld):
                wt(t, st.sy, wf_ld)
                wt(t, st.vt)
                last = None
                for dc in range(8):
                    hh_c = st.hT[:, dc * 16 + e: dc * 16 + e + 1]
                    wh_ = ew_f[:, dc * 128:(dc + 1) * 128]
                    last = t.matmul(st.pb[6][0:1, 0:128], hh_c, wh_, start=(dc == 0), stop=(dc == 7))
                inc(st.tt, last)
            t_u = st.tt[1]

            @block.vector
            def _(v, fc=fc, t_u=t_u):
                wt(v, st.tt, t_u)
                inc(st.vt, v.tensor_copy(out=espr[:, 0:128], in_=st.pb[6][0:1, 0:128]))
            v_e = st.vt[1]

            @block.tensor
            def _(t, fc=fc, v_e=v_e):
                wt(t, st.vt, v_e)
                inc(st.tt, t.transpose(st.pb[7][:, 0:1], espr[:, 0:128], st.idf[0:1, 0:1]))
            t_tr = st.tt[1]

            @block.vector
            def _(v, dst=dst, fc=fc, t_tr=t_tr):
                wt(v, st.tt, t_tr)
                inc(st.vt, v.tensor_copy(out=dst[:, fc:fc + 1], in_=st.pb[7][:, 0:1]))

        if st.debug and e == 0 and mi == 0:
            @block.scalar
            def _(s):
                wt(s, st.vt)
                s.dma_start(out=st.dbg["d_eur"], in_=uTe[:, :]).then_inc(st.sd[0], dinc(st.sd))

            @block.vector
            def _(v):
                wt(v, st.sd)
                inc(st.vt, v.engine_nop())

    @block.vector
    def _(v, e=e):
        wt(v, st.vt)
        v.tensor_tensor(out=uTe[:, :], in0=uTe[:, :], in1=st.b1T[:, e::16], op=AL.add)
        inc(st.vt, v.tensor_tensor(out=gTe[:, :], in0=gTe[:, :], in1=st.bgT[:, e::16], op=AL.add))
    v_b = st.vt[1]

    @block.scalar
    def _(s, v_b=v_b):
        wt(s, st.vt, v_b)
        s.activation(out=uTe[:, :], in_=uTe[:, :], func=AF.Silu)
        inc(st.ct, s.activation(out=gTe[:, :], in_=gTe[:, :], func=AF.Silu))
    s_si = st.ct[1]

    if st.debug and e == 0:
        @block.scalar
        def _(s, s_si=s_si):
            wt(s, st.ct, s_si)
            s.dma_start(out=st.dbg["d_eu"], in_=uTe[:, :]).then_inc(st.sd[0], dinc(st.sd))
            s.dma_start(out=st.dbg["d_eg"], in_=gTe[:, :]).then_inc(st.sd[0], dinc(st.sd))

        @block.vector
        def _(v):
            wt(v, st.sd)
            inc(st.vt, v.engine_nop())

    @block.vector
    def _(v, s_si=s_si):
        wt(v, st.ct, s_si)
        inc(st.vt, v.tensor_tensor(out=uTe[:, :], in0=uTe[:, :], in1=gTe[:, :], op=AL.mult))
    v_ug = st.vt[1]

    for fc in range(32):
        @block.sync
        def _(s, e=e, fc=fc):
            wt(s, st.vt)
            wt(s, st.tt)
            s.dma_start(out=ew_f[:, :], in_=st.eW2[e, fc * 128:(fc + 1) * 128, :]).then_inc(st.sy[0], dinc(st.sy))
        wf_ld = st.sy[1]

        @block.tensor
        def _(t, e=e, fc=fc, wf_ld=wf_ld):
            wt(t, st.sy, wf_ld)
            wt(t, st.vt)
            last = None
            ug_c = uTe[:, fc:fc + 1]
            for oc in range(2):
                bnk = st.pb[6] if oc == 0 else st.pb[7]
                last = t.matmul(bnk[0:1, 0:512], ug_c, ew_f[:, oc * 512:(oc + 1) * 512],
                                start=(fc == 0), stop=(fc == 31))
            inc(st.tt, last)
    t_y = st.tt[1]

    yrow = AR[0:1, 19456:20480]

    @block.vector
    def _(v, t_y=t_y):
        wt(v, st.tt, t_y)
        inc(st.vt, v.tensor_copy(out=AR[0:1, 19456:19968], in_=st.pb[6][0:1, 0:512]))
        inc(st.vt, v.tensor_copy(out=AR[0:1, 19968:20480], in_=st.pb[7][0:1, 0:512]))
    v_y = st.vt[1]

    if st.debug and e == 0:
        @block.scalar
        def _(s, v_y=v_y):
            wt(s, st.vt, v_y)
            s.dma_start(out=st.dbg["d_ey"], in_=yrow[:, :]).then_inc(st.sd[0], dinc(st.sd))

        @block.vector
        def _(v):
            wt(v, st.sd)
            inc(st.vt, v.engine_nop())

    for dc in range(8):
        @block.tensor
        def _(t, dc=dc, v_y=v_y):
            wt(t, st.vt, v_y)
            wt(t, st.vt)
            inc(st.tt, t.transpose(st.pb[6 if dc % 2 == 0 else 7][:, 0:1],
                                   yrow[:, dc * 128:(dc + 1) * 128], st.idf[0:1, 0:1]))
        t_tr = st.tt[1]

        @block.vector
        def _(v, e=e, dc=dc, t_tr=t_tr):
            wt(v, st.tt, t_tr)
            inc(st.vt, v.scalar_tensor_tensor(out=oT[:, dc * 16 + e:dc * 16 + e + 1],
                                              in0=st.pb[6 if dc % 2 == 0 else 7][:, 0:1],
                                              scalar=1.0, in1=oT[:, dc * 16 + e:dc * 16 + e + 1],
                                              op0=AL.mult, op1=AL.add))


def emit_cpath(st):
    nc, block = st.nc, st.block
    AR = st.AR
    wnat = st.wnat
    kho = AR[0:16, 23568:24592]
    vho = AR[0:16, 24592:25616]
    khoT2 = AR[0:64, 25888:26144]
    bqfm2 = AR[0:64, 25744:25760]
    sbrow = st.espr
    WqT2 = AR[0:64, 16384:18432]
    Mall = AR[:, 20480:22528]
    aCt4 = AR[:, 16384:20480]
    Pc = st.Pc
    negm = AR[:, 29712:29728]
    den = AR[:, 29728:29744]
    inv = AR[:, 29744:29760]
    pTs = AR[0:16, 27248:27376]
    cboB = AR[:, 22544:23568]
    brow = AR[0:1, 18432:19456]

    for (W, bvec, dst) in [(st.c_Wk, st.c_bk, kho), (st.c_Wv, st.c_bv, vho)]:
        @block.sync
        def _(s, W=W, bvec=bvec):
            wt(s, st.vt)
            with nc.allow_non_contiguous_dma(reason="bias row"):
                s.dma_start(out=brow[:, :], in_=bvec.unsqueeze(0)).then_inc(st.sy[0], dinc(st.sy))
        b_ld = st.sy[1]

        @block.tensor
        def _(t, b_ld=b_ld):
            wt(t, st.sy, b_ld)
            wt(t, st.vt)
            t.matmul(st.pb[2][0:16, 0:512], st.ones1[0:1, 0:16], brow[:, 0:512], start=True, stop=True)
            inc(st.tt, t.matmul(st.pb[3][0:16, 0:512], st.ones1[0:1, 0:16], brow[:, 512:1024],
                                start=True, stop=True))

        for dc in range(8):
            sl = (dc % 2) * 1024

            @block.sync
            def _(s, W=W, dc=dc, sl=sl):
                wt(s, st.vt)
                s.dma_start(out=wnat[:, sl:sl + 1024], in_=W[dc * 128:(dc + 1) * 128, :]).then_inc(st.sy[0], dinc(st.sy))
            w_ld = st.sy[1]

            @block.tensor
            def _(t, dc=dc, sl=sl, w_ld=w_ld):
                wt(t, st.sy, w_ld)
                last = None
                for oc in range(2):
                    last = t.matmul(st.pb[oc][0:16, 0:512], st.oT[:, dc * 16:(dc + 1) * 16],
                                    wnat[:, sl + oc * 512:sl + (oc + 1) * 512],
                                    start=(dc == 0), stop=(dc == 7))
                inc(st.tt, last)

            @block.vector
            def _(v):
                wt(v, st.tt)
                inc(st.vt, v.engine_nop())
        t_p = st.tt[1]

        @block.vector
        def _(v, dst=dst, t_p=t_p):
            wt(v, st.tt, t_p)
            for oc in range(2):
                v.tensor_copy(out=dst[:, oc * 512:(oc + 1) * 512], in_=st.pb[oc][0:16, 0:512])
            for oc in range(2):
                v.scalar_tensor_tensor(out=dst[:, oc * 512:(oc + 1) * 512],
                                       in0=st.pb[2 + oc][0:16, 0:512], scalar=1.0,
                                       in1=dst[:, oc * 512:(oc + 1) * 512], op0=AL.mult, op1=AL.add)
            inc(st.vt, v.engine_nop())
    v_kv = st.vt[1]

    if KCA < 2:
        st.v_cp = st.vt[1]
        return
    @block.sync
    def _(s):
        wt(s, st.vt, v_kv)
        with nc.allow_non_contiguous_dma(reason="bq per-head fm"):
            s.dma_start(out=bqfm2[:, :], in_=st.c_bq.rearrange("(h o) -> o h", o=64),
                        allow_slow_non_contiguous=True).then_inc(st.sy[0], dinc(st.sy))
    bq_ld = st.sy[1]

    @block.tensor
    def _(t, bq_ld=bq_ld):
        wt(t, st.sy, bq_ld)
        wt(t, st.vt)
        for h in range(15):
            t.transpose(st.pb[4][0:64, h * 16:(h + 1) * 16], kho[:, h * 64:(h + 1) * 64],
                        st.idf[0:16, 0:16])
        inc(st.tt, t.transpose(st.pb[4][0:64, 240:256], kho[:, 960:1024], st.idf[0:16, 0:16]))
    t_kT = st.tt[1]

    @block.vector
    def _(v, t_kT=t_kT):
        wt(v, st.tt, t_kT)
        inc(st.vt, v.tensor_copy(out=khoT2[:, :], in_=st.pb[4][0:64, 0:256]))
    v_kT = st.vt[1]

    if KCB < 2:
        st.v_cp = st.vt[1]
        return
    @block.tensor
    def _(t, v_kT=v_kT):
        wt(t, st.vt, v_kT)
        last = None
        for h in range(16):
            last = t.matmul(st.pb[5][0:1, h * 16:(h + 1) * 16],
                            bqfm2[:, h:h + 1],
                            khoT2[:, h * 16:(h + 1) * 16],
                            start=True, stop=True)
        inc(st.tt, last)
    t_sb = st.tt[1]

    @block.vector
    def _(v, t_sb=t_sb):
        wt(v, st.tt, t_sb)
        inc(st.vt, v.tensor_scalar_mul(sbrow[:, 0:256], st.pb[5][0:1, 0:256], 0.125))
    v_sb = st.vt[1]

    if KCA < 3:
        st.v_cp = st.vt[1]
        return
    @block.sync
    def _(s):
        wt(s, st.vt, v_sb)
        with nc.allow_non_contiguous_dma(reason="bias row"):
            s.dma_start(out=brow[:, :], in_=st.c_bo.unsqueeze(0)).then_inc(st.sy[0], dinc(st.sy))
    bo_ld = st.sy[1]

    @block.tensor
    def _(t, bo_ld=bo_ld):
        wt(t, st.sy, bo_ld)
        wt(t, st.vt)
        t.matmul(st.pb[6][:, 0:512], st.ones1[:, :], brow[:, 0:512], start=True, stop=True)
        inc(st.tt, t.matmul(st.pb[7][:, 0:512], st.ones1[:, :], brow[:, 512:1024], start=True, stop=True))
    t_bo = st.tt[1]

    @block.vector
    def _(v, t_bo=t_bo):
        wt(v, st.tt, t_bo)
        v.tensor_copy(out=cboB[:, 0:512], in_=st.pb[6][:, 0:512])
        inc(st.vt, v.tensor_copy(out=cboB[:, 512:1024], in_=st.pb[7][:, 0:512]))
    v_bo = st.vt[1]

    if KCA < 4:
        st.v_cp = st.vt[1]
        return
    for dcd in range(8):
        sl = (dcd % 2) * 1024

        @block.sync
        def _(s, dcd=dcd, sl=sl):
            wt(s, st.vt)
            s.dma_start(out=wnat[:, sl:sl + 1024], in_=st.c_Wq[dcd * 128:(dcd + 1) * 128, :]).then_inc(st.sy[0], dinc(st.sy))
        wq_ld = st.sy[1]

        for h2 in range(8):
            @block.tensor
            def _(t, h2=h2, sl=sl, wq_ld=wq_ld):
                wt(t, st.sy, wq_ld)
                wt(t, st.vt)
                for k in range(2):
                    h = h2 * 2 + k
                    inc(st.tt, t.transpose(st.pb[4 + k][0:64, 0:128],
                                           wnat[:, sl + h * 64:sl + (h + 1) * 64], st.idf[:, :]))
            t_tp = st.tt[1]

            @block.vector
            def _(v, h2=h2, t_tp=t_tp):
                wt(v, st.tt, t_tp)
                for k in range(2):
                    h = h2 * 2 + k
                    v.tensor_copy(out=WqT2[:, h * 128:(h + 1) * 128], in_=st.pb[4 + k][0:64, 0:128])
                inc(st.vt, v.engine_nop())
        v_wT = st.vt[1]

        @block.tensor
        def _(t, v_wT=v_wT):
            wt(t, st.vt, v_wT)
            last = None
            for h in range(16):
                last = t.matmul(st.pb[6][:, h * 16:(h + 1) * 16],
                                WqT2[:, h * 128:(h + 1) * 128],
                                khoT2[:, h * 16:(h + 1) * 16],
                                start=True, stop=True)
            inc(st.tt, last)
        t_M = st.tt[1]

        @block.vector
        def _(v, dcd=dcd, t_M=t_M):
            wt(v, st.tt, t_M)
            inc(st.vt, v.tensor_scalar_mul(Mall[:, dcd * 256:(dcd + 1) * 256], st.pb[6][:, 0:256], 0.125))
    v_M = st.vt[1]

    if KCSUB < 2:
        st.v_cp = st.vt[1]
        return
    for itg in range(4):
        for itl in range(4):
            it = itg * 4 + itl

            @block.tensor
            def _(t, it=it, v_M=v_M):
                wt(t, st.vt)
                for dc in range(8):
                    xh = st.xn1T[:, dc * 2048 + it * 128: dc * 2048 + (it + 1) * 128]
                    t.matmul(st.pb[0][:, 0:256], xh, Mall[:, dc * 256:(dc + 1) * 256],
                             start=(dc == 0), stop=False)
                last = t.matmul(st.pb[0][:, 0:256], st.ones1[:, :], sbrow[:, 0:256],
                                start=False, stop=True)
                inc(st.tt, last)
            t_S = st.tt[1]

            @block.vector
            def _(v, t_S=t_S):
                wt(v, st.tt, t_S)
                inc(st.vt, v.tensor_copy(out=Pc[:, :], in_=st.pb[0][:, 0:256]))
                wt(v, st.vt)
                inc(st.vt, v.tensor_reduce(out=st.sm2[:, 0:16],
                                           in_=Pc[:, :].rearrange("p (a b) -> p a b", b=16),
                                           axis=AX.X, op=AL.max))
                wt(v, st.vt)
                inc(st.vt, v.tensor_scalar_mul(negm[:, :], st.sm2[:, 0:16], -1.0))
                wt(v, st.vt)
                inc(st.vt, v.engine_nop())
            v_m = st.vt[1]

            @block.scalar
            def _(s, v_m=v_m):
                wt(s, st.vt, v_m)
                last = None
                for h in range(16):
                    last = s.activation(out=Pc[:, h * 16:(h + 1) * 16], in_=Pc[:, h * 16:(h + 1) * 16],
                                        func=AF.Exp, bias=negm[:, h:h + 1], accum_out=den[:, h:h + 1])
                inc(st.ct, last)
            s_e = st.ct[1]

            @block.scalar
            def _(s, s_e=s_e):
                wt(s, st.ct, s_e)
                s.activation(out=inv[:, :], in_=den[:, :], func=AF.Ln)
                inc(st.ct, s.activation(out=inv[:, :], in_=inv[:, :], func=AF.Exp, scale=-1.0))
            s_i = st.ct[1]

            @block.vector
            def _(v, s_i=s_i):
                wt(v, st.ct, s_i)
                w = st.sm2[:, 0:16]
                for _i in range(2):
                    inc(st.vt, v.tensor_tensor(out=w, in0=inv[:, :], in1=den[:, :], op=AL.mult))
                    wt(v, st.vt)
                    inc(st.vt, v.tensor_scalar(out=w, in0=w, scalar1=-1.0, scalar2=2.0,
                                               op0=AL.mult, op1=AL.add))
                    wt(v, st.vt)
                    inc(st.vt, v.tensor_tensor(out=inv[:, :], in0=inv[:, :], in1=w, op=AL.mult))
                    wt(v, st.vt)
                inc(st.vt, v.tensor_tensor(
                    out=Pc[:, :].rearrange("p (a b) -> p a b", b=16),
                    in0=Pc[:, :].rearrange("p (a b) -> p a b", b=16),
                    in1=inv[:, :].unsqueeze(2).broadcast_to([128, 16, 16]), op=AL.mult))
            v_P = st.vt[1]

            for h in (range(16) if KCSUB >= 3 else []):
                b0 = (h % 2) * 64

                @block.tensor
                def _(t, h=h, v_P=v_P):
                    wt(t, st.vt, v_P)
                    wt(t, st.vt)
                    inc(st.tt, t.transpose(st.pb[4 + (h % 2)][0:16, 0:128],
                                           Pc[:, h * 16:(h + 1) * 16], st.idf[:, :]))
                t_pT = st.tt[1]

                @block.vector
                def _(v, h=h, t_pT=t_pT):
                    wt(v, st.tt, t_pT)
                    inc(st.vt, v.tensor_copy(out=pTs[:, :], in_=st.pb[4 + (h % 2)][0:16, 0:128]))
                v_pT = st.vt[1]

                @block.tensor
                def _(t, h=h, b0=b0, v_pT=v_pT):
                    wt(t, st.vt, v_pT)
                    inc(st.tt, t.matmul(st.pb[7][b0:b0 + 64, (h % 2) * 128:(h % 2) * 128 + 128],
                                        vho[:, h * 64:(h + 1) * 64], pTs[:, :],
                                        start=True, stop=True))
                t_av = st.tt[1]

                @block.vector
                def _(v, h=h, b0=b0, itl=itl, t_av=t_av):
                    wt(v, st.tt, t_av)
                    inc(st.vt, v.tensor_copy(
                        out=aCt4[b0:b0 + 64, itl * 1024 + (h // 2) * 128: itl * 1024 + (h // 2) * 128 + 128],
                        in_=st.pb[7][b0:b0 + 64, (h % 2) * 128:(h % 2) * 128 + 128]))
        v_g = st.vt[1]

        for dc in (range(8) if KCSUB >= 4 else []):
            sl = (dc % 2) * 1024

            @block.sync
            def _(s, dc=dc, sl=sl, v_g=v_g):
                wt(s, st.vt)
                s.dma_start(out=wnat[:, sl:sl + 1024], in_=st.c_Wo[dc * 128:(dc + 1) * 128, :]).then_inc(st.sy[0], dinc(st.sy))
            wo_ld = st.sy[1]

            @block.tensor
            def _(t, dc=dc, sl=sl, wo_ld=wo_ld, v_g=v_g):
                wt(t, st.sy, wo_ld)
                wt(t, st.vt)
                last = None
                for itl in range(4):
                    for oc in range(2):
                        last = t.matmul(st.pb[itl * 2 + oc][:, 0:512],
                                        aCt4[:, itl * 1024 + dc * 128: itl * 1024 + (dc + 1) * 128],
                                        wnat[:, sl + oc * 512:sl + (oc + 1) * 512],
                                        start=(dc == 0), stop=(dc == 7))
                inc(st.tt, last)

            @block.vector
            def _(v):
                wt(v, st.tt)
                inc(st.vt, v.engine_nop())
        t_op = st.tt[1]

        @block.vector
        def _(v, itg=itg, t_op=t_op):
            if KCSUB < 4:
                inc(st.vt, v.engine_nop())
                return
            wt(v, st.tt, t_op)
            for itl in range(4):
                it = itg * 4 + itl
                for oc in range(2):
                    xa = st.X[:, it * D + oc * 512: it * D + (oc + 1) * 512]
                    v.scalar_tensor_tensor(out=xa, in0=st.pb[itl * 2 + oc][:, 0:512], scalar=1.0,
                                           in1=xa, op0=AL.mult, op1=AL.add)
            for itl in range(4):
                it = itg * 4 + itl
                for oc in range(2):
                    xa = st.X[:, it * D + oc * 512: it * D + (oc + 1) * 512]
                    v.tensor_tensor(out=xa, in0=xa, in1=cboB[:, oc * 512:(oc + 1) * 512], op=AL.add)
            inc(st.vt, v.engine_nop())
    st.v_cp = st.vt[1]


def kernel(**inputs):
    nc = build_program(debug=False)
    in_maps = make_in_maps(inputs)
    res = run_bass_kernel_spmd(nc, in_maps, core_ids=list(range(8)))
    outs = [r["out"].reshape(T, D) for r in res.results]
    return np.stack(outs, axis=0)


def make_in_maps(inputs):
    iota = np.arange(256, dtype=np.float32)
    jcol = np.arange(128, dtype=np.float32)
    consts = {
        "c_identf": np.eye(128, dtype=np.float32),
        "c_identb": np.eye(128, dtype=np.float32).astype(ml_dtypes.bfloat16),
        "c_mask0": (iota[None, :] >= jcol[:, None]).astype(np.float32),
        "c_mask1": (iota[None, :] >= (jcol[:, None] + 128.0)).astype(np.float32),
        "c_sel16": np.repeat(np.eye(16, dtype=np.float32), 128, axis=1),
        "c_sel64": np.concatenate([np.zeros((64, 64), np.float32),
                                   np.eye(64, dtype=np.float32)], axis=0),
    }
    in_maps = []
    for b in range(8):
        m = dict(consts)
        for k, v in inputs.items():
            if k == "L0_attn_u":
                m[k] = np.ascontiguousarray(np.asarray(v)[b])
            else:
                m[k] = np.asarray(v)
        in_maps.append(m)
    return in_maps


# version 04 (final)
# speedup vs baseline: 1.3824x; 1.3650x over previous
import contextlib
import os
KPHASES = int(os.environ.get("KPHASES", "5"))
KCSUB = int(os.environ.get("KCSUB", "4"))
KCA = int(os.environ.get("KCA", "4"))
KCB = int(os.environ.get("KCB", "3"))
KEXP = int(os.environ.get("KEXP", "1"))
KATT = int(os.environ.get("KATT", "1"))
import numpy as np
import ml_dtypes
import concourse.bass as bass
import concourse.mybir as mybir
from concourse.bass_utils import run_bass_kernel_spmd

F32 = mybir.dt.float32
BF16 = mybir.dt.bfloat16
AL = mybir.AluOpType
AF = mybir.ActivationFunctionType
AX = mybir.AxisListType

T, NT, D, DI, NH, DS, CONVD, DIP = 2048, 16, 1024, 2048, 16, 64, 2176, 4240
LN2 = float(np.log(2.0))
LN_EPS = 1e-5
RMS_EPS = 1e-5


class St:
    pass


def inc(rec, inst, k=1):
    rec[1] += k
    inst.then_inc(rec[0], k)
    return rec[1]


def wt(eng, rec, val=None):
    eng.wait_ge(rec[0], rec[1] if val is None else val)


def dinc(rec):
    rec[1] += 16
    return 16


def mxfp4_quant_emit(v, st, x_ap, nrows, ncols, kmin, kmax, scl2d, iscl2d, out_ap):
    nb = ncols // 32
    x3 = x_ap.rearrange("p (b k) -> p b k", k=32)
    amax = st.qs[0:nrows, 0:nb]
    pexp = st.qs[0:nrows, nb:2 * nb]
    ipexp = st.qs[0:nrows, 2 * nb:3 * nb]
    tmpb = st.qs[0:nrows, 3 * nb:4 * nb]
    tmpb2 = st.qs[0:nrows, 4 * nb:5 * nb]
    inc(st.vt, v.tensor_reduce(out=amax, in_=x3, axis=AX.X, op=AL.max, apply_absolute_value=True))
    wt(v, st.vt)
    v.tensor_scalar(out=pexp, in0=amax, scalar1=float(2.0 ** kmin), scalar2=float(2.0 ** kmin),
                    op0=AL.is_ge, op1=AL.mult)
    v.tensor_scalar(out=ipexp, in0=amax, scalar1=float(2.0 ** kmin), scalar2=float(2.0 ** -kmin),
                    op0=AL.is_ge, op1=AL.mult)
    for k in range(kmin + 1, kmax + 1):
        v.tensor_scalar(out=tmpb, in0=amax, scalar1=float(2.0 ** k),
                        scalar2=float(2.0 ** k - 2.0 ** (k - 1)), op0=AL.is_ge, op1=AL.mult)
        v.tensor_scalar(out=tmpb2, in0=amax, scalar1=float(2.0 ** k),
                        scalar2=float(2.0 ** -k - 2.0 ** (-k + 1)), op0=AL.is_ge, op1=AL.mult)
        v.tensor_tensor(out=pexp, in0=pexp, in1=tmpb, op=AL.add)
        v.tensor_tensor(out=ipexp, in0=ipexp, in1=tmpb2, op=AL.add)
    v.tensor_tensor(out=ipexp, in0=ipexp, in1=iscl2d, op=AL.mult)
    v.tensor_tensor(out=pexp, in0=pexp, in1=scl2d, op=AL.mult)
    xs = st.q1[0:nrows, 0:ncols]
    v.tensor_tensor(out=xs.rearrange("p (b k) -> p b k", k=32), in0=x3,
                    in1=ipexp.unsqueeze(2).broadcast_to([nrows, nb, 32]), op=AL.mult)
    sgn = st.q2[0:nrows, 0:ncols]
    v.tensor_scalar(out=sgn, in0=xs, scalar1=0.0, scalar2=2.0, op0=AL.is_ge, op1=AL.mult)
    v.tensor_scalar(out=sgn, in0=sgn, scalar1=-1.0, scalar2=None, op0=AL.add)
    a = xs
    v.tensor_tensor(out=a, in0=xs, in1=sgn, op=AL.mult)
    v.tensor_scalar(out=a, in0=a, scalar1=6.0, scalar2=None, op0=AL.min)
    tq = out_ap
    v.tensor_scalar(out=tq, in0=a, scalar1=4.0, scalar2=0.5, op0=AL.is_lt, op1=AL.mult)
    v.tensor_scalar(out=tq, in0=tq, scalar1=0.5, scalar2=None, op0=AL.add)
    v.scalar_tensor_tensor(out=tq, in0=a, scalar=2.0, in1=tq, op0=AL.is_lt, op1=AL.add)
    v.tensor_tensor(out=tq, in0=tq, in1=a, op=AL.mult)
    r = st.lndump[0:nrows, 0:ncols]
    v.tensor_scalar(out=r, in0=tq, scalar1=0.5, scalar2=None, op0=AL.is_gt)
    for thr, op in [(1.5, AL.is_ge), (2.5, AL.is_gt), (3.5, AL.is_ge), (4.5, AL.is_gt), (5.5, AL.is_ge)]:
        v.scalar_tensor_tensor(out=r, in0=tq, scalar=float(thr), in1=r, op0=op, op1=AL.add)
    stp = tq
    v.tensor_scalar(out=stp, in0=a, scalar1=4.0, scalar2=-1.0, op0=AL.is_lt, op1=AL.mult)
    v.tensor_scalar(out=stp, in0=stp, scalar1=2.0, scalar2=None, op0=AL.add)
    v.tensor_scalar(out=a, in0=a, scalar1=2.0, scalar2=-0.5, op0=AL.is_lt, op1=AL.mult)
    v.tensor_tensor(out=stp, in0=stp, in1=a, op=AL.add)
    v.tensor_tensor(out=out_ap, in0=stp, in1=r, op=AL.mult)
    v.tensor_tensor(out=out_ap.rearrange("p (b k) -> p b k", k=32),
                    in0=out_ap.rearrange("p (b k) -> p b k", k=32),
                    in1=pexp.unsqueeze(2).broadcast_to([nrows, nb, 32]), op=AL.mult)
    v.tensor_tensor(out=out_ap, in0=out_ap, in1=sgn, op=AL.mult)


def build_program(debug=False):
    nc = bass.Bass("TRN2", target_bir_lowering=False, detect_race_conditions=False)
    st = St()
    st.nc = nc
    st.es = contextlib.ExitStack()
    st.debug = debug

    def di(name, shape, dtype=F32):
        return nc.dram_tensor(name, shape, dtype, kind="ExternalInput").ap()

    st.u = di("L0_attn_u", [T, D])
    st.ipw = di("L0_attn_in_proj_w", [DIP, D])
    st.convw = di("L0_attn_conv_w", [CONVD, 4])
    st.convb = di("L0_attn_conv_b", [CONVD])
    st.dt_bias = di("L0_attn_dt_bias", [NH])
    st.A_log = di("L0_attn_A_log", [NH])
    st.Dp = di("L0_attn_D", [NH])
    st.xnw = di("L0_attn_x_norm_w", [D])
    st.xnb = di("L0_attn_x_norm_b", [D])
    st.rmsw = di("L0_attn_rms_w", [DI])
    st.opw = di("L0_attn_out_proj_w", [D, DI])
    st.fw = di("L0_ffn_weight", [D, D])
    st.fb = di("L0_ffn_bias", [D])
    st.fdw = di("L0_ffn_delta_w", [D, 32])
    st.fda = di("L0_ffn_delta_act", [1])
    st.rot = di("L1_attn_rotation_params", [16, 64])
    st.ent = di("L1_attn_entangle_params", [16, 64])
    st.sq_in = di("L1_ffn_slot_query", [1, 16, D])
    for n in ["d_Wq", "d_Wk", "d_Wv", "d_Wo", "c_Wq", "c_Wk", "c_Wv", "c_Wo"]:
        setattr(st, n, di("L1_ffn_" + n, [D, D]))
    for n in ["d_bq", "d_bk", "d_bv", "d_bo", "c_bq", "c_bk", "c_bv", "c_bo"]:
        setattr(st, n, di("L1_ffn_" + n, [D]))
    st.elng = di("L1_ffn_e_ln_g", [16, D])
    st.elnb = di("L1_ffn_e_ln_b", [16, D])
    st.eW1 = di("L1_ffn_e_W1", [16, D, 4096])
    st.eb1 = di("L1_ffn_e_b1", [16, 4096])
    st.eWg = di("L1_ffn_e_Wg", [16, D, 4096])
    st.ebg = di("L1_ffn_e_bg", [16, 4096])
    st.eW2 = di("L1_ffn_e_W2", [16, 4096, D])
    st.eb2 = di("L1_ffn_e_b2", [16, D])
    st.identf = di("c_identf", [128, 128])
    st.identb = di("c_identb", [128, 128], BF16)
    st.mask0 = di("c_mask0", [128, 256])
    st.mask1 = di("c_mask1", [128, 256])
    st.sel16i = di("c_sel16", [16, 2048])
    st.sel64i = di("c_sel64", [128, 64])

    st.out = nc.dram_tensor("out", [T, D], F32, kind="ExternalOutput").ap()
    st.s_ipW = nc.dram_tensor("s_ipW", [D, DIP], F32, kind="Internal").ap()
    st.s_opW = nc.dram_tensor("s_opW", [DI, D], F32, kind="Internal").ap()
    st.s_fqW = nc.dram_tensor("s_fqW", [D, D], F32, kind="Internal").ap()

    st.dbg = {}
    if debug:
        for name, shape in [
            ("d_xn0", [128, 2 * D]), ("d_u", [128, 2 * D]),
            ("d_gT", [128, 16 * 256]), ("d_xBC", [128, 17 * 256]),
            ("d_xBCp", [128, 17 * 256]), ("d_dt1", [16, 2048]),
            ("d_xn1h", [128, 8192]), ("d_o", [16, 1024]), ("d_o2", [128, 128]),
            ("d_aX", [128, 16384]), ("d_negm", [1, 2048]), ("d_S0", [128, 512]),
            ("d_Qh", [65, 1024]), ("d_Ql", [65, 1024]), ("d_Kh", [65, 1024]), ("d_Kl", [65, 1024]),
            ("d_dqh", [16, 1024]), ("d_dkh", [128, 2048]), ("d_dvh", [128, 2048]),
            ("d_eh", [16, 1024]), ("d_eu", [128, 32]), ("d_eg", [128, 32]), ("d_ey", [1, 1024]),
            ("d_eur", [128, 32]),
            ("d_eb1T", [128, 512]),
            ("d_dS", [16, 2048]), ("d_daD2", [64, 256]),
            ("d_P0", [128, 512]), ("d_aO0", [65, 1024]), ("d_dd", [128, 16]),
            ("d_dt", [16, 2048]), ("d_xq", [128, 2 * D]),
            ("d_xL0", [128, NT * D]),
        ]:
            st.dbg[name] = nc.dram_tensor(name, shape, F32, kind="ExternalOutput").ap()

    with st.es:
        es = st.es
        sbt = lambda name, shape, dt=F32: es.enter_context(nc.sbuf_tensor(name, shape, dt))

        st.X = sbt("X", [128, NT * D])
        st.idf = sbt("idf", [128, 128])
        st.idb = sbt("idb", [128, 128], BF16)
        st.msk = sbt("msk", [128, 512])
        st.ones1 = sbt("ones1", [1, 128])
        st.onesc = sbt("onesc", [128, 1])
        st.xwb = sbt("xwb", [128, D])
        st.xbb = sbt("xbb", [128, D])
        st.fbb = sbt("fbb", [128, D])
        st.cwT = sbt("cwT", [128, 68])
        st.cbT = sbt("cbT", [128, 17])
        st.rwT = sbt("rwT", [128, 16])
        st.dtbc = sbt("dtbc", [16, 1])
        st.Acol = sbt("Acol", [16, 1])
        st.DB = sbt("DB", [128, 16])
        st.sm1 = sbt("sm1", [1, 528])
        st.sm2 = sbt("sm2", [128, 16])
        st.srow = sbt("srow", [1, 256])
        st.sel16 = sbt("sel16", [16, 2048])
        st.sel64 = sbt("sel64", [128, 64])
        st.AR = sbt("AR", [128, 29824])
        AR = st.AR
        st.xn0c = AR[:, 0:2048]
        st.gT = AR[:, 2048:6144]
        st.xBCc = AR[:, 6144:10496]
        st.q1 = AR[:, 10496:11520]
        st.q2 = AR[:, 11520:12544]
        st.uT = AR[:, 12544:14592]
        st.wip = AR[:, 14592:16640]
        st.ghl = AR[:, 16640:18688]
        st.wop = AR[:, 18688:20736]
        st.wfq = AR[:, 20736:22784]
        st.dtp = AR[0:16, 22784:24832]
        st.PS = AR[0:64, 24832:26880]
        st.attb = AR[:, 26880:27392]
        st.Eb = AR[:, 27392:27648]
        st.Cf32 = AR[0:64, 28160:28416]
        st.btok = AR[:, 28416:28544]
        st.xhT = AR[:, 28544:28800]
        st.wBh = AR[:, 28800:28928]
        st.sS = AR[:, 28928:29024]
        st.cdB = AR[:, 29024:29040]
        st.hist = AR[:, 29040:29092]
        st.cin = AR[:, 29092:29352]
        st.qs = AR[:, 29352:29512]
        st.rows = AR[0:1, 0:3088]
        st.uc = st.gT[:, 0:2048]
        st.lndump = st.gT[:, 2048:3072]
        st.pnat = st.gT
        st.pspl = st.uT
        st.ptp = AR[:, 6144:8192]
        st.psc = st.qs[:, 144:152]
        st.xqT = st.uT
        st.xn1T = AR[:, 0:16384]
        st.t_tok = AR[:, 16384:17408]
        st.ldmp1 = AR[:, 18432:19456]
        st.Sd = AR[0:16, 16384:18432]
        st.khT = AR[:, 23568:25616]
        st.vhT = AR[:, 20480:22528]
        st.wnat = AR[:, 27664:29712]
        st.sqT = AR[:, 25616:25744]
        st.qhC = AR[0:16, 17408:18432]
        st.o_tok = AR[0:16, 24592:25616]
        st.Q65 = AR[0:65, 16384:18432]
        st.K65 = AR[0:65, 18432:20480]
        st.aT = AR[:, 20480:21504]
        st.Vb = AR[:, 21504:22544]
        st.Pb = AR[:, 22544:23568]
        st.aO = AR[0:65, 23568:24592]
        st.b1T = AR[:, 25616:26128]
        st.bgT = AR[:, 26128:26640]
        st.hT = AR[:, 26640:26768]
        st.uTe = AR[:, 26896:26928]
        st.gTe = AR[:, 26928:26960]
        st.espr = AR[0:1, 26992:27248]
        st.eb2n = AR[0:16, 25616:26640]
        st.ebnat = AR[0:16, 20480:24576]
        st.kvoT = AR[:, 29184:29312]
        st.Pc = AR[:, 26640:26896]
        st.rotT = AR[0:64, 29760:29776]
        st.entT = AR[0:64, 29776:29792]
        st.dbT = AR[:, 29792:29808]

        st.pb = [es.enter_context(nc.psum_tensor(f"pb{i}", [128, 512], F32)) for i in range(8)]

        def sem(name):
            return [es.enter_context(nc.semaphore(name)), 0]
        st.sy = sem("sy")
        st.sd = sem("sd")
        st.gd = sem("gd")
        st.ot = sem("ot")
        st.tt = sem("tt")
        st.vt = sem("vt")
        st.ct = sem("ct")
        st.ipc = sem("ipc")
        st.ipe = sem("ipe")
        st.opc = sem("opc")
        st.fqc = sem("fqc")

        st.block = es.enter_context(nc.Block())

        emit_p0(st)
        emit_prep(st)
        for c in range(8):
            emit_l0_chunk(st, c)
        emit_p2(st)
        if KPHASES >= 3:
            emit_dpath(st)
        if KPHASES >= 4:
            emit_p4(st)
        if KPHASES >= 5:
            emit_cpath(st)

        block = st.block

        @block.sync
        def _(s):
            wt(s, st.vt)
            wt(s, st.tt)
            wt(s, st.ct)
            s.dma_start(out=st.out.rearrange("(t p) d -> p t d", p=128),
                        in_=st.X[:, :].rearrange("p (t d) -> p t d", d=D)).then_inc(st.ot[0], dinc(st.ot))
            s.wait_ge(st.ot[0], 16)

        if debug:
            @block.scalar
            def _(s):
                wt(s, st.vt)
                wt(s, st.tt)
                s.dma_start(out=st.dbg["d_xL0"], in_=st.X[:, :]).then_inc(st.sd[0], dinc(st.sd))
                s.wait_ge(st.sd[0], st.sd[1])

    return nc


def emit_p0(st):
    nc, block = st.nc, st.block

    @block.sync
    def _(s):
        s.dma_start(out=st.X[:, :].rearrange("p (t d) -> p t d", d=D),
                    in_=st.u.rearrange("(t p) d -> p t d", p=128)).then_inc(st.sy[0], dinc(st.sy))
        s.dma_start(out=st.idf[:, :], in_=st.identf).then_inc(st.sy[0], dinc(st.sy))
        s.dma_start(out=st.idb[:, :], in_=st.identb).then_inc(st.sy[0], dinc(st.sy))
        s.dma_start(out=st.msk[:, 0:256], in_=st.mask0).then_inc(st.sy[0], dinc(st.sy))
        s.dma_start(out=st.msk[:, 256:512], in_=st.mask1).then_inc(st.sy[0], dinc(st.sy))
        s.dma_start(out=st.sel16[:, :], in_=st.sel16i).then_inc(st.sy[0], dinc(st.sy))
        s.dma_start(out=st.sel64[:, :], in_=st.sel64i).then_inc(st.sy[0], dinc(st.sy))
        with nc.allow_non_contiguous_dma(reason="one-time small constant rearranges"):
            s.dma_start(out=st.rows[0:1, 0:1024], in_=st.xnw.unsqueeze(0)).then_inc(st.sy[0], dinc(st.sy))
            s.dma_start(out=st.rows[0:1, 1024:2048], in_=st.xnb.unsqueeze(0)).then_inc(st.sy[0], dinc(st.sy))
            s.dma_start(out=st.rows[0:1, 2048:3072], in_=st.fb.unsqueeze(0)).then_inc(st.sy[0], dinc(st.sy))
            s.dma_start(out=st.rows[0:1, 3072:3088], in_=st.Dp.unsqueeze(0)).then_inc(st.sy[0], dinc(st.sy))
            s.dma_start(out=st.cwT[:, :].rearrange("p (ct k) -> p ct k", k=4),
                        in_=st.convw.rearrange("(ct p) k -> p ct k", p=128)).then_inc(st.sy[0], dinc(st.sy))
            s.dma_start(out=st.cbT[:, :], in_=st.convb.rearrange("(ct p) -> p ct", p=128),
                        allow_slow_non_contiguous=True).then_inc(st.sy[0], dinc(st.sy))
            s.dma_start(out=st.rwT[:, :], in_=st.rmsw.rearrange("(i p) -> p i", p=128),
                        allow_slow_non_contiguous=True).then_inc(st.sy[0], dinc(st.sy))
            s.dma_start(out=st.dtbc[:, :], in_=st.dt_bias.unsqueeze(1)).then_inc(st.sy[0], dinc(st.sy))
            s.dma_start(out=st.Acol[:, :], in_=st.A_log.unsqueeze(1)).then_inc(st.sy[0], dinc(st.sy))
            s.dma_start(out=st.sm1[0:1, 0:1], in_=st.fda.unsqueeze(0)).then_inc(st.sy[0], dinc(st.sy))
    st.LOADS0 = st.sy[1]

    @block.vector
    def _(v):
        wt(v, st.sy, st.LOADS0)
        v.memset(st.ones1[:, :], 1.0)
        inc(st.vt, v.memset(st.onesc[:, :], 1.0))
    v_init = st.vt[1]

    @block.scalar
    def _(s):
        wt(s, st.sy, st.LOADS0)
        s.activation(out=st.Acol[:, :], in_=st.Acol[:, :], func=AF.Exp)
        s.activation(out=st.sm1[0:1, 1:2], in_=st.sm1[0:1, 0:1], func=AF.Tanh, scale=4.0)
        s.activation(out=st.sm1[0:1, 2:3], in_=st.sm1[0:1, 1:2], func=AF.Exp, scale=0.5 * LN2)
        inc(st.ct, s.activation(out=st.sm1[0:1, 3:4], in_=st.sm1[0:1, 1:2], func=AF.Exp, scale=-0.5 * LN2))
    s_da = st.ct[1]

    @block.vector
    def _(v):
        wt(v, st.ct, s_da)
        inc(st.vt, v.tensor_scalar_mul(st.Acol[:, :], st.Acol[:, :], -1.0))

    @block.tensor
    def _(t):
        wt(t, st.vt, v_init)
        wt(t, st.ct, s_da)
        t.matmul(st.pb[0][:, :], st.ones1[:, :], st.rows[0:1, 0:512], start=True, stop=True)
        t.matmul(st.pb[1][:, :], st.ones1[:, :], st.rows[0:1, 512:1024], start=True, stop=True)
        t.matmul(st.pb[2][:, :], st.ones1[:, :], st.rows[0:1, 1024:1536], start=True, stop=True)
        t.matmul(st.pb[3][:, :], st.ones1[:, :], st.rows[0:1, 1536:2048], start=True, stop=True)
        t.matmul(st.pb[4][:, :], st.ones1[:, :], st.rows[0:1, 2048:2560], start=True, stop=True)
        t.matmul(st.pb[5][:, :], st.ones1[:, :], st.rows[0:1, 2560:3072], start=True, stop=True)
        t.matmul(st.pb[6][:, 0:16], st.ones1[:, :], st.rows[0:1, 3072:3088], start=True, stop=True)
        inc(st.tt, t.matmul(st.pb[7][:, 0:2], st.ones1[:, :], st.sm1[0:1, 2:4], start=True, stop=True))
    t_bc = st.tt[1]
    st.t_bc = t_bc

    @block.vector
    def _(v):
        wt(v, st.tt, t_bc)
        v.tensor_copy(out=st.xwb[:, 0:512], in_=st.pb[0][:, :])
        v.tensor_copy(out=st.xwb[:, 512:1024], in_=st.pb[1][:, :])
        v.tensor_copy(out=st.xbb[:, 0:512], in_=st.pb[2][:, :])
        v.tensor_copy(out=st.xbb[:, 512:1024], in_=st.pb[3][:, :])
        v.tensor_copy(out=st.fbb[:, 0:512], in_=st.pb[4][:, :])
        v.tensor_copy(out=st.fbb[:, 512:1024], in_=st.pb[5][:, :])
        v.tensor_copy(out=st.DB[:, :], in_=st.pb[6][:, 0:16])
        v.tensor_scalar(out=st.sm2[:, 0:1], in0=st.pb[7][:, 0:1], scalar1=0.25, scalar2=None, op0=AL.mult)
        v.tensor_scalar(out=st.sm2[:, 1:2], in0=st.pb[7][:, 1:2], scalar1=4.0, scalar2=None, op0=AL.mult)
        v.memset(st.hist[:, :], 0.0)
        v.memset(st.dtp[:, :], 0.0)
        inc(st.vt, v.memset(st.PS[:, :], 0.0))


def emit_norm_scale(st, natv, nat_rows, two_cols, extra_mult, dma_done):
    nc, block = st.nc, st.block

    @block.scalar
    def _(s):
        wt(s, st.sy, dma_done)
        s.activation(out=st.q1[0:nat_rows, :], in_=natv[:, 0:1024], func=AF.Square,
                     accum_out=st.psc[0:nat_rows, 0:1])
        if two_cols:
            s.activation(out=st.q1[0:nat_rows, :], in_=natv[:, 1024:2048], func=AF.Square,
                         accum_out=st.psc[0:nat_rows, 6:7])
        inc(st.ct, s.activation(out=st.psc[0:nat_rows, 7:8], in_=st.psc[0:nat_rows, 0:1], func=AF.Identity))
    s_sq = st.ct[1]

    @block.vector
    def _(v):
        wt(v, st.ct, s_sq)
        if two_cols:
            inc(st.vt, v.tensor_tensor(out=st.psc[0:nat_rows, 0:1], in0=st.psc[0:nat_rows, 0:1],
                                       in1=st.psc[0:nat_rows, 6:7], op=AL.add))
            wt(v, st.vt)
        inc(st.vt, v.tensor_scalar(out=st.psc[0:nat_rows, 1:2], in0=st.psc[0:nat_rows, 0:1],
                                   scalar1=1e-40, scalar2=None, op0=AL.max))
    v_m = st.vt[1]

    @block.scalar
    def _(s):
        wt(s, st.vt, v_m)
        s.activation(out=st.psc[0:nat_rows, 2:3], in_=st.psc[0:nat_rows, 1:2], func=AF.Ln)
        inc(st.ct, s.activation(out=st.psc[0:nat_rows, 3:4], in_=st.psc[0:nat_rows, 2:3],
                                func=AF.Exp, scale=-0.5))
    s_seed = st.ct[1]

    @block.vector
    def _(v):
        wt(v, st.ct, s_seed)
        y = st.psc[0:nat_rows, 3:4]
        for _i in range(2):
            inc(st.vt, v.tensor_tensor(out=st.psc[0:nat_rows, 4:5], in0=y, in1=y, op=AL.mult))
            wt(v, st.vt)
            inc(st.vt, v.tensor_tensor(out=st.psc[0:nat_rows, 4:5], in0=st.psc[0:nat_rows, 4:5],
                                       in1=st.psc[0:nat_rows, 1:2], op=AL.mult))
            wt(v, st.vt)
            inc(st.vt, v.tensor_scalar(out=st.psc[0:nat_rows, 4:5], in0=st.psc[0:nat_rows, 4:5],
                                       scalar1=-0.5, scalar2=1.5, op0=AL.mult, op1=AL.add))
            wt(v, st.vt)
            inc(st.vt, v.tensor_tensor(out=y, in0=y, in1=st.psc[0:nat_rows, 4:5], op=AL.mult))
            wt(v, st.vt)
        inc(st.vt, v.tensor_scalar(out=st.psc[0:nat_rows, 5:6], in0=y,
                                   scalar1=1e6, scalar2=extra_mult, op0=AL.min, op1=AL.mult))


def emit_prep_store(st, src_ap, nat_rows, dcols, dst, col0):
    nc, block = st.nc, st.block
    ncolt = dcols // 128
    for dc in range(ncolt):
        bh = st.pb[4 + (dc % 2)]

        @block.tensor
        def _(t, dc=dc, bh=bh):
            wt(t, st.vt)
            inc(st.tt, t.transpose(bh[:, 0:nat_rows],
                                   src_ap[0:nat_rows, dc * 128:(dc + 1) * 128],
                                   st.idf[0:nat_rows, 0:nat_rows]))
        t_tp = st.tt[1]

        @block.vector
        def _(v, dc=dc, bh=bh, t_tp=t_tp):
            wt(v, st.tt, t_tp)
            if dc == 0:
                wt(v, st.sd, st.ptp_free)
            inc(st.vt, v.tensor_copy(out=st.ptp[:, dc * nat_rows:(dc + 1) * nat_rows],
                                     in_=bh[:, 0:nat_rows]))
    v_done = st.vt[1]

    @block.scalar
    def _(s):
        wt(s, st.vt, v_done)
        s.dma_start(out=dst[0:dcols, col0:col0 + nat_rows].rearrange("(dc p) o -> p dc o", p=128),
                    in_=st.ptp[:, 0:ncolt * nat_rows]).then_inc(st.sd[0], dinc(st.sd))
    st.ptp_free = st.sd[1]


def emit_prep(st):
    nc, block = st.nc, st.block
    st.ptp_free = 0

    for ot in range(34):
        nat_rows = 128 if ot < 33 else 16
        par = ot % 2
        natv = st.pnat[0:nat_rows, par * 2048: par * 2048 + 1024]

        @block.sync
        def _(s, ot=ot, natv=natv, nat_rows=nat_rows):
            if ot == 0:
                wt(s, st.tt, st.t_bc)
            if ot >= 2:
                wt(s, st.vt)
            s.dma_start(out=natv, in_=st.ipw[ot * 128: ot * 128 + nat_rows, :]).then_inc(st.sy[0], dinc(st.sy))
        emit_norm_scale(st, natv, nat_rows, False, float(2.0 ** -5), st.sy[1])

        @block.vector
        def _(v, natv=natv, nat_rows=nat_rows):
            wt(v, st.vt)
            inc(st.vt, v.tensor_scalar(out=natv, in0=natv, scalar1=st.psc[0:nat_rows, 5:6],
                                       scalar2=None, op0=AL.mult))
        emit_prep_store(st, natv, nat_rows, 1024, st.s_ipW, ot * 128)

    for ot in range(8):
        par = ot % 2
        natv = st.pnat[:, par * 2048:(par + 1) * 2048]

        @block.sync
        def _(s, ot=ot, natv=natv):
            wt(s, st.vt)
            s.dma_start(out=natv, in_=st.opw[ot * 128:(ot + 1) * 128, :]).then_inc(st.sy[0], dinc(st.sy))
        emit_norm_scale(st, natv, 128, True, 1.0, st.sy[1])

        @block.vector
        def _(v, natv=natv):
            inc(st.vt, v.tensor_scalar(out=natv, in0=natv, scalar1=st.psc[:, 5:6],
                                       scalar2=None, op0=AL.mult))
        emit_prep_store(st, natv, 128, 2048, st.s_opW, ot * 128)

    for ot in range(8):
        par = ot % 2
        natv = st.pnat[:, par * 2048: par * 2048 + 1024]

        @block.sync
        def _(s, ot=ot, natv=natv):
            wt(s, st.vt)
            s.dma_start(out=natv, in_=st.fw[ot * 128:(ot + 1) * 128, :]).then_inc(st.sy[0], dinc(st.sy))
            s.dma_start(out=st.qs[:, 96:128], in_=st.fdw[ot * 128:(ot + 1) * 128, :]).then_inc(st.sy[0], dinc(st.sy))
        dma_d = st.sy[1]

        @block.scalar
        def _(s, dma_d=dma_d):
            wt(s, st.sy, dma_d)
            s.activation(out=st.qs[:, 128:160] if False else st.q1[:, 0:32],
                         in_=st.qs[:, 96:128], func=AF.Tanh, scale=4.0)
            s.activation(out=st.q2[:, 0:32], in_=st.q1[:, 0:32], func=AF.Exp, scale=0.5 * LN2)
            inc(st.ct, s.activation(out=st.q2[:, 32:64], in_=st.q1[:, 0:32], func=AF.Exp, scale=-0.5 * LN2))
        s_dd = st.ct[1]

        @block.vector
        def _(v, natv=natv, s_dd=s_dd):
            wt(v, st.ct, s_dd)
            v.tensor_scalar(out=st.cin[:, 32:64], in0=st.q2[:, 0:32], scalar1=0.25,
                            scalar2=None, op0=AL.mult)
            v.tensor_scalar(out=st.cin[:, 0:32], in0=st.q2[:, 32:64], scalar1=4.0,
                            scalar2=None, op0=AL.mult)
            mxfp4_quant_emit(v, st, natv, 128, 1024, -16, -1,
                             st.cin[:, 32:64], st.cin[:, 0:32], st.xn0c[:, 0:1024])
            inc(st.vt, v.engine_nop())
        emit_prep_store(st, st.xn0c[:, 0:1024], 128, 1024, st.s_fqW, ot * 128)

    st.prep_sd = st.sd[1]


def emit_ln(st, src_ap, dst_ap, w_bcast, b_bcast, ncols, dump=None):
    nc, block = st.nc, st.block

    if dump is None:
        dump = st.lndump

    @block.scalar
    def _(s):
        wt(s, st.vt)
        inc(st.ct, s.activation(out=dump[:, 0:ncols], in_=src_ap, func=AF.Identity,
                                accum_out=st.sm2[:, 4:5]))
    s_sum = st.ct[1]

    @block.vector
    def _(v):
        wt(v, st.ct, s_sum)
        inc(st.vt, v.tensor_scalar_mul(st.sm2[:, 4:5], st.sm2[:, 4:5], -1.0 / ncols))
    v_mu = st.vt[1]

    @block.scalar
    def _(s):
        wt(s, st.vt, v_mu)
        inc(st.ct, s.activation(out=dump[:, 0:ncols], in_=src_ap, func=AF.Square,
                                bias=st.sm2[:, 4:5], accum_out=st.sm2[:, 5:6]))
    s_sq = st.ct[1]

    @block.vector
    def _(v):
        wt(v, st.ct, s_sq)
        inc(st.vt, v.tensor_scalar(out=st.sm2[:, 6:7], in0=st.sm2[:, 5:6], scalar1=1.0 / ncols,
                                   scalar2=LN_EPS, op0=AL.mult, op1=AL.add))
    v_v = st.vt[1]

    @block.scalar
    def _(s):
        wt(s, st.vt, v_v)
        s.activation(out=st.sm2[:, 7:8], in_=st.sm2[:, 6:7], func=AF.Ln)
        inc(st.ct, s.activation(out=st.sm2[:, 5:6], in_=st.sm2[:, 7:8], func=AF.Exp, scale=-0.5))
    s_seed = st.ct[1]

    @block.vector
    def _(v):
        wt(v, st.ct, s_seed)
        y = st.sm2[:, 5:6]
        for _i in range(2):
            inc(st.vt, v.tensor_tensor(out=st.sm2[:, 7:8], in0=y, in1=y, op=AL.mult))
            wt(v, st.vt)
            inc(st.vt, v.tensor_tensor(out=st.sm2[:, 7:8], in0=st.sm2[:, 7:8], in1=st.sm2[:, 6:7], op=AL.mult))
            wt(v, st.vt)
            inc(st.vt, v.tensor_scalar(out=st.sm2[:, 7:8], in0=st.sm2[:, 7:8], scalar1=-0.5, scalar2=1.5,
                                       op0=AL.mult, op1=AL.add))
            wt(v, st.vt)
            inc(st.vt, v.tensor_tensor(out=y, in0=y, in1=st.sm2[:, 7:8], op=AL.mult))
            wt(v, st.vt)
        v.tensor_scalar(out=dst_ap, in0=src_ap, scalar1=st.sm2[:, 4:5], scalar2=None, op0=AL.add)
        v.tensor_scalar(out=dst_ap, in0=dst_ap, scalar1=y, scalar2=None, op0=AL.mult)
        if w_bcast is not None:
            v.tensor_tensor(out=dst_ap, in0=dst_ap, in1=w_bcast, op=AL.mult)
            v.tensor_tensor(out=dst_ap, in0=dst_ap, in1=b_bcast, op=AL.add)
        inc(st.vt, v.engine_nop())


def emit_l0_chunk(st, c):
    nc, block = st.nc, st.block
    xsl = st.X[:, (2 * c) * D:(2 * c + 2) * D]

    @block.tensor
    def _(t):
        wt(t, st.vt)
        wt(t, st.ct)
        wt(t, st.ipe)
        inc(st.tt, t.matmul(st.pb[7][0:1, 4:5], st.ones1[:, 0:1], st.ones1[0:1, 0:1], start=True, stop=True))

    @block.vector
    def _(v):
        wt(v, st.ct)
        wt(v, st.tt)
        wt(v, st.ipc)
        wt(v, st.opc)
        wt(v, st.fqc)
        inc(st.vt, v.engine_nop())

    @block.scalar
    def _(s):
        wt(s, st.vt)
        wt(s, st.tt)
        inc(st.ct, s.activation(out=st.sm1[0:1, 8:9], in_=st.sm1[0:1, 0:1], func=AF.Identity))

    for tsub in range(2):
        emit_ln(st, xsl[:, tsub * D:(tsub + 1) * D], st.xn0c[:, tsub * D:(tsub + 1) * D], None, None, D)
        emit_ln(st, st.xn0c[:, tsub * D:(tsub + 1) * D], st.uc[:, tsub * D:(tsub + 1) * D],
                st.xwb[:, :], st.xbb[:, :], D)

    @block.vector
    def _(v):
        if c > 0:
            wt(v, st.ipc, 34 * c)
        inc(st.vt, v.engine_nop())
    v_usp = st.vt[1]

    if st.debug and c == 0:
        @block.scalar
        def _(s):
            wt(s, st.vt, v_usp)
            s.dma_start(out=st.dbg["d_xn0"], in_=st.xn0c[:, :]).then_inc(st.sd[0], dinc(st.sd))
            s.dma_start(out=st.dbg["d_u"], in_=st.uc[:, :]).then_inc(st.sd[0], dinc(st.sd))

        @block.vector
        def _(v):
            wt(v, st.sd)
            inc(st.vt, v.engine_nop())


    for i in range(16):
        tsub, dc = divmod(i, 8)
        bnk = st.pb[4 + (i % 2)]

        @block.tensor
        def _(t, i=i, tsub=tsub, dc=dc, bnk=bnk):
            wt(t, st.vt)
            inc(st.tt, t.transpose(bnk[:, 0:128],
                                   st.uc[:, tsub * 1024 + dc * 128: tsub * 1024 + (dc + 1) * 128],
                                   st.idf[:, :]))
        t_tp = st.tt[1]

        @block.vector
        def _(v, tsub=tsub, dc=dc, bnk=bnk, t_tp=t_tp):
            wt(v, st.tt, t_tp)
            inc(st.vt, v.tensor_copy(out=st.uT[:, dc * 256 + tsub * 128: dc * 256 + (tsub + 1) * 128],
                                     in_=bnk[:, 0:128]))
    v_uT = st.vt[1]

    for ot in range(34):
        nat_rows = 128 if ot < 33 else 16
        slot = ot % 2
        unit = c * 34 + ot

        @block.sync
        def _(s, ot=ot, slot=slot, unit=unit, nat_rows=nat_rows):
            if unit == 0:
                wt(s, st.sd, st.prep_sd)
            if unit >= 2:
                wt(s, st.ipc, unit - 1)
            s.dma_start(out=st.wip[:, slot * 1024: slot * 1024 + 8 * nat_rows],
                        in_=st.s_ipW[:, ot * 128: ot * 128 + nat_rows].rearrange("(dc p) o -> p dc o", p=128)).then_inc(st.sy[0], dinc(st.sy))
        w_done = st.sy[1]

        bnk = st.pb[ot % 2]

        @block.tensor
        def _(t, ot=ot, slot=slot, unit=unit, nat_rows=nat_rows, bnk=bnk, w_done=w_done):
            wt(t, st.sy, w_done)
            if ot == 0:
                wt(t, st.vt, v_uT)
            if unit >= 2:
                wt(t, st.ipe, unit - 1)
            last = None
            for dc in range(8):
                wh = st.wip[:, slot * 1024 + dc * nat_rows: slot * 1024 + (dc + 1) * nat_rows]
                uh = st.uT[:, dc * 256:(dc + 1) * 256]
                last = t.matmul(bnk[0:nat_rows, 0:256], wh, uh, start=(dc == 0), stop=(dc == 7))
            inc(st.ipc, last)

        if ot < 16:
            @block.scalar
            def _(s, ot=ot, unit=unit, bnk=bnk):
                wt(s, st.ipc, unit + 1)
                a = s.activation(out=st.gT[:, ot * 256:(ot + 1) * 256], in_=bnk[:, 0:256], func=AF.Silu)
                inc(st.ipe, a)
        elif ot < 33:
            @block.vector
            def _(v, ot=ot, unit=unit, bnk=bnk):
                wt(v, st.ipc, unit + 1)
                inc(st.ipe, v.tensor_copy(out=st.xBCc[:, (ot - 16) * 256:(ot - 15) * 256], in_=bnk[:, 0:256]))
        else:
            @block.vector
            def _(v, unit=unit, bnk=bnk):
                wt(v, st.ipc, unit + 1)
                inc(st.ipe, v.tensor_copy(out=st.dtp[0:16, 0:256], in_=bnk[0:16, 0:256]))
    ip_done = st.ipe[1]

    if st.debug and c == 1:
        @block.scalar
        def _(s):
            wt(s, st.ipe, ip_done)
            s.dma_start(out=st.dbg["d_xBC"], in_=st.xBCc[:, :]).then_inc(st.sd[0], dinc(st.sd))

        @block.vector
        def _(v):
            wt(v, st.sd)
            inc(st.vt, v.engine_nop())

    @block.vector
    def _(v):
        wt(v, st.ipe, ip_done)
        for ct in range(17):
            tile = st.xBCc[:, ct * 256:(ct + 1) * 256]
            v.tensor_copy(out=st.cin[:, 0:3], in_=st.hist[:, ct * 3:(ct + 1) * 3])
            inc(st.vt, v.tensor_copy(out=st.cin[:, 3:259], in_=tile))
            wt(v, st.vt)
            v.tensor_scalar(out=tile, in0=st.cin[:, 0:256], scalar1=st.cwT[:, 4 * ct:4 * ct + 1],
                            scalar2=None, op0=AL.mult)
            for k in range(1, 4):
                v.scalar_tensor_tensor(out=tile, in0=st.cin[:, k:k + 256],
                                       scalar=st.cwT[:, 4 * ct + k:4 * ct + k + 1],
                                       in1=tile, op0=AL.mult, op1=AL.add)
            v.tensor_copy(out=st.hist[:, ct * 3:(ct + 1) * 3], in_=st.cin[:, 256:259])
        inc(st.vt, v.engine_nop())
    v_conv = st.vt[1]

    @block.scalar
    def _(s):
        wt(s, st.vt, v_conv)
        for ct in range(17):
            tile = st.xBCc[:, ct * 256:(ct + 1) * 256]
            s.activation(out=tile, in_=tile, func=AF.Silu, bias=st.cbT[:, ct:ct + 1])
        inc(st.ct, s.activation(out=st.dtp[0:16, 256:512], in_=st.dtp[0:16, 0:256],
                                func=AF.Abs, bias=st.dtbc[:, :]))
    s_conv = st.ct[1]

    if st.debug and c == 0:
        @block.scalar
        def _(s):
            s.dma_start(out=st.dbg["d_gT"], in_=st.gT[:, :]).then_inc(st.sd[0], dinc(st.sd))

        @block.vector
        def _(v):
            wt(v, st.sd)
            inc(st.vt, v.engine_nop())

    if st.debug and c == 1:
        @block.vector
        def _(v):
            wt(v, st.ct, s_conv)
            inc(st.vt, v.engine_nop())

        @block.scalar
        def _(s):
            wt(s, st.vt)
            s.dma_start(out=st.dbg["d_xBCp"], in_=st.xBCc[:, :]).then_inc(st.sd[0], dinc(st.sd))

        @block.vector
        def _(v):
            wt(v, st.sd)
            inc(st.vt, v.engine_nop())

    @block.vector
    def _(v):
        wt(v, st.ct, s_conv)
        inc(st.vt, v.tensor_scalar(out=st.dtp[0:16, 0:256], in0=st.dtp[0:16, 0:256],
                                   scalar1=st.dtbc[:, :], scalar2=None, op0=AL.add))
    v_dtx = st.vt[1]

    @block.scalar
    def _(s):
        wt(s, st.vt, v_dtx)
        s.activation(out=st.dtp[0:16, 512:768], in_=st.dtp[0:16, 256:512], func=AF.Exp, scale=-1.0)
        inc(st.ct, s.activation(out=st.dtp[0:16, 512:768], in_=st.dtp[0:16, 512:768], func=AF.Ln, bias=1.0))
    s_dt1 = st.ct[1]

    @block.vector
    def _(v):
        wt(v, st.ct, s_dt1)
        v.scalar_tensor_tensor(out=st.dtp[0:16, 256:512], in0=st.dtp[0:16, 0:256], scalar=0.0,
                               in1=st.dtp[0:16, 512:768], op0=AL.max, op1=AL.add)
        v.tensor_scalar(out=st.dtp[0:16, 512:768], in0=st.dtp[0:16, 256:512],
                        scalar1=st.Acol[:, :], scalar2=None, op0=AL.mult)
        v.tensor_tensor_scan(out=st.dtp[0:16, 768:1024], data0=st.dtp[0:16, 512:768],
                             data1=st.dtp[0:16, 512:768], initial=0.0, op0=AL.add, op1=AL.bypass)
        inc(st.vt, v.tensor_scalar_mul(st.dtp[0:16, 1024:1280], st.dtp[0:16, 768:1024], -1.0))
    v_dt2 = st.vt[1]

    @block.scalar
    def _(s):
        wt(s, st.vt, v_dt2)
        s.activation(out=st.dtp[0:16, 1280:1536], in_=st.dtp[0:16, 1024:1280], func=AF.Exp,
                     bias=st.dtp[0:16, 1023:1024])
        s.activation(out=st.dtp[0:16, 1536:1792], in_=st.dtp[0:16, 768:1024], func=AF.Exp)
        inc(st.ct, s.activation(out=st.dtp[0:16, 1792:1793], in_=st.dtp[0:16, 1023:1024], func=AF.Exp))
    s_dt3 = st.ct[1]

    @block.vector
    def _(v):
        wt(v, st.ct, s_dt3)
        inc(st.vt, v.tensor_tensor(out=st.dtp[0:16, 1280:1536], in0=st.dtp[0:16, 1280:1536],
                                   in1=st.dtp[0:16, 256:512], op=AL.mult))
    v_dt4 = st.vt[1]

    @block.tensor
    def _(t):
        wt(t, st.vt, v_dt4)
        for js in range(2):
            t.transpose(st.pb[2 + js][:, 0:16], st.dtp[0:16, 256 + js * 128:256 + (js + 1) * 128],
                        st.idf[0:16, 0:16])
            t.transpose(st.pb[4 + js][:, 0:16], st.dtp[0:16, 1280 + js * 128:1280 + (js + 1) * 128],
                        st.idf[0:16, 0:16])
            t.transpose(st.pb[6 + js][:, 0:16], st.dtp[0:16, 1024 + js * 128:1024 + (js + 1) * 128],
                        st.idf[0:16, 0:16])
        inc(st.tt, t.transpose(st.pb[2][0:1, 16:32], st.dtp[0:16, 1792:1793], st.idf[0:16, 0:16]))
    t_dts = st.tt[1]

    @block.vector
    def _(v):
        wt(v, st.tt, t_dts)
        for js in range(2):
            v.tensor_copy(out=st.sS[:, js * 16:(js + 1) * 16], in_=st.pb[2 + js][:, 0:16])
            v.tensor_copy(out=st.sS[:, 32 + js * 16:32 + (js + 1) * 16], in_=st.pb[4 + js][:, 0:16])
            v.tensor_copy(out=st.sS[:, 64 + js * 16:64 + (js + 1) * 16], in_=st.pb[6 + js][:, 0:16])
        inc(st.vt, v.tensor_copy(out=st.srow[0:1, 0:16], in_=st.pb[2][0:1, 16:32]))
    v_dts = st.vt[1]

    @block.tensor
    def _(t):
        wt(t, st.vt, v_dts)
        inc(st.tt, t.matmul(st.pb[3][:, 16:32], st.ones1[:, :], st.srow[0:1, 0:16], start=True, stop=True))
    t_cd = st.tt[1]

    @block.tensor
    def _(t):
        wt(t, st.ct, s_conv)
        inc(st.tt, t.matmul(st.pb[6][0:64, 0:256], st.sel64[:, :],
                            st.xBCc[:, 16 * 256:17 * 256], start=True, stop=True))
    t_cf = st.tt[1]

    @block.vector
    def _(v):
        wt(v, st.tt, t_cf)
        wt(v, st.tt, t_cd)
        v.tensor_copy(out=st.Cf32[:, :], in_=st.pb[6][0:64, 0:256])
        inc(st.vt, v.tensor_copy(out=st.cdB[:, :], in_=st.pb[3][:, 16:32]))
    v_bc = st.vt[1]

    @block.tensor
    def _(t):
        wt(t, st.vt, v_bc)
        t.transpose(st.pb[4][:, 0:64], st.xBCc[0:64, 16 * 256 + 0 * 128:16 * 256 + 1 * 128], st.idf[0:64, 0:64])
        inc(st.tt, t.transpose(st.pb[5][:, 0:64], st.xBCc[0:64, 16 * 256 + 1 * 128:16 * 256 + 2 * 128],
                               st.idf[0:64, 0:64]))
    t_btok = st.tt[1]

    @block.vector
    def _(v):
        wt(v, st.tt, t_btok)
        v.tensor_copy(out=st.btok[:, 0:64], in_=st.pb[4][:, 0:64])
        inc(st.vt, v.tensor_copy(out=st.btok[:, 64:128], in_=st.pb[5][:, 0:64]))
    v_btok = st.vt[1]

    if st.debug and c == 0:
        @block.scalar
        def _(s):
            wt(s, st.vt, v_btok)
            s.dma_start(out=st.dbg["d_dt"], in_=st.dtp[:, :]).then_inc(st.sd[0], dinc(st.sd))

        @block.vector
        def _(v):
            wt(v, st.sd)
            inc(st.vt, v.engine_nop())

    if st.debug and c == 1:
        @block.scalar
        def _(s):
            wt(s, st.vt, v_btok)
            s.dma_start(out=st.dbg["d_dt1"], in_=st.dtp[:, :]).then_inc(st.sd[0], dinc(st.sd))

        @block.vector
        def _(v):
            wt(v, st.sd)
            inc(st.vt, v.engine_nop())

    for h in range(NH):
        xh = st.xBCc[:, h * 256:(h + 1) * 256]

        @block.vector
        def _(v, h=h):
            wt(v, st.vt)
            v.tensor_scalar(out=st.wBh[:, 0:64], in0=st.btok[:, 0:64],
                            scalar1=st.sS[:, 32 + h:32 + h + 1], scalar2=None, op0=AL.mult)
            inc(st.vt, v.tensor_scalar(out=st.wBh[:, 64:128], in0=st.btok[:, 64:128],
                                       scalar1=st.sS[:, 48 + h:48 + h + 1], scalar2=None, op0=AL.mult))
        v_wb = st.vt[1]

        @block.tensor
        def _(t, h=h, xh=xh, v_wb=v_wb):
            wt(t, st.vt, v_wb)
            t.transpose(st.pb[4][:, 0:128], xh[:, 0:128], st.idf[:, :])
            inc(st.tt, t.transpose(st.pb[4][:, 128:256], xh[:, 128:256], st.idf[:, :]))
            BT = st.xBCc[0:64, 16 * 256:17 * 256]
            for js in range(2):
                t.matmul(st.pb[js][:, 0:256], BT[:, js * 128:(js + 1) * 128],
                         st.Cf32[:, :], start=True, stop=True)
            t.matmul(st.pb[2][:, 0:256], st.sel16[:, h * 128:(h + 1) * 128],
                     st.dtp[0:16, 768:1024], start=True, stop=True)
            inc(st.tt, t.matmul(st.pb[3][:, 0:256], st.sel16[:, h * 128:(h + 1) * 128],
                                st.dtp[0:16, 1536:1792], start=True, stop=True))
        t_sd = st.tt[1]

        @block.vector
        def _(v, h=h, t_sd=t_sd):
            wt(v, st.tt, t_sd)
            v.tensor_copy(out=st.xhT[:, 0:128], in_=st.pb[4][:, 0:128])
            inc(st.vt, v.tensor_copy(out=st.xhT[:, 128:256], in_=st.pb[4][:, 128:256]))
        v_xh = st.vt[1]

        for js in range(2):
            @block.vector
            def _(v, h=h, js=js, t_sd=t_sd):
                wt(v, st.tt, t_sd)
                inc(st.vt, v.tensor_scalar(out=st.Eb[:, :], in0=st.pb[2][:, 0:256],
                                           scalar1=st.sS[:, 64 + js * 16 + h:64 + js * 16 + h + 1],
                                           scalar2=0.0, op0=AL.add, op1=AL.min))
            v_d2 = st.vt[1]

            @block.scalar
            def _(s, v_d2=v_d2):
                wt(s, st.vt, v_d2)
                inc(st.ct, s.activation(out=st.Eb[:, :], in_=st.Eb[:, :], func=AF.Exp))
            s_e = st.ct[1]

            @block.vector
            def _(v, h=h, js=js, s_e=s_e):
                wt(v, st.ct, s_e)
                v.scalar_tensor_tensor(out=st.attb[:, js * 256:(js + 1) * 256], in0=st.Eb[:, :],
                                       scalar=st.sS[:, js * 16 + h:js * 16 + h + 1],
                                       in1=st.pb[js][:, 0:256], op0=AL.mult, op1=AL.mult)
                inc(st.vt, v.tensor_tensor(out=st.attb[:, js * 256:(js + 1) * 256],
                                           in0=st.attb[:, js * 256:(js + 1) * 256],
                                           in1=st.msk[:, js * 256:(js + 1) * 256], op=AL.mult))
        v_att = st.vt[1]

        @block.tensor
        def _(t, h=h, v_att=v_att, v_xh=v_xh):
            wt(t, st.vt, v_att)
            for js in range(2):
                t.matmul(st.pb[5][:, 0:256], st.xhT[:, js * 128:(js + 1) * 128],
                         st.attb[:, js * 256:(js + 1) * 256], start=(js == 0), stop=(js == 1))
            for js in range(2):
                t.matmul(st.pb[6][0:64, 0:128], st.wBh[:, js * 64:(js + 1) * 64],
                         st.xhT[:, js * 128:(js + 1) * 128], start=(js == 0), stop=(js == 1))
            inc(st.tt, t.matmul(st.pb[7][:, 0:256], st.PS[0:64, h * 128:(h + 1) * 128],
                                st.Cf32[:, :], start=True, stop=True))
        t_y = st.tt[1]

        @block.vector
        def _(v, h=h, t_y=t_y):
            wt(v, st.tt, t_y)
            v.tensor_copy(out=st.attb[:, 0:256], in_=st.pb[3][:, 0:256])
            v.tensor_tensor(out=st.Eb[:, :], in0=st.pb[7][:, 0:256], in1=st.attb[:, 0:256], op=AL.mult)
            v.tensor_tensor(out=st.Eb[:, :], in0=st.Eb[:, :], in1=st.pb[5][:, 0:256], op=AL.add)
            v.scalar_tensor_tensor(out=st.Eb[:, :], in0=st.xBCc[:, h * 256:(h + 1) * 256],
                                   scalar=st.DB[:, h:h + 1], in1=st.Eb[:, :], op0=AL.mult, op1=AL.add)
            v.tensor_tensor(out=st.gT[:, h * 256:(h + 1) * 256], in0=st.Eb[:, :],
                            in1=st.gT[:, h * 256:(h + 1) * 256], op=AL.mult)
            inc(st.vt, v.scalar_tensor_tensor(out=st.PS[0:64, h * 128:(h + 1) * 128],
                                              in0=st.PS[0:64, h * 128:(h + 1) * 128],
                                              scalar=st.cdB[0:64, h:h + 1],
                                              in1=st.pb[6][0:64, 0:128], op0=AL.mult, op1=AL.add))
    v_scan = st.vt[1]

    for dc in range(16):
        bq = st.q1 if dc % 2 == 0 else st.q2

        @block.scalar
        def _(s, dc=dc, bq=bq, v_scan=v_scan):
            if dc == 0:
                wt(s, st.vt, v_scan)
            if dc >= 2:
                wt(s, st.tt)
            inc(st.ct, s.activation(out=bq[:, 0:256], in_=st.gT[:, dc * 256:(dc + 1) * 256], func=AF.Square))
        s_gsq = st.ct[1]

        @block.tensor
        def _(t, dc=dc, bq=bq, s_gsq=s_gsq):
            wt(t, st.ct, s_gsq)
            inc(st.tt, t.matmul(st.pb[4][0:1, 0:256], st.onesc[:, :], bq[:, 0:256],
                                start=(dc == 0), stop=(dc == 15)))
    t_ss = st.tt[1]

    @block.vector
    def _(v):
        wt(v, st.tt, t_ss)
        inc(st.vt, v.tensor_scalar(out=st.sm1[0:1, 16:272], in0=st.pb[4][0:1, 0:256],
                                   scalar1=1.0 / DI, scalar2=RMS_EPS, op0=AL.mult, op1=AL.add))
    v_rp = st.vt[1]

    @block.scalar
    def _(s):
        wt(s, st.vt, v_rp)
        s.activation(out=st.srow[0:1, 0:256], in_=st.sm1[0:1, 16:272], func=AF.Ln)
        inc(st.ct, s.activation(out=st.srow[0:1, 0:256], in_=st.srow[0:1, 0:256], func=AF.Exp, scale=-0.5))
    s_rs = st.ct[1]

    @block.vector
    def _(v):
        wt(v, st.ct, s_rs)
        y = st.srow[0:1, 0:256]
        a = st.sm1[0:1, 16:272]
        w2 = st.sm1[0:1, 272:528]
        for _i in range(2):
            v.tensor_tensor(out=w2, in0=y, in1=y, op=AL.mult)
            v.tensor_tensor(out=w2, in0=w2, in1=a, op=AL.mult)
            v.tensor_scalar(out=w2, in0=w2, scalar1=-0.5, scalar2=1.5, op0=AL.mult, op1=AL.add)
            v.tensor_tensor(out=y, in0=y, in1=w2, op=AL.mult)
        inc(st.vt, v.engine_nop())
    v_rms = st.vt[1]

    @block.tensor
    def _(t):
        wt(t, st.vt, v_rms)
        inc(st.tt, t.matmul(st.pb[3][:, 0:256], st.ones1[:, :], st.srow[0:1, 0:256], start=True, stop=True))
    t_sb = st.tt[1]

    for dcq in range(4):
        slot_g = dcq % 2

        @block.vector
        def _(v, dcq=dcq, slot_g=slot_g, t_sb=t_sb):
            if dcq == 0:
                wt(v, st.tt, t_sb)
            if dcq >= 2:
                wt(v, st.opc, c * 16 + (dcq - 1) * 4)
            for dcl in range(4):
                dc = dcq * 4 + dcl
                v.tensor_scalar(out=st.ghl[:, slot_g * 1024 + dcl * 256: slot_g * 1024 + (dcl + 1) * 256],
                                in0=st.gT[:, dc * 256:(dc + 1) * 256],
                                scalar1=st.rwT[:, dc:dc + 1], scalar2=None, op0=AL.mult)
            for dcl in range(4):
                v.tensor_tensor(out=st.ghl[:, slot_g * 1024 + dcl * 256: slot_g * 1024 + (dcl + 1) * 256],
                                in0=st.ghl[:, slot_g * 1024 + dcl * 256: slot_g * 1024 + (dcl + 1) * 256],
                                in1=st.pb[3][:, 0:256], op=AL.mult)
            inc(st.vt, v.engine_nop())
        v_spl = st.vt[1]

        for oc in range(2):
            for dh in range(2):
                unit = c * 16 + dcq * 4 + oc * 2 + dh
                slot_w = unit % 2
                r0 = dcq * 512 + dh * 256

                @block.sync
                def _(s, oc=oc, unit=unit, slot_w=slot_w, r0=r0):
                    if unit >= 2:
                        wt(s, st.opc, unit - 1)
                    s.dma_start(out=st.wop[:, slot_w * 1024: slot_w * 1024 + 1024],
                                in_=st.s_opW[r0:r0 + 256, oc * 512:(oc + 1) * 512]
                                .rearrange("(dc p) o -> p dc o", p=128)).then_inc(st.sy[0], dinc(st.sy))
                w_done = st.sy[1]

                @block.tensor
                def _(t, dcq=dcq, oc=oc, dh=dh, unit=unit, slot_w=slot_w, slot_g=slot_g,
                      w_done=w_done, v_spl=v_spl):
                    wt(t, st.sy, w_done)
                    wt(t, st.vt, v_spl)
                    last = None
                    for i in range(2):
                        dc = dcq * 4 + dh * 2 + i
                        dcl = dh * 2 + i
                        wh = st.wop[:, slot_w * 1024 + i * 512: slot_w * 1024 + (i + 1) * 512]
                        for tsub in range(2):
                            gh = st.ghl[:, slot_g * 1024 + dcl * 256 + tsub * 128:
                                        slot_g * 1024 + dcl * 256 + (tsub + 1) * 128]
                            bnk = st.pb[4 + 2 * oc + tsub]
                            last = t.matmul(bnk[:, 0:512], gh, wh, start=(dc == 0), stop=(dc == 15))
                    inc(st.opc, last)
    t_op = st.opc[1]

    @block.vector
    def _(v):
        wt(v, st.opc, t_op)
        for oc in range(2):
            for tsub in range(2):
                xa = st.X[:, (2 * c + tsub) * D + oc * 512:(2 * c + tsub) * D + (oc + 1) * 512]
                v.tensor_tensor(out=xa, in0=xa, in1=st.pb[4 + 2 * oc + tsub][:, 0:512], op=AL.add)
        inc(st.vt, v.engine_nop())
    v_op = st.vt[1]

    @block.vector
    def _(v):
        wt(v, st.vt, v_op)
        wt(v, st.ct)
        for tsub in range(2):
            mxfp4_quant_emit(v, st, st.xn0c[:, tsub * D:(tsub + 1) * D], 128, D, -8, 5,
                             st.sm2[:, 0:1].broadcast_to([128, 32]),
                             st.sm2[:, 1:2].broadcast_to([128, 32]),
                             st.uc[:, tsub * D:(tsub + 1) * D])
        inc(st.vt, v.engine_nop())
    v_xq = st.vt[1]

    if st.debug and c == 0:
        @block.scalar
        def _(s):
            wt(s, st.vt, v_xq)
            s.dma_start(out=st.dbg["d_xq"], in_=st.uc[:, :]).then_inc(st.sd[0], dinc(st.sd))

        @block.vector
        def _(v):
            wt(v, st.sd)
            inc(st.vt, v.engine_nop())

    for i in range(16):
        tsub, dc = divmod(i, 8)
        bnk = st.pb[4 + (i % 2)]

        @block.tensor
        def _(t, i=i, tsub=tsub, dc=dc, bnk=bnk, v_xq=v_xq):
            wt(t, st.vt)
            inc(st.tt, t.transpose(bnk[:, 0:128],
                                   st.uc[:, tsub * 1024 + dc * 128: tsub * 1024 + (dc + 1) * 128],
                                   st.idf[:, :]))
        t_tp = st.tt[1]

        @block.vector
        def _(v, tsub=tsub, dc=dc, bnk=bnk, t_tp=t_tp):
            wt(v, st.tt, t_tp)
            inc(st.vt, v.tensor_copy(out=st.xqT[:, dc * 256 + tsub * 128: dc * 256 + (tsub + 1) * 128],
                                     in_=bnk[:, 0:128]))
    v_xqT = st.vt[1]

    for oc in range(2):
        for dh in range(2):
            unit = c * 4 + oc * 2 + dh
            r0 = dh * 512

            @block.sync
            def _(s, oc=oc, dh=dh, unit=unit, r0=r0):
                if unit >= 1:
                    wt(s, st.fqc, unit)
                s.dma_start(out=st.wfq[:, 0:2048],
                            in_=st.s_fqW[r0:r0 + 512, oc * 512:(oc + 1) * 512]
                            .rearrange("(dc p) o -> p dc o", p=128)).then_inc(st.sy[0], dinc(st.sy))
            w_done = st.sy[1]

            @block.tensor
            def _(t, oc=oc, dh=dh, unit=unit, w_done=w_done, vcur=st.vt[1]):
                wt(t, st.sy, w_done)
                wt(t, st.vt, vcur)
                last = None
                for i in range(4):
                    dc = dh * 4 + i
                    wh = st.wfq[:, i * 512:(i + 1) * 512]
                    for tsub in range(2):
                        xh_ = st.xqT[:, dc * 256 + tsub * 128: dc * 256 + (tsub + 1) * 128]
                        bnk = st.pb[tsub]
                        last = t.matmul(bnk[:, 0:512], xh_, wh, start=(dc == 0), stop=(dc == 7))
                inc(st.fqc, last)
        t_fq = st.fqc[1]

        @block.vector
        def _(v, oc=oc, t_fq=t_fq):
            wt(v, st.fqc, t_fq)
            for tsub in range(2):
                xa = st.X[:, (2 * c + tsub) * D + oc * 512:(2 * c + tsub) * D + (oc + 1) * 512]
                v.tensor_tensor(out=st.q1[:, 0:512], in0=st.pb[tsub][:, 0:512],
                                in1=st.fbb[:, oc * 512:(oc + 1) * 512], op=AL.add)
                v.tensor_tensor(out=xa, in0=xa, in1=st.q1[:, 0:512], op=AL.add)
            inc(st.vt, v.engine_nop())




def emit_p2(st):
    nc, block = st.nc, st.block

    @block.vector
    def _(v):
        wt(v, st.ct)
        wt(v, st.tt)
        wt(v, st.ipc)
        wt(v, st.opc)
        wt(v, st.fqc)
        inc(st.vt, v.engine_nop())

    @block.tensor
    def _(t):
        wt(t, st.vt)
        wt(t, st.ct)
        inc(st.tt, t.matmul(st.pb[7][0:1, 4:5], st.ones1[:, 0:1], st.ones1[0:1, 0:1], start=True, stop=True))

    @block.scalar
    def _(s):
        wt(s, st.vt)
        wt(s, st.tt)
        inc(st.ct, s.activation(out=st.sm1[0:1, 8:9], in_=st.sm1[0:1, 0:1], func=AF.Identity))

    @block.sync
    def _(s):
        wt(s, st.vt)
        with nc.allow_non_contiguous_dma(reason="small L1 constants"):
            s.dma_start(out=st.rotT[:, :], in_=st.rot.rearrange("h d -> d h"),
                        allow_slow_non_contiguous=True).then_inc(st.sy[0], dinc(st.sy))
            s.dma_start(out=st.entT[:, :], in_=st.ent.rearrange("h d -> d h"),
                        allow_slow_non_contiguous=True).then_inc(st.sy[0], dinc(st.sy))
            s.dma_start(out=st.dbT[:, 0:8], in_=st.d_bk.rearrange("(a p) -> p a", p=128),
                        allow_slow_non_contiguous=True).then_inc(st.sy[0], dinc(st.sy))
            s.dma_start(out=st.dbT[:, 8:16], in_=st.d_bv.rearrange("(a p) -> p a", p=128),
                        allow_slow_non_contiguous=True).then_inc(st.sy[0], dinc(st.sy))
    st.L1LOADS = st.sy[1]

    @block.vector
    def _(v):
        wt(v, st.sy, st.L1LOADS)
        inc(st.vt, v.tensor_scalar_mul(st.rotT[:, :], st.rotT[:, :], 0.125))

    for tt2 in range(16):
        xsl = st.X[:, tt2 * D:(tt2 + 1) * D]
        emit_ln(st, xsl, st.t_tok[:, :], None, None, D, dump=st.ldmp1)

        for dc in range(8):
            bnk = st.pb[4 + (dc % 2)]

            @block.tensor
            def _(t, tt2=tt2, dc=dc, bnk=bnk):
                wt(t, st.vt)
                inc(st.tt, t.transpose(bnk[:, 0:128], st.t_tok[:, dc * 128:(dc + 1) * 128],
                                       st.idf[:, :]))
            t_tp = st.tt[1]

            @block.vector
            def _(v, tt2=tt2, dc=dc, bnk=bnk, t_tp=t_tp):
                wt(v, st.tt, t_tp)
                inc(st.vt, v.tensor_copy(out=st.xn1T[:, dc * 2048 + tt2 * 128: dc * 2048 + (tt2 + 1) * 128],
                                         in_=bnk[:, 0:128]))
    st.v_xn1 = st.vt[1]

    if st.debug:
        @block.scalar
        def _(s):
            wt(s, st.vt, st.v_xn1)
            s.dma_start(out=st.dbg["d_xn1h"], in_=st.xn1T[:, 0:8192]).then_inc(st.sd[0], dinc(st.sd))

        @block.vector
        def _(v):
            wt(v, st.sd)
            inc(st.vt, v.engine_nop())



def emit_dpath(st):
    nc, block = st.nc, st.block
    dbrow = st.AR[0:1, 18432:19456]
    pT = st.AR[:, 19584:19600]
    bbv = st.AR[:, 19600:19728]
    st.aD2 = st.AR[0:64, 19728:19984]

    @block.sync
    def _(s):
        wt(s, st.vt)
        s.dma_start(out=st.t_tok[0:16, :], in_=st.sq_in[0, :, :]).then_inc(st.sy[0], dinc(st.sy))
        with nc.allow_non_contiguous_dma(reason="bias row"):
            s.dma_start(out=dbrow[:, :], in_=st.d_bq.unsqueeze(0)).then_inc(st.sy[0], dinc(st.sy))
    sq_ld = st.sy[1]

    @block.tensor
    def _(t):
        wt(t, st.sy, sq_ld)
        wt(t, st.vt)
        for dc in range(8):
            t.transpose(st.pb[4][:, dc * 16:(dc + 1) * 16], st.t_tok[0:16, dc * 128:(dc + 1) * 128],
                        st.idf[0:16, 0:16])
        inc(st.tt, t.matmul(st.pb[2][0:16, 0:512], st.ones1[0:1, 0:16], dbrow[:, 0:512],
                            start=True, stop=True))
        inc(st.tt, t.matmul(st.pb[3][0:16, 0:512], st.ones1[0:1, 0:16], dbrow[:, 512:1024],
                            start=True, stop=True))
    t_sq = st.tt[1]

    @block.vector
    def _(v):
        wt(v, st.tt, t_sq)
        inc(st.vt, v.tensor_copy(out=st.sqT[:, :], in_=st.pb[4][:, 0:128]))
    v_sq = st.vt[1]

    for dc in range(8):
        sl_w = (dc % 2) * 1024

        @block.sync
        def _(s, dc=dc, sl_w=sl_w):
            wt(s, st.vt)
            s.dma_start(out=st.wnat[:, sl_w:sl_w + 1024], in_=st.d_Wq[dc * 128:(dc + 1) * 128, :]).then_inc(st.sy[0], dinc(st.sy))
        w_ld = st.sy[1]

        @block.tensor
        def _(t, dc=dc, sl_w=sl_w, w_ld=w_ld, v_sq=v_sq):
            wt(t, st.sy, w_ld)
            wt(t, st.vt)
            last = None
            for oc in range(2):
                sh = st.sqT[:, dc * 16:(dc + 1) * 16]
                last = t.matmul(st.pb[oc][0:16, 0:512], sh,
                                st.wnat[:, sl_w + oc * 512:sl_w + (oc + 1) * 512],
                                start=(dc == 0), stop=(dc == 7))
            inc(st.tt, last)

        @block.vector
        def _(v):
            wt(v, st.tt)
            inc(st.vt, v.engine_nop())
    t_qh = st.tt[1]

    @block.vector
    def _(v):
        wt(v, st.tt, t_qh)
        for oc in range(2):
            v.tensor_scalar(out=st.qhC[0:16, oc * 512:(oc + 1) * 512], in0=st.pb[oc][0:16, 0:512],
                            scalar1=0.125, scalar2=None, op0=AL.mult)
            v.scalar_tensor_tensor(out=st.qhC[0:16, oc * 512:(oc + 1) * 512],
                                   in0=st.pb[2 + oc][0:16, 0:512], scalar=0.125,
                                   in1=st.qhC[0:16, oc * 512:(oc + 1) * 512], op0=AL.mult, op1=AL.add)
        inc(st.vt, v.engine_nop())
    v_qh = st.vt[1]

    if st.debug:
        @block.scalar
        def _(s):
            wt(s, st.vt, v_qh)
            s.dma_start(out=st.dbg["d_dqh"], in_=st.qhC[0:16, :]).then_inc(st.sd[0], dinc(st.sd))

        @block.vector
        def _(v):
            wt(v, st.sd)
            inc(st.vt, v.engine_nop())

    @block.tensor
    def _(t):
        wt(t, st.vt, v_qh)
        for dc in range(8):
            inc(st.tt, t.transpose(st.pb[4][:, dc * 16:(dc + 1) * 16],
                                   st.qhC[0:16, dc * 128:(dc + 1) * 128], st.idf[0:16, 0:16]))
    t_qT = st.tt[1]

    @block.vector
    def _(v):
        wt(v, st.tt, t_qT)
        inc(st.vt, v.tensor_copy(out=st.sqT[:, :], in_=st.pb[4][:, 0:128]))
    v_qT = st.vt[1]

    for p in range(8):
        @block.sync
        def _(s, p=p):
            wt(s, st.vt)
            wt(s, st.tt)
            with nc.allow_non_contiguous_dma(reason="column slice of d_Wk"):
                s.dma_start(out=st.wnat[:, 0:1024].rearrange("p (dc o) -> p dc o", o=128),
                            in_=st.d_Wk[:, p * 128:(p + 1) * 128].rearrange("(dc p) o -> p dc o", p=128)).then_inc(st.sy[0], dinc(st.sy))
        w_ld = st.sy[1]

        for ncol in range(4):
            bnk = st.pb[ncol % 2]

            @block.tensor
            def _(t, ncol=ncol, bnk=bnk, w_ld=w_ld):
                wt(t, st.sy, w_ld)
                wt(t, st.vt)
                if ncol >= 2:
                    wt(t, st.ct)
                last = None
                for dc in range(8):
                    wh = st.wnat[:, dc * 128:(dc + 1) * 128]
                    xh = st.xn1T[:, dc * 2048 + ncol * 512: dc * 2048 + (ncol + 1) * 512]
                    last = t.matmul(bnk[:, 0:512], wh, xh, start=(dc == 0), stop=(dc == 7))
                inc(st.tt, last)
            t_kh = st.tt[1]

            @block.scalar
            def _(s, p=p, ncol=ncol, bnk=bnk, t_kh=t_kh):
                wt(s, st.tt, t_kh)
                inc(st.ct, s.activation(out=st.khT[:, ncol * 512:(ncol + 1) * 512], in_=bnk[:, 0:512],
                                        func=AF.Identity, bias=st.dbT[:, p:p + 1]))
        s_kh = st.ct[1]

        @block.vector
        def _(v, s_kh=s_kh):
            wt(v, st.ct, s_kh)
            inc(st.vt, v.engine_nop())
        v_khs = st.vt[1]

        if st.debug and p == 0:
            @block.scalar
            def _(s, v_khs=v_khs):
                wt(s, st.vt, v_khs)
                s.dma_start(out=st.dbg["d_dkh"], in_=st.khT[:, :]).then_inc(st.sd[0], dinc(st.sd))

            @block.vector
            def _(v):
                wt(v, st.sd)
                inc(st.vt, v.engine_nop())

        @block.sync
        def _(s, p=p, v_khs=v_khs):
            wt(s, st.vt, v_khs)
            with nc.allow_non_contiguous_dma(reason="column slice of d_Wv"):
                s.dma_start(out=st.wnat[:, 0:1024].rearrange("p (dc o) -> p dc o", o=128),
                            in_=st.d_Wv[:, p * 128:(p + 1) * 128].rearrange("(dc p) o -> p dc o", p=128)).then_inc(st.sy[0], dinc(st.sy))
        wv_ld = st.sy[1]

        @block.tensor
        def _(t, p=p, wv_ld=wv_ld):
            wt(t, st.sy, wv_ld)
            wt(t, st.vt)
            inc(st.tt, t.transpose(st.pb[2][0:1, 0:128], st.dbT[:, 8 + p:9 + p], st.idf[:, :]))
        t_bv = st.tt[1]

        @block.vector
        def _(v, t_bv=t_bv):
            wt(v, st.tt, t_bv)
            inc(st.vt, v.tensor_copy(out=st.espr[:, 0:128], in_=st.pb[2][0:1, 0:128]))
        v_bvr = st.vt[1]

        @block.tensor
        def _(t, v_bvr=v_bvr):
            wt(t, st.vt, v_bvr)
            inc(st.tt, t.matmul(st.pb[2][:, 0:128], st.ones1[:, :], st.espr[:, 0:128], start=True, stop=True))
        t_bb = st.tt[1]

        @block.vector
        def _(v, t_bb=t_bb):
            wt(v, st.tt, t_bb)
            inc(st.vt, v.tensor_copy(out=bbv[:, :], in_=st.pb[2][:, 0:128]))
        v_bb = st.vt[1]

        for tt2 in range(16):
            bnk = st.pb[(tt2 % 2)]

            @block.tensor
            def _(t, tt2=tt2, bnk=bnk, v_bb=v_bb):
                wt(t, st.vt)
                last = None
                for dc in range(8):
                    sh = st.xn1T[:, dc * 2048 + tt2 * 128: dc * 2048 + (tt2 + 1) * 128]
                    wh = st.wnat[:, dc * 128:(dc + 1) * 128]
                    last = t.matmul(bnk[:, 0:128], sh, wh, start=(dc == 0), stop=(dc == 7))
                inc(st.tt, last)
            t_vh = st.tt[1]

            @block.vector
            def _(v, tt2=tt2, bnk=bnk, t_vh=t_vh):
                wt(v, st.tt, t_vh)
                inc(st.vt, v.tensor_tensor(out=st.vhT[:, tt2 * 128:(tt2 + 1) * 128], in0=bnk[:, 0:128],
                                           in1=bbv[:, :], op=AL.add))
        v_vh = st.vt[1]

        if st.debug and p == 0:
            @block.scalar
            def _(s, v_vh=v_vh):
                wt(s, st.vt, v_vh)
                s.dma_start(out=st.dbg["d_dvh"], in_=st.vhT[:, :]).then_inc(st.sd[0], dinc(st.sd))

            @block.vector
            def _(v):
                wt(v, st.sd)
                inc(st.vt, v.engine_nop())

        for hh in range(2):
            h = 2 * p + hh
            b0 = hh * 64

            for ncol in range(4):
                bnk = st.pb[2 + (ncol % 2)]

                @block.tensor
                def _(t, h=h, b0=b0, ncol=ncol, bnk=bnk, v_vh=v_vh):
                    wt(t, st.vt)
                    qh_ = st.sqT[b0:b0 + 64, (h // 2) * 16:(h // 2) * 16 + 16]
                    kh_ = st.khT[b0:b0 + 64, ncol * 512:(ncol + 1) * 512]
                    inc(st.tt, t.matmul(bnk[0:16, 0:512], qh_, kh_, start=True, stop=True))
                t_s = st.tt[1]

                @block.vector
                def _(v, ncol=ncol, bnk=bnk, t_s=t_s):
                    wt(v, st.tt, t_s)
                    inc(st.vt, v.tensor_copy(out=st.Sd[:, ncol * 512:(ncol + 1) * 512], in_=bnk[0:16, 0:512]))

            @block.vector
            def _(v):
                wt(v, st.vt)
                inc(st.vt, v.reduce_max(out=st.sm2[0:16, 8:9], in_=st.Sd[:, :], axis=AX.X, negate=True))
            v_m = st.vt[1]

            @block.scalar
            def _(s, v_m=v_m):
                wt(s, st.vt, v_m)
                s.activation(out=st.Sd[:, :], in_=st.Sd[:, :], func=AF.Exp,
                             bias=st.sm2[0:16, 8:9], accum_out=st.sm2[0:16, 9:10])
                s.activation(out=st.sm2[0:16, 10:11], in_=st.sm2[0:16, 9:10], func=AF.Ln)
                inc(st.ct, s.activation(out=st.sm2[0:16, 11:12], in_=st.sm2[0:16, 10:11],
                                        func=AF.Exp, scale=-1.0))
            s_r = st.ct[1]

            @block.vector
            def _(v, s_r=s_r):
                wt(v, st.ct, s_r)
                y = st.sm2[0:16, 11:12]
                for _i in range(2):
                    inc(st.vt, v.tensor_tensor(out=st.sm2[0:16, 12:13], in0=y, in1=st.sm2[0:16, 9:10], op=AL.mult))
                    wt(v, st.vt)
                    inc(st.vt, v.tensor_scalar(out=st.sm2[0:16, 12:13], in0=st.sm2[0:16, 12:13],
                                               scalar1=-1.0, scalar2=2.0, op0=AL.mult, op1=AL.add))
                    wt(v, st.vt)
                    inc(st.vt, v.tensor_tensor(out=y, in0=y, in1=st.sm2[0:16, 12:13], op=AL.mult))
                    wt(v, st.vt)
                inc(st.vt, v.tensor_scalar(out=st.Sd[:, :], in0=st.Sd[:, :], scalar1=y,
                                           scalar2=None, op0=AL.mult))
            v_P = st.vt[1]

            if st.debug and h == 0:
                @block.scalar
                def _(s, v_P=v_P):
                    wt(s, st.vt, v_P)
                    s.dma_start(out=st.dbg["d_dS"], in_=st.Sd[:, :]).then_inc(st.sd[0], dinc(st.sd))

                @block.vector
                def _(v):
                    wt(v, st.sd)
                    inc(st.vt, v.engine_nop())

            for jt in range(16):
                bnk = st.pb[4 + (jt % 2)]

                @block.tensor
                def _(t, jt=jt, bnk=bnk, v_P=v_P):
                    wt(t, st.vt)
                    inc(st.tt, t.transpose(bnk[:, 0:16], st.Sd[:, jt * 128:(jt + 1) * 128],
                                           st.idf[0:16, 0:16]))
                t_pt = st.tt[1]

                @block.vector
                def _(v, bnk=bnk, t_pt=t_pt):
                    wt(v, st.tt, t_pt)
                    inc(st.vt, v.tensor_copy(out=pT[:, :], in_=bnk[:, 0:16]))
                v_pt = st.vt[1]

                @block.tensor
                def _(t, b0=b0, jt=jt, v_pt=v_pt):
                    wt(t, st.vt, v_pt)
                    inc(st.tt, t.matmul(st.pb[6][0:64, 0:16], st.vhT[:, jt * 128 + b0: jt * 128 + b0 + 64],
                                        pT[:, :], start=(jt == 0), stop=(jt == 15)))
            t_av = st.tt[1]

            @block.vector
            def _(v, h=h, t_av=t_av):
                wt(v, st.tt, t_av)
                inc(st.vt, v.tensor_copy(out=st.aD2[:, h * 16:(h + 1) * 16], in_=st.pb[6][0:64, 0:16]))

    if st.debug:
        @block.scalar
        def _(s):
            wt(s, st.vt)
            s.dma_start(out=st.dbg["d_daD2"], in_=st.aD2[:, :]).then_inc(st.sd[0], dinc(st.sd))

        @block.vector
        def _(v):
            wt(v, st.sd)
            inc(st.vt, v.engine_nop())

    for h in range(16):
        @block.sync
        def _(s, h=h):
            wt(s, st.vt)
            s.dma_start(out=st.wnat[0:64, 0:1024], in_=st.d_Wo[h * 64:(h + 1) * 64, :]).then_inc(st.sy[0], dinc(st.sy))
        w_ld = st.sy[1]

        @block.tensor
        def _(t, h=h, w_ld=w_ld):
            wt(t, st.sy, w_ld)
            wt(t, st.vt)
            last = None
            for oc in range(2):
                last = t.matmul(st.pb[oc][0:16, 0:512], st.aD2[:, h * 16:(h + 1) * 16],
                                st.wnat[0:64, oc * 512:(oc + 1) * 512],
                                start=(h == 0), stop=(h == 15))
            inc(st.tt, last)
        @block.vector
        def _(v):
            wt(v, st.tt)
            inc(st.vt, v.engine_nop())
    t_o = st.tt[1]

    @block.sync
    def _(s):
        wt(s, st.vt)
        with nc.allow_non_contiguous_dma(reason="bias row"):
            s.dma_start(out=dbrow[:, :], in_=st.d_bo.unsqueeze(0)).then_inc(st.sy[0], dinc(st.sy))
        s.dma_start(out=st.AR[0:16, 20480:21504], in_=st.sq_in[0, :, :]).then_inc(st.sy[0], dinc(st.sy))
    o_ld = st.sy[1]

    @block.tensor
    def _(t, o_ld=o_ld):
        wt(t, st.sy, o_ld)
        t.matmul(st.pb[2][0:16, 0:512], st.ones1[0:1, 0:16], dbrow[:, 0:512], start=True, stop=True)
        inc(st.tt, t.matmul(st.pb[3][0:16, 0:512], st.ones1[0:1, 0:16], dbrow[:, 512:1024],
                            start=True, stop=True))
    t_ob = st.tt[1]

    @block.vector
    def _(v, t_ob=t_ob):
        wt(v, st.tt, t_ob)
        sq2 = st.AR[0:16, 20480:21504]
        for oc in range(2):
            osl = st.o_tok[:, oc * 512:(oc + 1) * 512]
            v.tensor_copy(out=osl, in_=st.pb[oc][0:16, 0:512])
            v.tensor_tensor(out=osl, in0=osl, in1=st.pb[2 + oc][0:16, 0:512], op=AL.add)
            v.tensor_tensor(out=osl, in0=osl, in1=sq2[:, oc * 512:(oc + 1) * 512], op=AL.add)
        inc(st.vt, v.engine_nop())
    st.v_o = st.vt[1]

    if st.debug:
        @block.scalar
        def _(s):
            wt(s, st.vt, st.v_o)
            s.dma_start(out=st.dbg["d_o"], in_=st.o_tok[:, :]).then_inc(st.sd[0], dinc(st.sd))

        @block.vector
        def _(v):
            wt(v, st.sd)
            inc(st.vt, v.engine_nop())


DBGH = 0


def emit_p4(st):
    nc, block = st.nc, st.block
    AR = st.AR
    oT = AR[:, 26768:26896]
    uTe, gTe = st.uTe, st.gTe
    espr = st.espr
    ew_f = AR[:, 27664:28688]
    dd = AR[:, 29712:29728]

    @block.sync
    def _(s):
        wt(s, st.vt, st.v_o)
        s.dma_start(out=st.ebnat[:, 0:4096], in_=st.eb1[:, :]).then_inc(st.sy[0], dinc(st.sy))
        s.dma_start(out=st.AR[0:16, 16384:17408], in_=st.elng[:, :]).then_inc(st.sy[0], dinc(st.sy))
        s.dma_start(out=st.AR[0:16, 17408:18432], in_=st.elnb[:, :]).then_inc(st.sy[0], dinc(st.sy))
    p4_ld = st.sy[1]

    hrow = AR[0:16, 18432:19456]
    elg = st.AR[0:16, 16384:17408]
    elb = st.AR[0:16, 17408:18432]

    @block.scalar
    def _(s):
        wt(s, st.sy, p4_ld)
        inc(st.ct, s.activation(out=hrow[:, :], in_=st.o_tok[:, :], func=AF.Identity,
                                accum_out=st.sm2[0:16, 8:9]))
    s_su = st.ct[1]

    @block.vector
    def _(v):
        wt(v, st.ct, s_su)
        inc(st.vt, v.tensor_scalar_mul(st.sm2[0:16, 8:9], st.sm2[0:16, 8:9], -1.0 / 1024))
    v_mu = st.vt[1]

    @block.scalar
    def _(s):
        wt(s, st.vt, v_mu)
        inc(st.ct, s.activation(out=hrow[:, :], in_=st.o_tok[:, :], func=AF.Square,
                                bias=st.sm2[0:16, 8:9], accum_out=st.sm2[0:16, 9:10]))
    s_sq = st.ct[1]

    @block.vector
    def _(v):
        wt(v, st.ct, s_sq)
        inc(st.vt, v.tensor_scalar(out=st.sm2[0:16, 10:11], in0=st.sm2[0:16, 9:10],
                                   scalar1=1.0 / 1024, scalar2=LN_EPS, op0=AL.mult, op1=AL.add))
    v_v = st.vt[1]

    @block.scalar
    def _(s):
        wt(s, st.vt, v_v)
        s.activation(out=st.sm2[0:16, 11:12], in_=st.sm2[0:16, 10:11], func=AF.Ln)
        inc(st.ct, s.activation(out=st.sm2[0:16, 11:12], in_=st.sm2[0:16, 11:12], func=AF.Exp, scale=-0.5))
    s_sd = st.ct[1]

    @block.vector
    def _(v):
        wt(v, st.ct, s_sd)
        y = st.sm2[0:16, 11:12]
        for _i in range(2):
            inc(st.vt, v.tensor_tensor(out=st.sm2[0:16, 12:13], in0=y, in1=y, op=AL.mult))
            wt(v, st.vt)
            inc(st.vt, v.tensor_tensor(out=st.sm2[0:16, 12:13], in0=st.sm2[0:16, 12:13],
                                       in1=st.sm2[0:16, 10:11], op=AL.mult))
            wt(v, st.vt)
            inc(st.vt, v.tensor_scalar(out=st.sm2[0:16, 12:13], in0=st.sm2[0:16, 12:13],
                                       scalar1=-0.5, scalar2=1.5, op0=AL.mult, op1=AL.add))
            wt(v, st.vt)
            inc(st.vt, v.tensor_tensor(out=y, in0=y, in1=st.sm2[0:16, 12:13], op=AL.mult))
            wt(v, st.vt)
        v.tensor_scalar(out=hrow[:, :], in0=st.o_tok[:, :], scalar1=st.sm2[0:16, 8:9],
                        scalar2=y, op0=AL.add, op1=AL.mult)
        v.tensor_tensor(out=hrow[:, :], in0=hrow[:, :], in1=elg[:, :], op=AL.mult)
        inc(st.vt, v.tensor_tensor(out=hrow[:, :], in0=hrow[:, :], in1=elb[:, :], op=AL.add))
    v_h = st.vt[1]

    if st.debug:
        @block.scalar
        def _(s, v_h=v_h):
            wt(s, st.vt, v_h)
            s.dma_start(out=st.dbg["d_eh"], in_=hrow[:, :]).then_inc(st.sd[0], dinc(st.sd))

        @block.vector
        def _(v):
            wt(v, st.sd)
            inc(st.vt, v.engine_nop())

    @block.tensor
    def _(t):
        wt(t, st.vt, v_h)
        for dc in range(8):
            t.transpose(st.pb[6][:, dc * 16:(dc + 1) * 16], hrow[:, dc * 128:(dc + 1) * 128],
                        st.idf[0:16, 0:16])
        for dc in range(8):
            inc(st.tt, t.transpose(st.pb[7][:, dc * 16:(dc + 1) * 16],
                                   st.o_tok[:, dc * 128:(dc + 1) * 128], st.idf[0:16, 0:16]))
    t_h = st.tt[1]

    @block.vector
    def _(v):
        wt(v, st.tt, t_h)
        inc(st.vt, v.tensor_copy(out=st.hT[:, :], in_=st.pb[6][:, 0:128]))
        inc(st.vt, v.tensor_copy(out=oT[:, :], in_=st.pb[7][:, 0:128]))
    v_hs = st.vt[1]

    for fc2 in range(16):
        @block.tensor
        def _(t, fc2=fc2):
            wt(t, st.vt)
            t.transpose(st.pb[6][:, 0:16], st.ebnat[:, (2 * fc2) * 128:(2 * fc2 + 1) * 128],
                        st.idf[0:16, 0:16])
            inc(st.tt, t.transpose(st.pb[7][:, 0:16], st.ebnat[:, (2 * fc2 + 1) * 128:(2 * fc2 + 2) * 128],
                                   st.idf[0:16, 0:16]))
        t_p = st.tt[1]

        @block.vector
        def _(v, fc2=fc2, t_p=t_p):
            wt(v, st.tt, t_p)
            v.tensor_copy(out=st.b1T[:, (2 * fc2) * 16:(2 * fc2 + 1) * 16], in_=st.pb[6][:, 0:16])
            inc(st.vt, v.tensor_copy(out=st.b1T[:, (2 * fc2 + 1) * 16:(2 * fc2 + 2) * 16],
                                     in_=st.pb[7][:, 0:16]))
    v_b1 = st.vt[1]

    if st.debug:
        @block.scalar
        def _(s):
            wt(s, st.vt, v_b1)
            s.dma_start(out=st.dbg["d_eb1T"], in_=st.b1T[:, :]).then_inc(st.sd[0], dinc(st.sd))

        @block.vector
        def _(v):
            wt(v, st.sd)
            inc(st.vt, v.engine_nop())

    @block.sync
    def _(s):
        wt(s, st.vt, v_b1)
        s.dma_start(out=st.ebnat[:, 0:4096], in_=st.ebg[:, :]).then_inc(st.sy[0], dinc(st.sy))
    bg_ld = st.sy[1]

    for fc2 in range(16):
        @block.tensor
        def _(t, fc2=fc2, bg_ld=bg_ld):
            wt(t, st.sy, bg_ld)
            wt(t, st.vt)
            t.transpose(st.pb[6][:, 0:16], st.ebnat[:, (2 * fc2) * 128:(2 * fc2 + 1) * 128],
                        st.idf[0:16, 0:16])
            inc(st.tt, t.transpose(st.pb[7][:, 0:16], st.ebnat[:, (2 * fc2 + 1) * 128:(2 * fc2 + 2) * 128],
                                   st.idf[0:16, 0:16]))
        t_p = st.tt[1]

        @block.vector
        def _(v, fc2=fc2, t_p=t_p):
            wt(v, st.tt, t_p)
            v.tensor_copy(out=st.bgT[:, (2 * fc2) * 16:(2 * fc2 + 1) * 16], in_=st.pb[6][:, 0:16])
            inc(st.vt, v.tensor_copy(out=st.bgT[:, (2 * fc2 + 1) * 16:(2 * fc2 + 2) * 16],
                                     in_=st.pb[7][:, 0:16]))
    v_bg = st.vt[1]

    for h in range(16):
        if KATT:
            emit_attn_head(st, h)
        if KEXP:
            emit_expert(st, h, uTe, gTe, espr, ew_f, oT)

    @block.sync
    def _(s):
        wt(s, st.vt)
        s.dma_start(out=st.eb2n[:, :], in_=st.eb2[:, :]).then_inc(st.sy[0], dinc(st.sy))
    b2_ld = st.sy[1]

    @block.tensor
    def _(t):
        wt(t, st.sy, b2_ld)
        wt(t, st.vt)
        for dc in range(8):
            inc(st.tt, t.transpose(st.pb[6 if dc % 2 == 0 else 7][:, 0:16],
                                   st.eb2n[:, dc * 128:(dc + 1) * 128], st.idf[0:16, 0:16]))
    t_b2 = st.tt[1]

    @block.vector
    def _(v):
        base = t_b2 - 8
        for dc in range(8):
            wt(v, st.tt, base + dc + 1)
            inc(st.vt, v.scalar_tensor_tensor(out=oT[:, dc * 16:(dc + 1) * 16],
                                              in0=st.pb[6 if dc % 2 == 0 else 7][:, 0:16],
                                              scalar=1.0, in1=oT[:, dc * 16:(dc + 1) * 16],
                                              op0=AL.mult, op1=AL.add))
    st.v_oT = st.vt[1]
    st.oT = oT

    if st.debug:
        @block.scalar
        def _(s):
            wt(s, st.vt, st.v_oT)
            s.dma_start(out=st.dbg["d_o2"], in_=oT[:, :]).then_inc(st.sd[0], dinc(st.sd))
            s.dma_start(out=st.dbg["d_aX"], in_=st.X[:, :]).then_inc(st.sd[0], dinc(st.sd))

        @block.vector
        def _(v):
            wt(v, st.sd)
            inc(st.vt, v.engine_nop())


def emit_attn_head(st, h):
    nc, block = st.nc, st.block
    AR = st.AR
    dc_h = h // 2
    b0 = (h % 2) * 64
    dd = AR[:, 29712:29728]
    negmrow = AR[0:1, 27664:29712]

    for (dst, scol) in [(st.Q65, st.rotT), (st.K65, st.entT)]:
        for nc2 in range(2):
            for nq in range(2):
                ncol = nc2 * 2 + nq
                bnk = st.pb[nq]

                @block.tensor
                def _(t, ncol=ncol, bnk=bnk):
                    wt(t, st.vt)
                    mv = st.idf[:, 0:64] if b0 == 0 else st.idf[:, 64:128]
                    xs = st.xn1T[:, dc_h * 2048 + ncol * 512: dc_h * 2048 + (ncol + 1) * 512]
                    inc(st.tt, t.matmul(bnk[0:64, 0:512], mv, xs, start=True, stop=True))
            t_mv = st.tt[1]

            @block.vector
            def _(v, dst=dst, scol=scol, nc2=nc2, t_mv=t_mv):
                wt(v, st.tt, t_mv)
                for nq in range(2):
                    ncol = nc2 * 2 + nq
                    v.tensor_scalar(out=dst[0:64, ncol * 512:(ncol + 1) * 512],
                                    in0=st.pb[nq][0:64, 0:512],
                                    scalar1=scol[:, h:h + 1], scalar2=None, op0=AL.mult)
                inc(st.vt, v.engine_nop())
    v_qk = st.vt[1]

    @block.vector
    def _(v):
        inc(st.vt, v.memset(st.Vb[:, :].rearrange("p (a b) -> p a b", b=65)[:, :, 64:65], 1.0))
    v_vones = st.vt[1]

    identV = st.idf[0:64, 0:64] if b0 == 0 else st.idf[64:128, 64:128]
    for jt2 in range(8):
        @block.tensor
        def _(t, jt2=jt2, identV=identV):
            wt(t, st.vt)
            for k in range(2):
                jt = jt2 * 2 + k
                inc(st.tt, t.transpose(st.pb[4 + k][:, 0:64],
                                       st.xn1T[b0:b0 + 64, dc_h * 2048 + jt * 128: dc_h * 2048 + (jt + 1) * 128],
                                       identV))
        t_v = st.tt[1]

        @block.vector
        def _(v, jt2=jt2, t_v=t_v):
            wt(v, st.tt, t_v)
            for k in range(2):
                jt = jt2 * 2 + k
                v.tensor_copy(out=st.Vb[:, jt * 65:jt * 65 + 64], in_=st.pb[4 + k][:, 0:64])
            inc(st.vt, v.engine_nop())
    v_V = st.vt[1]

    for it in range(16):
        @block.tensor
        def _(t, it=it):
            wt(t, st.vt)
            for jc in range(4):
                inc(st.tt, t.matmul(st.pb[jc][:, 0:512], st.Q65[0:64, it * 128:(it + 1) * 128],
                                    st.K65[0:64, jc * 512:(jc + 1) * 512], start=True, stop=True))
        t_st = st.tt[1]

        @block.vector
        def _(v, it=it, t_st=t_st):
            wt(v, st.tt, t_st)
            for jc in range(4):
                v.tensor_reduce(out=st.sm2[:, 8 + jc:9 + jc], in_=st.pb[jc][:, 0:512],
                                axis=AX.X, op=AL.max)
            inc(st.vt, v.engine_nop())
            wt(v, st.vt)
            inc(st.vt, v.tensor_reduce(out=st.sm2[:, 12:13], in_=st.sm2[:, 8:12], axis=AX.X, op=AL.max))
            wt(v, st.vt)
            inc(st.vt, v.tensor_scalar_mul(st.sm2[:, 13:14], st.sm2[:, 12:13], -1.0))
        v_m = st.vt[1]

        @block.tensor
        def _(t, it=it, v_m=v_m):
            wt(t, st.vt, v_m)
            inc(st.tt, t.transpose(st.pb[4][0:1, 0:128], st.sm2[:, 13:14], st.idf[:, :]))
        t_nm = st.tt[1]

        @block.vector
        def _(v, it=it, t_nm=t_nm, negmrow=negmrow):
            wt(v, st.tt, t_nm)
            inc(st.vt, v.tensor_copy(out=negmrow[:, it * 128:(it + 1) * 128],
                                     in_=st.pb[4][0:1, 0:128]))
    v_stats = st.vt[1]

    if st.debug and h == DBGH:
        @block.scalar
        def _(s):
            wt(s, st.vt, v_stats)
            s.dma_start(out=st.dbg["d_negm"], in_=negmrow[:, :]).then_inc(st.sd[0], dinc(st.sd))

        @block.vector
        def _(v):
            wt(v, st.sd)
            inc(st.vt, v.engine_nop())

    for ic in range(2):
        for nc2 in range(2):
            icol = ic * 2 + nc2

            def st_mm(jt, icol=icol):
                sb = st.pb[jt % 2]

                @block.tensor
                def _(t, jt=jt, sb=sb, icol=icol):
                    wt(t, st.vt)
                    Ks = st.K65[0:64, jt * 128:(jt + 1) * 128]
                    Qs = st.Q65[0:64, icol * 512:(icol + 1) * 512]
                    t.matmul(sb[:, 0:512], Ks, Qs, start=True, stop=False)
                    inc(st.tt, t.matmul(sb[:, 0:512], st.ones1[:, :],
                                        negmrow[:, icol * 512:(icol + 1) * 512],
                                        start=False, stop=True))
                return st.tt[1]

            def exp_op(jt, t_s):
                sb = st.pb[jt % 2]

                @block.scalar
                def _(s, jt=jt, sb=sb, t_s=t_s):
                    wt(s, st.tt, t_s)
                    inc(st.ct, s.activation(out=st.Pb[:, (jt % 2) * 512:(jt % 2) * 512 + 512],
                                            in_=sb[:, 0:512], func=AF.Exp))
                return st.ct[1]

            def av_mm(jt, s_e):
                @block.tensor
                def _(t, jt=jt, s_e=s_e):
                    wt(t, st.ct, s_e)
                    inc(st.tt, t.matmul(st.pb[2][0:65, 0:512], st.Vb[:, jt * 65:(jt + 1) * 65],
                                        st.Pb[:, (jt % 2) * 512:(jt % 2) * 512 + 512],
                                        start=(jt == 0), stop=(jt == 15)))

            t_s = st_mm(0)
            for jt in range(16):
                s_e = exp_op(jt, t_s)
                if jt < 15:
                    t_s = st_mm(jt + 1)
                av_mm(jt, s_e)
            t_av = st.tt[1]

            @block.vector
            def _(v, icol=icol, nc2=nc2, t_av=t_av):
                wt(v, st.tt, t_av)
                inc(st.vt, v.tensor_copy(out=st.aO[:, nc2 * 512:(nc2 + 1) * 512], in_=st.pb[2][0:65, 0:512]))
        v_ao = st.vt[1]

        for itl2 in range(4):
            @block.tensor
            def _(t, ic=ic, itl2=itl2, v_ao=v_ao):
                wt(t, st.vt)
                for k in range(2):
                    itl = itl2 * 2 + k
                    inc(st.tt, t.transpose(st.pb[3 + k][:, 0:65], st.aO[:, itl * 128:(itl + 1) * 128],
                                           st.idf[0:65, 0:65]))
            t_at = st.tt[1]

            @block.vector
            def _(v, ic=ic, itl2=itl2, t_at=t_at):
                wt(v, st.tt, t_at)
                for k in range(2):
                    itl = itl2 * 2 + k
                    it = ic * 8 + itl
                    v.tensor_copy(out=st.aT[:, it * 64:(it + 1) * 64], in_=st.pb[3 + k][:, 0:64])
                    inc(st.vt, v.tensor_copy(out=dd[:, it:it + 1], in_=st.pb[3 + k][:, 64:65]))
    v_aT = st.vt[1]

    @block.scalar
    def _(s, v_aT=v_aT):
        wt(s, st.vt, v_aT)
        s.activation(out=AR[:, 29728:29744], in_=dd[:, :], func=AF.Ln)
        inc(st.ct, s.activation(out=AR[:, 29728:29744], in_=AR[:, 29728:29744], func=AF.Exp, scale=-1.0))
    s_i = st.ct[1]

    @block.vector
    def _(v, s_i=s_i):
        wt(v, st.ct, s_i)
        y = AR[:, 29728:29744]
        w2 = AR[:, 29744:29760]
        for _i in range(2):
            inc(st.vt, v.tensor_tensor(out=w2, in0=y, in1=dd[:, :], op=AL.mult))
            wt(v, st.vt)
            inc(st.vt, v.tensor_scalar(out=w2, in0=w2, scalar1=-1.0, scalar2=2.0, op0=AL.mult, op1=AL.add))
            wt(v, st.vt)
            inc(st.vt, v.tensor_tensor(out=y, in0=y, in1=w2, op=AL.mult))
            wt(v, st.vt)
        for it in range(16):
            v.scalar_tensor_tensor(out=st.X[:, it * D + h * 64: it * D + h * 64 + 64],
                                   in0=st.aT[:, it * 64:(it + 1) * 64], scalar=y[:, it:it + 1],
                                   in1=st.X[:, it * D + h * 64: it * D + h * 64 + 64],
                                   op0=AL.mult, op1=AL.add)
        inc(st.vt, v.engine_nop())


def emit_expert(st, e, uTe, gTe, espr, ew_f, oT):
    nc, block = st.nc, st.block
    AR = st.AR
    ewA = AR[:, 27664:29712]
    ewB = AR[:, 18432:20480]
    urow = AR[0:1, 20480:21504]
    yrow = AR[0:1, 19456:20480]

    for mi, (W, dst) in enumerate([(st.eW1, uTe), (st.eWg, gTe)]):
        tmm = []
        for i in range(16):
            buf = ewA if i % 2 == 0 else ewB

            @block.sync
            def _(s, e=e, W=W, i=i, buf=buf, gate=(tmm[i - 2] if i >= 2 else None)):
                if gate is None:
                    wt(s, st.vt)
                else:
                    s.wait_ge(st.tt[0], gate)
                with nc.allow_non_contiguous_dma(reason="expert W column chunk"):
                    s.dma_start(out=buf[:, :].rearrange("p (dc o) -> p dc o", o=256),
                                in_=W[e, :, i * 256:(i + 1) * 256].rearrange("(dc p) o -> p dc o", p=128)).then_inc(st.sy[0], dinc(st.sy))
            wf_ld = st.sy[1]

            @block.tensor
            def _(t, e=e, i=i, buf=buf, wf_ld=wf_ld):
                wt(t, st.sy, wf_ld)
                wt(t, st.vt)
                bnk = st.pb[6] if (i % 4) < 2 else st.pb[7]
                c0 = (i % 2) * 256
                last = None
                for dc in range(8):
                    hh_c = st.hT[:, dc * 16 + e: dc * 16 + e + 1]
                    last = t.matmul(bnk[0:1, c0:c0 + 256], hh_c, buf[:, dc * 256:(dc + 1) * 256],
                                    start=(dc == 0), stop=(dc == 7))
                inc(st.tt, last)
            tmm.append(st.tt[1])

            if i % 4 == 3:
                g = i // 4

                @block.vector
                def _(v, g=g, t_g=tmm[-1]):
                    wt(v, st.tt, t_g)
                    v.tensor_copy(out=urow[:, 0:512], in_=st.pb[6][0:1, 0:512])
                    inc(st.vt, v.tensor_copy(out=urow[:, 512:1024], in_=st.pb[7][0:1, 0:512]))
                v_r = st.vt[1]

                @block.tensor
                def _(t, g=g, v_r=v_r):
                    wt(t, st.vt, v_r)
                    for k in range(8):
                        inc(st.tt, t.transpose(st.pb[4][:, k:k + 1], urow[:, k * 128:(k + 1) * 128],
                                               st.idf[0:1, 0:1]))
                t_tr = st.tt[1]

                @block.vector
                def _(v, g=g, dst=dst, t_tr=t_tr):
                    wt(v, st.tt, t_tr)
                    inc(st.vt, v.tensor_copy(out=dst[:, g * 8:(g + 1) * 8], in_=st.pb[4][:, 0:8]))

        if st.debug and e == 0 and mi == 0:
            @block.scalar
            def _(s):
                wt(s, st.vt)
                s.dma_start(out=st.dbg["d_eur"], in_=uTe[:, :]).then_inc(st.sd[0], dinc(st.sd))

            @block.vector
            def _(v):
                wt(v, st.sd)
                inc(st.vt, v.engine_nop())

    @block.vector
    def _(v, e=e):
        wt(v, st.vt)
        v.tensor_tensor(out=uTe[:, :], in0=uTe[:, :], in1=st.b1T[:, e::16], op=AL.add)
        inc(st.vt, v.tensor_tensor(out=gTe[:, :], in0=gTe[:, :], in1=st.bgT[:, e::16], op=AL.add))
    v_b = st.vt[1]

    @block.scalar
    def _(s, v_b=v_b):
        wt(s, st.vt, v_b)
        s.activation(out=uTe[:, :], in_=uTe[:, :], func=AF.Silu)
        inc(st.ct, s.activation(out=gTe[:, :], in_=gTe[:, :], func=AF.Silu))
    s_si = st.ct[1]

    if st.debug and e == 0:
        @block.scalar
        def _(s, s_si=s_si):
            wt(s, st.ct, s_si)
            s.dma_start(out=st.dbg["d_eu"], in_=uTe[:, :]).then_inc(st.sd[0], dinc(st.sd))
            s.dma_start(out=st.dbg["d_eg"], in_=gTe[:, :]).then_inc(st.sd[0], dinc(st.sd))

        @block.vector
        def _(v):
            wt(v, st.sd)
            inc(st.vt, v.engine_nop())

    @block.vector
    def _(v, s_si=s_si):
        wt(v, st.ct, s_si)
        inc(st.vt, v.tensor_tensor(out=uTe[:, :], in0=uTe[:, :], in1=gTe[:, :], op=AL.mult))
    v_ug = st.vt[1]

    tmm = []
    for i in range(16):
        buf = ewA if i % 2 == 0 else ewB

        @block.sync
        def _(s, e=e, i=i, buf=buf, gate=(tmm[i - 2] if i >= 2 else None)):
            if gate is None:
                wt(s, st.vt)
            else:
                s.wait_ge(st.tt[0], gate)
            s.dma_start(out=buf[:, :].rearrange("p (fc o) -> p fc o", o=1024),
                        in_=st.eW2[e, i * 256:(i + 1) * 256, :].rearrange("(fc p) o -> p fc o", p=128)).then_inc(st.sy[0], dinc(st.sy))
        wf_ld = st.sy[1]

        @block.tensor
        def _(t, e=e, i=i, buf=buf, wf_ld=wf_ld):
            wt(t, st.sy, wf_ld)
            wt(t, st.vt)
            last = None
            for fcl in range(2):
                ug_c = uTe[:, i * 2 + fcl: i * 2 + fcl + 1]
                for oc in range(2):
                    bnk = st.pb[6] if oc == 0 else st.pb[7]
                    last = t.matmul(bnk[0:1, 0:512], ug_c,
                                    buf[:, fcl * 1024 + oc * 512: fcl * 1024 + (oc + 1) * 512],
                                    start=(i == 0 and fcl == 0), stop=(i == 15 and fcl == 1))
            inc(st.tt, last)
        tmm.append(st.tt[1])
    t_y = st.tt[1]

    @block.vector
    def _(v, t_y=t_y):
        wt(v, st.tt, t_y)
        inc(st.vt, v.tensor_copy(out=AR[0:1, 19456:19968], in_=st.pb[6][0:1, 0:512]))
        inc(st.vt, v.tensor_copy(out=AR[0:1, 19968:20480], in_=st.pb[7][0:1, 0:512]))
    v_y = st.vt[1]

    if st.debug and e == 0:
        @block.scalar
        def _(s, v_y=v_y):
            wt(s, st.vt, v_y)
            s.dma_start(out=st.dbg["d_ey"], in_=yrow[:, :]).then_inc(st.sd[0], dinc(st.sd))

        @block.vector
        def _(v):
            wt(v, st.sd)
            inc(st.vt, v.engine_nop())

    @block.tensor
    def _(t, v_y=v_y):
        wt(t, st.vt, v_y)
        for k in range(8):
            inc(st.tt, t.transpose(st.pb[4][:, k:k + 1], yrow[:, k * 128:(k + 1) * 128],
                                   st.idf[0:1, 0:1]))
    t_tr = st.tt[1]

    @block.vector
    def _(v, e=e, t_tr=t_tr):
        wt(v, st.tt, t_tr)
        for dc in range(8):
            v.scalar_tensor_tensor(out=oT[:, dc * 16 + e:dc * 16 + e + 1],
                                   in0=st.pb[4][:, dc:dc + 1],
                                   scalar=1.0, in1=oT[:, dc * 16 + e:dc * 16 + e + 1],
                                   op0=AL.mult, op1=AL.add)
        inc(st.vt, v.engine_nop())


def emit_cpath(st):
    nc, block = st.nc, st.block
    AR = st.AR
    wnat = st.wnat
    kho = AR[0:16, 23568:24592]
    vho = AR[0:16, 24592:25616]
    khoT2 = AR[0:64, 25888:26144]
    bqfm2 = AR[0:64, 25744:25760]
    sbrow = st.espr
    WqT2 = AR[0:64, 16384:18432]
    Mall = AR[:, 20480:22528]
    aCt4 = AR[:, 16384:20480]
    Pc = st.Pc
    negm = AR[:, 29712:29728]
    den = AR[:, 29728:29744]
    inv = AR[:, 29744:29760]
    pTs = AR[0:16, 27248:27376]
    cboB = AR[:, 22544:23568]
    brow = AR[0:1, 18432:19456]

    for (W, bvec, dst) in [(st.c_Wk, st.c_bk, kho), (st.c_Wv, st.c_bv, vho)]:
        @block.sync
        def _(s, W=W, bvec=bvec):
            wt(s, st.vt)
            with nc.allow_non_contiguous_dma(reason="bias row"):
                s.dma_start(out=brow[:, :], in_=bvec.unsqueeze(0)).then_inc(st.sy[0], dinc(st.sy))
        b_ld = st.sy[1]

        @block.tensor
        def _(t, b_ld=b_ld):
            wt(t, st.sy, b_ld)
            wt(t, st.vt)
            t.matmul(st.pb[2][0:16, 0:512], st.ones1[0:1, 0:16], brow[:, 0:512], start=True, stop=True)
            inc(st.tt, t.matmul(st.pb[3][0:16, 0:512], st.ones1[0:1, 0:16], brow[:, 512:1024],
                                start=True, stop=True))

        for dc in range(8):
            sl = (dc % 2) * 1024

            @block.sync
            def _(s, W=W, dc=dc, sl=sl):
                wt(s, st.vt)
                s.dma_start(out=wnat[:, sl:sl + 1024], in_=W[dc * 128:(dc + 1) * 128, :]).then_inc(st.sy[0], dinc(st.sy))
            w_ld = st.sy[1]

            @block.tensor
            def _(t, dc=dc, sl=sl, w_ld=w_ld):
                wt(t, st.sy, w_ld)
                last = None
                for oc in range(2):
                    last = t.matmul(st.pb[oc][0:16, 0:512], st.oT[:, dc * 16:(dc + 1) * 16],
                                    wnat[:, sl + oc * 512:sl + (oc + 1) * 512],
                                    start=(dc == 0), stop=(dc == 7))
                inc(st.tt, last)

            @block.vector
            def _(v):
                wt(v, st.tt)
                inc(st.vt, v.engine_nop())
        t_p = st.tt[1]

        @block.vector
        def _(v, dst=dst, t_p=t_p):
            wt(v, st.tt, t_p)
            for oc in range(2):
                v.tensor_copy(out=dst[:, oc * 512:(oc + 1) * 512], in_=st.pb[oc][0:16, 0:512])
            for oc in range(2):
                v.scalar_tensor_tensor(out=dst[:, oc * 512:(oc + 1) * 512],
                                       in0=st.pb[2 + oc][0:16, 0:512], scalar=1.0,
                                       in1=dst[:, oc * 512:(oc + 1) * 512], op0=AL.mult, op1=AL.add)
            inc(st.vt, v.engine_nop())
    v_kv = st.vt[1]

    if KCA < 2:
        st.v_cp = st.vt[1]
        return
    @block.sync
    def _(s):
        wt(s, st.vt, v_kv)
        with nc.allow_non_contiguous_dma(reason="bq per-head fm"):
            s.dma_start(out=bqfm2[:, :], in_=st.c_bq.rearrange("(h o) -> o h", o=64),
                        allow_slow_non_contiguous=True).then_inc(st.sy[0], dinc(st.sy))
    bq_ld = st.sy[1]

    @block.tensor
    def _(t, bq_ld=bq_ld):
        wt(t, st.sy, bq_ld)
        wt(t, st.vt)
        for h in range(15):
            t.transpose(st.pb[4][0:64, h * 16:(h + 1) * 16], kho[:, h * 64:(h + 1) * 64],
                        st.idf[0:16, 0:16])
        inc(st.tt, t.transpose(st.pb[4][0:64, 240:256], kho[:, 960:1024], st.idf[0:16, 0:16]))
    t_kT = st.tt[1]

    @block.vector
    def _(v, t_kT=t_kT):
        wt(v, st.tt, t_kT)
        inc(st.vt, v.tensor_copy(out=khoT2[:, :], in_=st.pb[4][0:64, 0:256]))
    v_kT = st.vt[1]

    if KCB < 2:
        st.v_cp = st.vt[1]
        return
    @block.tensor
    def _(t, v_kT=v_kT):
        wt(t, st.vt, v_kT)
        last = None
        for h in range(16):
            last = t.matmul(st.pb[5][0:1, h * 16:(h + 1) * 16],
                            bqfm2[:, h:h + 1],
                            khoT2[:, h * 16:(h + 1) * 16],
                            start=True, stop=True)
        inc(st.tt, last)
    t_sb = st.tt[1]

    @block.vector
    def _(v, t_sb=t_sb):
        wt(v, st.tt, t_sb)
        inc(st.vt, v.tensor_scalar_mul(sbrow[:, 0:256], st.pb[5][0:1, 0:256], 0.125))
    v_sb = st.vt[1]

    if KCA < 3:
        st.v_cp = st.vt[1]
        return
    @block.sync
    def _(s):
        wt(s, st.vt, v_sb)
        with nc.allow_non_contiguous_dma(reason="bias row"):
            s.dma_start(out=brow[:, :], in_=st.c_bo.unsqueeze(0)).then_inc(st.sy[0], dinc(st.sy))
    bo_ld = st.sy[1]

    @block.tensor
    def _(t, bo_ld=bo_ld):
        wt(t, st.sy, bo_ld)
        wt(t, st.vt)
        t.matmul(st.pb[6][:, 0:512], st.ones1[:, :], brow[:, 0:512], start=True, stop=True)
        inc(st.tt, t.matmul(st.pb[7][:, 0:512], st.ones1[:, :], brow[:, 512:1024], start=True, stop=True))
    t_bo = st.tt[1]

    @block.vector
    def _(v, t_bo=t_bo):
        wt(v, st.tt, t_bo)
        v.tensor_copy(out=cboB[:, 0:512], in_=st.pb[6][:, 0:512])
        inc(st.vt, v.tensor_copy(out=cboB[:, 512:1024], in_=st.pb[7][:, 0:512]))
    v_bo = st.vt[1]

    if KCA < 4:
        st.v_cp = st.vt[1]
        return
    for dcd in range(8):
        sl = (dcd % 2) * 1024

        @block.sync
        def _(s, dcd=dcd, sl=sl):
            wt(s, st.vt)
            s.dma_start(out=wnat[:, sl:sl + 1024], in_=st.c_Wq[dcd * 128:(dcd + 1) * 128, :]).then_inc(st.sy[0], dinc(st.sy))
        wq_ld = st.sy[1]

        for h2 in range(8):
            @block.tensor
            def _(t, h2=h2, sl=sl, wq_ld=wq_ld):
                wt(t, st.sy, wq_ld)
                wt(t, st.vt)
                for k in range(2):
                    h = h2 * 2 + k
                    inc(st.tt, t.transpose(st.pb[4 + k][0:64, 0:128],
                                           wnat[:, sl + h * 64:sl + (h + 1) * 64], st.idf[:, :]))
            t_tp = st.tt[1]

            @block.vector
            def _(v, h2=h2, t_tp=t_tp):
                wt(v, st.tt, t_tp)
                for k in range(2):
                    h = h2 * 2 + k
                    v.tensor_copy(out=WqT2[:, h * 128:(h + 1) * 128], in_=st.pb[4 + k][0:64, 0:128])
                inc(st.vt, v.engine_nop())
        v_wT = st.vt[1]

        @block.tensor
        def _(t, v_wT=v_wT):
            wt(t, st.vt, v_wT)
            last = None
            for h in range(16):
                last = t.matmul(st.pb[6][:, h * 16:(h + 1) * 16],
                                WqT2[:, h * 128:(h + 1) * 128],
                                khoT2[:, h * 16:(h + 1) * 16],
                                start=True, stop=True)
            inc(st.tt, last)
        t_M = st.tt[1]

        @block.vector
        def _(v, dcd=dcd, t_M=t_M):
            wt(v, st.tt, t_M)
            inc(st.vt, v.tensor_scalar_mul(Mall[:, dcd * 256:(dcd + 1) * 256], st.pb[6][:, 0:256], 0.125))
    v_M = st.vt[1]

    if KCSUB < 2:
        st.v_cp = st.vt[1]
        return
    for itg in range(4):
        for itl in range(4):
            it = itg * 4 + itl

            @block.tensor
            def _(t, it=it, v_M=v_M):
                wt(t, st.vt)
                for dc in range(8):
                    xh = st.xn1T[:, dc * 2048 + it * 128: dc * 2048 + (it + 1) * 128]
                    t.matmul(st.pb[0][:, 0:256], xh, Mall[:, dc * 256:(dc + 1) * 256],
                             start=(dc == 0), stop=False)
                last = t.matmul(st.pb[0][:, 0:256], st.ones1[:, :], sbrow[:, 0:256],
                                start=False, stop=True)
                inc(st.tt, last)
            t_S = st.tt[1]

            @block.vector
            def _(v, t_S=t_S):
                wt(v, st.tt, t_S)
                inc(st.vt, v.tensor_copy(out=Pc[:, :], in_=st.pb[0][:, 0:256]))
                wt(v, st.vt)
                inc(st.vt, v.tensor_reduce(out=st.sm2[:, 0:16],
                                           in_=Pc[:, :].rearrange("p (a b) -> p a b", b=16),
                                           axis=AX.X, op=AL.max))
                wt(v, st.vt)
                inc(st.vt, v.tensor_scalar_mul(negm[:, :], st.sm2[:, 0:16], -1.0))
                wt(v, st.vt)
                inc(st.vt, v.engine_nop())
            v_m = st.vt[1]

            @block.scalar
            def _(s, v_m=v_m):
                wt(s, st.vt, v_m)
                last = None
                for h in range(16):
                    last = s.activation(out=Pc[:, h * 16:(h + 1) * 16], in_=Pc[:, h * 16:(h + 1) * 16],
                                        func=AF.Exp, bias=negm[:, h:h + 1], accum_out=den[:, h:h + 1])
                inc(st.ct, last)
            s_e = st.ct[1]

            @block.scalar
            def _(s, s_e=s_e):
                wt(s, st.ct, s_e)
                s.activation(out=inv[:, :], in_=den[:, :], func=AF.Ln)
                inc(st.ct, s.activation(out=inv[:, :], in_=inv[:, :], func=AF.Exp, scale=-1.0))
            s_i = st.ct[1]

            @block.vector
            def _(v, s_i=s_i):
                wt(v, st.ct, s_i)
                w = st.sm2[:, 0:16]
                for _i in range(2):
                    inc(st.vt, v.tensor_tensor(out=w, in0=inv[:, :], in1=den[:, :], op=AL.mult))
                    wt(v, st.vt)
                    inc(st.vt, v.tensor_scalar(out=w, in0=w, scalar1=-1.0, scalar2=2.0,
                                               op0=AL.mult, op1=AL.add))
                    wt(v, st.vt)
                    inc(st.vt, v.tensor_tensor(out=inv[:, :], in0=inv[:, :], in1=w, op=AL.mult))
                    wt(v, st.vt)
                inc(st.vt, v.tensor_tensor(
                    out=Pc[:, :].rearrange("p (a b) -> p a b", b=16),
                    in0=Pc[:, :].rearrange("p (a b) -> p a b", b=16),
                    in1=inv[:, :].unsqueeze(2).broadcast_to([128, 16, 16]), op=AL.mult))
            v_P = st.vt[1]

            for h in (range(16) if KCSUB >= 3 else []):
                b0 = (h % 2) * 64

                @block.tensor
                def _(t, h=h, v_P=v_P):
                    wt(t, st.vt, v_P)
                    wt(t, st.vt)
                    inc(st.tt, t.transpose(st.pb[4 + (h % 2)][0:16, 0:128],
                                           Pc[:, h * 16:(h + 1) * 16], st.idf[:, :]))
                t_pT = st.tt[1]

                @block.vector
                def _(v, h=h, t_pT=t_pT):
                    wt(v, st.tt, t_pT)
                    inc(st.vt, v.tensor_copy(out=pTs[:, :], in_=st.pb[4 + (h % 2)][0:16, 0:128]))
                v_pT = st.vt[1]

                @block.tensor
                def _(t, h=h, b0=b0, v_pT=v_pT):
                    wt(t, st.vt, v_pT)
                    inc(st.tt, t.matmul(st.pb[7][b0:b0 + 64, (h % 2) * 128:(h % 2) * 128 + 128],
                                        vho[:, h * 64:(h + 1) * 64], pTs[:, :],
                                        start=True, stop=True))
                t_av = st.tt[1]

                @block.vector
                def _(v, h=h, b0=b0, itl=itl, t_av=t_av):
                    wt(v, st.tt, t_av)
                    inc(st.vt, v.tensor_copy(
                        out=aCt4[b0:b0 + 64, itl * 1024 + (h // 2) * 128: itl * 1024 + (h // 2) * 128 + 128],
                        in_=st.pb[7][b0:b0 + 64, (h % 2) * 128:(h % 2) * 128 + 128]))
        v_g = st.vt[1]

        for dc in (range(8) if KCSUB >= 4 else []):
            sl = (dc % 2) * 1024

            @block.sync
            def _(s, dc=dc, sl=sl, v_g=v_g):
                wt(s, st.vt)
                s.dma_start(out=wnat[:, sl:sl + 1024], in_=st.c_Wo[dc * 128:(dc + 1) * 128, :]).then_inc(st.sy[0], dinc(st.sy))
            wo_ld = st.sy[1]

            @block.tensor
            def _(t, dc=dc, sl=sl, wo_ld=wo_ld, v_g=v_g):
                wt(t, st.sy, wo_ld)
                wt(t, st.vt)
                last = None
                for itl in range(4):
                    for oc in range(2):
                        last = t.matmul(st.pb[itl * 2 + oc][:, 0:512],
                                        aCt4[:, itl * 1024 + dc * 128: itl * 1024 + (dc + 1) * 128],
                                        wnat[:, sl + oc * 512:sl + (oc + 1) * 512],
                                        start=(dc == 0), stop=(dc == 7))
                inc(st.tt, last)

            @block.vector
            def _(v):
                wt(v, st.tt)
                inc(st.vt, v.engine_nop())
        t_op = st.tt[1]

        @block.vector
        def _(v, itg=itg, t_op=t_op):
            if KCSUB < 4:
                inc(st.vt, v.engine_nop())
                return
            wt(v, st.tt, t_op)
            for itl in range(4):
                it = itg * 4 + itl
                for oc in range(2):
                    xa = st.X[:, it * D + oc * 512: it * D + (oc + 1) * 512]
                    v.scalar_tensor_tensor(out=xa, in0=st.pb[itl * 2 + oc][:, 0:512], scalar=1.0,
                                           in1=xa, op0=AL.mult, op1=AL.add)
            for itl in range(4):
                it = itg * 4 + itl
                for oc in range(2):
                    xa = st.X[:, it * D + oc * 512: it * D + (oc + 1) * 512]
                    v.tensor_tensor(out=xa, in0=xa, in1=cboB[:, oc * 512:(oc + 1) * 512], op=AL.add)
            inc(st.vt, v.engine_nop())
    st.v_cp = st.vt[1]


def kernel(**inputs):
    nc = build_program(debug=False)
    in_maps = make_in_maps(inputs)
    res = run_bass_kernel_spmd(nc, in_maps, core_ids=list(range(8)))
    outs = [r["out"].reshape(T, D) for r in res.results]
    return np.stack(outs, axis=0)


def make_in_maps(inputs):
    iota = np.arange(256, dtype=np.float32)
    jcol = np.arange(128, dtype=np.float32)
    consts = {
        "c_identf": np.eye(128, dtype=np.float32),
        "c_identb": np.eye(128, dtype=np.float32).astype(ml_dtypes.bfloat16),
        "c_mask0": (iota[None, :] >= jcol[:, None]).astype(np.float32),
        "c_mask1": (iota[None, :] >= (jcol[:, None] + 128.0)).astype(np.float32),
        "c_sel16": np.repeat(np.eye(16, dtype=np.float32), 128, axis=1),
        "c_sel64": np.concatenate([np.zeros((64, 64), np.float32),
                                   np.eye(64, dtype=np.float32)], axis=0),
    }
    in_maps = []
    for b in range(8):
        m = dict(consts)
        for k, v in inputs.items():
            if k == "L0_attn_u":
                m[k] = np.ascontiguousarray(np.asarray(v)[b])
            else:
                m[k] = np.asarray(v)
        in_maps.append(m)
    return in_maps
